# version 07
# baseline (speedup 1.0000x reference)
import contextlib
import numpy as np
import ml_dtypes
import concourse.bass as bass
import concourse.mybir as mybir
from concourse.bass_utils import run_bass_kernel_spmd

F32 = mybir.dt.float32
BF16 = mybir.dt.bfloat16
U8 = mybir.dt.uint8
U16 = mybir.dt.uint16
I32 = mybir.dt.int32
AF = mybir.ActivationFunctionType
ALU = mybir.AluOpType
AXX = mybir.AxisListType.X

D = 2048
T = 1024
NMLP = 16384
BF = ml_dtypes.bfloat16
RACE_DETECT = True


def _new_nc():
    return bass.Bass("TRN2", target_bir_lowering=False,
                     detect_race_conditions=RACE_DETECT)


def _pi(n):
    kap = np.arange(n) // 128
    p = np.arange(n) % 128
    return (256 * (kap // 2) + 2 * p + (kap % 2)).astype(np.int64)


PI2048 = _pi(2048)
PI16384 = _pi(16384)
INV2048 = np.argsort(PI2048)


def _bytes_of_packed(w_int32):
    return np.ascontiguousarray(w_int32.view(np.uint8).reshape(*w_int32.shape, 4)[..., 0])


ENGS = ("sync", "scalar", "vector", "tensor", "gpsimd")


class Sem:
    __slots__ = ("h", "n", "name")

    def __init__(self, ctx, nc, name):
        self.h = ctx.enter_context(nc.semaphore(name))
        self.n = 0
        self.name = name

    def reserve(self, by=1):
        self.n += by
        return self.n


class Prog:
    def __init__(self, nc, ctx):
        self.nc = nc
        self.ctx = ctx
        self.q = {e: [] for e in ENGS}
        self.sb = {}
        self.ps = {}

    def sem(self, name):
        return Sem(self.ctx, self.nc, name)

    def sbuf(self, name, shape, dt):
        t = self.ctx.enter_context(self.nc.sbuf_tensor(name, list(shape), dt))
        self.sb[name] = t
        return t

    def psum(self, name, shape, dt=F32):
        t = self.ctx.enter_context(self.nc.psum_tensor(name, list(shape), dt))
        self.ps[name] = t
        return t

    def on(self, eng, fn):
        self.q[eng].append(fn)

    def w(self, eng, sem, n=None):
        v = sem.n if n is None else n
        if v <= 0:
            return
        self.q[eng].append(lambda E, s=sem.h, v=v: E.wait_ge(s, v))

    CHAINED = ("vector", "scalar", "gpsimd")

    def do(self, eng, fn, sem=None, by=1, chain=True):
        if chain and eng in self.CHAINED:
            prev = getattr(self, "_chain", {}).get(eng)
            if prev is not None:
                self.q[eng].append(lambda E, s=prev[0], v=prev[1]: E.wait_ge(s, v))
        if sem is None:
            self.q[eng].append(fn)
            return None
        sem.reserve(by)
        self.q[eng].append(lambda E, f=fn, s=sem.h, b=by: f(E).then_inc(s, b))
        if eng in self.CHAINED and by == 1:
            if not hasattr(self, "_chain"):
                self._chain = {}
            self._chain[eng] = (sem.h, sem.n)
        return sem.n

    def emit(self):
        with self.nc.Block() as block:
            for e in ENGS:
                fns = self.q[e]
                if not fns:
                    continue

                def run(E, fns=fns):
                    for f in fns:
                        f(E)
                getattr(block, e)(run)



def alloc_decode_bufs(p, wmax):
    p.sbuf("pk0", [128, wmax], U8)
    p.sbuf("pk1", [128, wmax], U8)
    p.sbuf("dq_t", [128, wmax], U16)
    p.sbuf("dq_a", [128, wmax], U16)
    p.sbuf("dq_s", [128, wmax], U16)
    p.sbuf("dq_lo", [128, wmax], U16)
    p.sbuf("dq_hi", [128, wmax], U16)
    p.sbuf("pl0", [128, 2 * wmax], BF16)
    p.sbuf("pl1", [128, 2 * wmax], BF16)
    p.sbuf("sc_f32", [8, 8 * wmax], F32)
    p.sbuf("sc_b", [8, 8 * wmax], BF16)
    p.psum("pdec0", [128, 512])
    p.psum("pdec1", [128, 512])


def emit_decode_matrix(p, st8, *, n_bt, width, pk_dram, scT_dram, out_dram, out_row0,
                       ind, eng="vector", dma="sync"):
    dec, mm = st8["dec"], st8["mm"]
    lds = (st8["ld0"], st8["ld1"])
    ldsc = st8["ldsc"]
    sts = (st8["st0"], st8["st1"])
    pk_free = st8.setdefault("pk_free", {})
    pl_free = st8.setdefault("pl_free", {})
    ps_free = st8.setdefault("ps_free", {})
    W = width
    NWIN = (W + 511) // 512
    sb = p.sb

    p.w(dma, mm)
    p.do(dma, lambda E: E.dma_start(
        out=sb["sc_f32"][0:8, 0:n_bt * W],
        in_=bass.AP(scT_dram.tensor, 0, [[W, 8], [8 * W, n_bt], [1, W]])), ldsc, 16)
    p.w(eng, ldsc)
    p.do(eng, lambda E: E.tensor_copy(sb["sc_b"][0:8, 0:n_bt * W],
                                      sb["sc_f32"][0:8, 0:n_bt * W]), dec)
    sc_ready = dec.n

    for bt in range(n_bt):
        buf = bt % 2
        pkt = sb[f"pk{buf}"]
        plb = sb[f"pl{buf}"]
        if buf in pk_free:
            p.w(dma, dec, pk_free[buf])
        p.do(dma, lambda E, bt=bt, pkt=pkt: E.dma_start(
            out=pkt[:, 0:W], in_=pk_dram[128 * bt:128 * bt + 128, 0:W]), lds[buf], 16)
        p.w(eng, lds[buf])
        p.do(eng, lambda E, pkt=pkt: E.tensor_copy(sb["dq_t"][:, 0:W], pkt[:, 0:W]), dec)
        pk_free[buf] = dec.n
        t16 = sb["dq_t"]

        for pl, (mA, shA, mS, shS) in enumerate([(7, 6, 8, 12), (0x70, 2, 0x80, 8)]):
            dst = sb["dq_lo"] if pl == 0 else sb["dq_hi"]
            p.do(eng, lambda E, mA=mA, shA=shA: E.tensor_scalar(
                sb["dq_a"][:, 0:W], t16[:, 0:W], mA, shA,
                ALU.bitwise_and, ALU.logical_shift_left), dec)
            p.do(eng, lambda E, mS=mS, shS=shS: E.tensor_scalar(
                sb["dq_s"][:, 0:W], t16[:, 0:W], mS, shS,
                ALU.bitwise_and, ALU.logical_shift_left), dec)
            p.do(eng, lambda E: E.tensor_single_scalar(
                sb["dq_a"][:, 0:W], sb["dq_a"][:, 0:W], 16128, ALU.add), dec)
            p.do(eng, lambda E, dst=dst: E.tensor_scalar(
                dst[:, 0:W].bitcast(BF16), sb["dq_a"][:, 0:W].bitcast(BF16),
                2.0, -1.0, ALU.mult, ALU.add), dec)
            p.do(eng, lambda E, dst=dst: E.tensor_tensor(
                dst[:, 0:W].bitcast(BF16), sb["dq_a"][:, 0:W].bitcast(BF16),
                dst[:, 0:W].bitcast(BF16), ALU.min), dec)
            p.do(eng, lambda E, dst=dst: E.tensor_tensor(
                dst[:, 0:W], dst[:, 0:W], sb["dq_s"][:, 0:W], ALU.bitwise_xor), dec)

        if buf in pl_free:
            p.w(eng, sts[buf], pl_free[buf])
        for win in range(NWIN):
            w0 = 512 * win
            w1 = min(w0 + 512, W)
            psq = win % 2
            pn = f"pdec{psq}"
            if psq in ps_free:
                p.w("tensor", dec, ps_free[psq])
            p.w("tensor", dec, sc_ready)
            p.do("tensor", lambda E, bt=bt, w0=w0, w1=w1, pn=pn, ind=ind: E.matmul(
                p.ps[pn][:, 0:w1 - w0], ind[0:8, :],
                sb["sc_b"][0:8, bt * W + w0:bt * W + w1], start=True, stop=True), mm)
            p.w(eng, mm)
            p.do(eng, lambda E, w0=w0, w1=w1, pn=pn, plb=plb: E.tensor_tensor(
                plb[:, w0:w1], sb["dq_lo"][:, w0:w1].bitcast(BF16),
                p.ps[pn][:, 0:w1 - w0], ALU.mult), dec)
            p.do(eng, lambda E, w0=w0, w1=w1, pn=pn, plb=plb: E.tensor_tensor(
                plb[:, W + w0:W + w1], sb["dq_hi"][:, w0:w1].bitcast(BF16),
                p.ps[pn][:, 0:w1 - w0], ALU.mult), dec)
            ps_free[psq] = dec.n
        p.w(dma, dec)
        for pl in range(2):
            kap = 2 * bt + pl
            p.do(dma, lambda E, kap=kap, pl=pl, plb=plb: E.dma_start(
                out=out_dram[out_row0 + 128 * kap: out_row0 + 128 * (kap + 1), 0:W],
                in_=plb[:, pl * W:(pl + 1) * W]), sts[buf], 16)
        pl_free[buf] = sts[buf].n



def host_prepare(inp):
    g = {}
    pi_n = PI16384
    pi_d = PI2048
    gpk = _bytes_of_packed(inp["L0_ffn_gate_W_packed"])
    upk = _bytes_of_packed(inp["L0_ffn_up_W_packed"])
    dpk = _bytes_of_packed(inp["L0_ffn_down_W_packed"])
    g["gate_pkT"] = np.ascontiguousarray(gpk[pi_n].T)
    g["up_pkT"] = np.ascontiguousarray(upk[pi_n].T)
    g["down_pkT"] = np.ascontiguousarray(dpk[pi_d].T)
    g["gate_scT"] = np.ascontiguousarray(inp["L0_ffn_gate_scales"][pi_n].T)
    g["up_scT"] = np.ascontiguousarray(inp["L0_ffn_up_scales"][pi_n].T)
    g["down_scT"] = np.ascontiguousarray(inp["L0_ffn_down_scales"][pi_d].T)
    g["winT"] = np.ascontiguousarray(inp["L0_attn_W_in"].T[pi_d])
    g["woutT"] = np.ascontiguousarray(inp["L0_attn_W_out"].T[:, pi_d])
    g["decay_pt"] = np.ascontiguousarray(
        inp["L0_attn_decay"].reshape(16, 128).T)
    g["bias_row"] = np.ascontiguousarray(
        inp["L0_attn_bias"][pi_d].reshape(1, 2048))
    g["wq"] = np.ascontiguousarray(
        inp["L1_attn_Wq"].reshape(2048, 2048)[pi_d])
    g["wk"] = np.ascontiguousarray(inp["L1_attn_Wk"].reshape(2048, 512)[pi_d])
    g["wv"] = np.ascontiguousarray(inp["L1_attn_Wv"].reshape(2048, 512)[pi_d])
    g["wo"] = np.ascontiguousarray(inp["L1_attn_Wo"][:, pi_d])
    g["clfT"] = np.ascontiguousarray(inp["L1_ffn_classifier_w"].T[pi_d])
    g["escb"] = np.ascontiguousarray(
        inp["L1_ffn_e_score_correction_bias"].reshape(1, 12))
    g["l1gT"] = np.ascontiguousarray(
        inp["L1_ffn_gate_w"].reshape(8192, 2048).T[pi_d])
    g["l1uT"] = np.ascontiguousarray(
        inp["L1_ffn_up_w"].reshape(8192, 2048).T[pi_d])
    g["l1dT"] = np.ascontiguousarray(
        inp["L1_ffn_down_w"].transpose(0, 2, 1)[:, :, pi_d].reshape(8192, 2048))
    ind = np.zeros((8, 128), BF)
    for j in range(8):
        ind[j, 16 * j:16 * (j + 1)] = 1.0
    g["ind16"] = ind
    g["ind16h"] = (ind.astype(np.float32) * 0.5).astype(BF)
    g["ident"] = np.eye(128).astype(BF)
    return g


def shard_x(inp):
    xs = []
    x = inp["L0_attn_x"]
    for c in range(8):
        b, r = c // 2, c % 2
        xs.append(np.ascontiguousarray(x[b, r * T:(r + 1) * T, :].T[PI2048]))
    return xs


def unshard_out(shards):
    out = np.empty((4, 2048, 2048), np.float32)
    for c in range(8):
        b, r = c // 2, c % 2
        out[b, r * T:(r + 1) * T, :] = shards[c][INV2048].T
    return out




def build_prep():
    nc = _new_nc()
    ins = {}
    for nm, shape, dt in [
        ("gpk", [1024, 2048], U8), ("gsc", [64, 2048], F32),
        ("upk", [1024, 2048], U8), ("usc", [64, 2048], F32),
        ("dpk", [1024, 2048], U8), ("dsc", [64, 2048], F32),
        ("l1g", [2048, 1024], F32), ("l1u", [2048, 1024], F32),
        ("l1d", [1024, 2048], F32),
        ("winT_s", [256, 2048], F32), ("woutT_s", [256, 2048], F32),
        ("wq_s", [256, 2048], F32), ("wk_s", [256, 512], F32),
        ("wv_s", [256, 512], F32), ("wo_s", [256, 2048], F32),
        ("clf_s", [256, 12], F32),
        ("ind16", [8, 128], BF16), ("ind16h", [8, 128], BF16),
    ]:
        ins[nm] = nc.dram_tensor(nm, shape, dt, kind="ExternalInput").ap()
    outs = {}
    for nm, shape in [("gdec", [2048, 2048]), ("udec", [2048, 2048]),
                      ("ddec", [2048, 2048]), ("l1gb", [2048, 1024]),
                      ("l1ub", [2048, 1024]), ("l1db", [1024, 2048]),
                      ("winb_s", [256, 2048]), ("woutb_s", [256, 2048]),
                      ("wqb_s", [256, 2048]), ("wkb_s", [256, 512]),
                      ("wvb_s", [256, 512]), ("wob_s", [256, 2048]),
                      ("clfb_s", [256, 12])]:
        outs[nm] = nc.dram_tensor(nm, shape, BF16, kind="ExternalOutput").ap()

    with contextlib.ExitStack() as ctx:
        p = Prog(nc, ctx)
        alloc_decode_bufs(p, 2048)
        p.sbuf("c_ind16", [8, 128], BF16)
        p.sbuf("c_ind16h", [8, 128], BF16)
        p.sbuf("cv0", [128, 2048], F32)
        p.sbuf("cv1", [128, 2048], F32)
        p.sbuf("cb0", [128, 2048], BF16)
        p.sbuf("cb1", [128, 2048], BF16)
        sems = {k: p.sem(k) for k in ("ld", "ld0", "ld1", "ldsc", "dec", "mm",
                                      "st0", "st1", "c_ld0", "c_ld1", "c_dec",
                                      "c_st0", "c_st1")}
        p.do("sync", lambda E: E.dma_start(out=p.sb["c_ind16"][:], in_=ins["ind16"]), sems["ld"], 16)
        p.do("sync", lambda E: E.dma_start(out=p.sb["c_ind16h"][:], in_=ins["ind16h"]), sems["ld"], 16)
        p.w("tensor", sems["ld"])
        emit_decode_matrix(p, sems, n_bt=8, width=2048, pk_dram=ins["gpk"],
                           scT_dram=ins["gsc"], out_dram=outs["gdec"], out_row0=0,
                           ind=p.sb["c_ind16"])
        emit_decode_matrix(p, sems, n_bt=8, width=2048, pk_dram=ins["upk"],
                           scT_dram=ins["usc"], out_dram=outs["udec"], out_row0=0,
                           ind=p.sb["c_ind16"])
        emit_decode_matrix(p, sems, n_bt=8, width=2048, pk_dram=ins["dpk"],
                           scT_dram=ins["dsc"], out_dram=outs["ddec"], out_row0=0,
                           ind=p.sb["c_ind16h"])

        clds = (sems["c_ld0"], sems["c_ld1"])
        cdec = sems["c_dec"]
        csts = (sems["c_st0"], sems["c_st1"])
        conv_jobs = []
        for nm_in, nm_out, rows, cols in [("l1g", "l1gb", 2048, 1024),
                                          ("l1u", "l1ub", 2048, 1024),
                                          ("l1d", "l1db", 1024, 2048),
                                          ("winT_s", "winb_s", 256, 2048),
                                          ("woutT_s", "woutb_s", 256, 2048),
                                          ("wq_s", "wqb_s", 256, 2048),
                                          ("wk_s", "wkb_s", 256, 512),
                                          ("wv_s", "wvb_s", 256, 512),
                                          ("wo_s", "wob_s", 256, 2048),
                                          ("clf_s", "clfb_s", 256, 12)]:
            for t0 in range(0, rows, 128):
                conv_jobs.append((nm_in, nm_out, t0, cols))
        stored = {}
        for i, (nm_in, nm_out, t0, cols) in enumerate(conv_jobs):
            buf = i % 2
            cv, cb = p.sb[f"cv{buf}"], p.sb[f"cb{buf}"]
            if i - 2 >= 0:
                p.w("gpsimd", cdec, stored[i - 2][0])
            p.do("gpsimd", lambda E, nm_in=nm_in, t0=t0, cols=cols, cv=cv: E.dma_start(
                out=cv[:, 0:cols], in_=ins[nm_in][t0:t0 + 128, 0:cols]), clds[i % 2], 16)
            p.w("scalar", clds[i % 2])
            if i - 2 >= 0:
                p.w("scalar", csts[i % 2], stored[i - 2][1])
            p.do("scalar", lambda E, cols=cols, cv=cv, cb=cb: E.copy(
                cb[:, 0:cols], cv[:, 0:cols]), cdec)
            stored_dec = cdec.n
            p.w("gpsimd", cdec)
            p.do("gpsimd", lambda E, nm_out=nm_out, t0=t0, cols=cols, cb=cb: E.dma_start(
                out=outs[nm_out][t0:t0 + 128, 0:cols], in_=cb[:, 0:cols]), csts[i % 2], 16)
            stored[i] = (stored_dec, csts[i % 2].n)

        p.w("sync", sems["st0"])
        p.w("sync", sems["st1"])
        p.w("gpsimd", sems["c_st0"])
        p.w("gpsimd", sems["c_st1"])
        p.emit()
    return nc


def host_prep_inputs(g):
    maps = []
    for c in range(8):
        n0, n1 = 2048 * c, 2048 * (c + 1)
        maps.append({
            "gpk": np.ascontiguousarray(g["gate_pkT"][:, n0:n1]),
            "gsc": np.ascontiguousarray(g["gate_scT"][:, n0:n1]),
            "upk": np.ascontiguousarray(g["up_pkT"][:, n0:n1]),
            "usc": np.ascontiguousarray(g["up_scT"][:, n0:n1]),
            "dpk": np.ascontiguousarray(g["down_pkT"][1024 * c:1024 * (c + 1), :]),
            "dsc": np.ascontiguousarray(g["down_scT"][64 * c:64 * (c + 1), :]),
            "l1g": np.ascontiguousarray(g["l1gT"][:, 1024 * c:1024 * (c + 1)]),
            "l1u": np.ascontiguousarray(g["l1uT"][:, 1024 * c:1024 * (c + 1)]),
            "l1d": np.ascontiguousarray(g["l1dT"][1024 * c:1024 * (c + 1), :]),
            "winT_s": np.ascontiguousarray(g["winT"][256 * c:256 * (c + 1)]),
            "woutT_s": np.ascontiguousarray(g["woutT"][256 * c:256 * (c + 1)]),
            "wq_s": np.ascontiguousarray(g["wq"][256 * c:256 * (c + 1)]),
            "wk_s": np.ascontiguousarray(g["wk"][256 * c:256 * (c + 1)]),
            "wv_s": np.ascontiguousarray(g["wv"][256 * c:256 * (c + 1)]),
            "wo_s": np.ascontiguousarray(g["wo"][256 * c:256 * (c + 1)]),
            "clf_s": np.ascontiguousarray(g["clfT"][256 * c:256 * (c + 1)]),
            "ind16": g["ind16"], "ind16h": g["ind16h"],
        })
    return maps


def host_prep_assemble(results):
    w = {}
    w["gdec"] = np.concatenate([np.asarray(r["gdec"]) for r in results], axis=1)
    w["udec"] = np.concatenate([np.asarray(r["udec"]) for r in results], axis=1)
    w["ddec"] = np.concatenate([np.asarray(r["ddec"]) for r in results], axis=0)
    w["l1gb"] = np.concatenate([np.asarray(r["l1gb"]) for r in results], axis=1)
    w["l1ub"] = np.concatenate([np.asarray(r["l1ub"]) for r in results], axis=1)
    w["l1db"] = np.concatenate([np.asarray(r["l1db"]) for r in results], axis=0)
    for nm in ("winb", "woutb", "wqb", "wkb", "wvb", "wob", "clfb"):
        w[nm] = np.concatenate([np.asarray(r[nm + "_s"]) for r in results], axis=0)
    return w


def kernel(**inputs):
    global RACE_DETECT
    RACE_DETECT = False
    g = host_prepare(inputs)
    nc = build_fused()
    nonce = int(np.random.randint(1, 1 << 30))
    maps = host_fused_inputs(inputs, g, nonce)
    res = run_bass_kernel_spmd(nc, maps, core_ids=list(range(8)))
    return unshard_out([np.asarray(res.results[c]["xo"]) for c in range(8)])



def emit_ln(p, S, *, x, xn, scr, xn_guard=None, scr_guard=None):
    sb = p.sb
    v, a, t = S["v"], S["a"], S["t"]
    psf = S.setdefault("psfree", {})

    def pwait(pn):
        if pn in psf:
            p.w("tensor", psf[pn][0], psf[pn][1])

    for gd in (xn_guard or []):
        p.w("vector", gd[0], gd[1])
    for gd in (scr_guard or []):
        p.w("scalar", gd[0], gd[1])
    p.do("vector", lambda E: E.tensor_copy(sb[xn][:, :], sb[x][:, :]), v)
    xb_ready = v.n
    p.w("scalar", v, xb_ready)
    p.do("scalar", lambda E: E.activation(sb[scr][:, :], sb[xn][:, :], AF.Square), a)
    sq_ready = a.n
    for (src, pa, pb, rsem, rdy) in [(xn, "p0", "p1", v, xb_ready),
                                     (scr, "p2", "p3", a, sq_ready)]:
        for th, pn in ((0, pa), (1, pb)):
            pwait(pn)
            p.w("tensor", rsem, rdy)
            for kc in range(16):
                p.do("tensor", lambda E, src=src, th=th, pn=pn, kc=kc: E.matmul(
                    p.ps[pn][0:1, 0:512], sb["ones_col"][:, 0:1],
                    sb[src][:, kc * 1024 + th * 512: kc * 1024 + (th + 1) * 512],
                    start=(kc == 0), stop=(kc == 15)),
                    t if kc == 15 else None)
    mm_done = t.n
    S["ln_mm_done"] = mm_done
    rows = sb["rows"]
    p.w("vector", t, mm_done)
    for th in range(2):
        pa = ("p0", "p1")[th]
        pb = ("p2", "p3")[th]
        p.do("vector", lambda E, pa=pa: E.tensor_single_scalar(
            rows[0:1, 0:512], p.ps[pa][0:1, 0:512], 1.0 / 2048.0, ALU.mult), v)
        p.do("vector", lambda E, pb=pb: E.tensor_scalar(
            rows[0:1, 512:1024], p.ps[pb][0:1, 0:512], 1.0 / 2048.0, 1e-5,
            ALU.mult, ALU.add), v)
        p.do("vector", lambda E: E.tensor_mul(sb["t512a"][0:1, 0:512],
                                              rows[0:1, 0:512], rows[0:1, 0:512]), v)
        p.do("vector", lambda E: E.tensor_sub(rows[0:1, 512:1024], rows[0:1, 512:1024],
                                              sb["t512a"][0:1, 0:512]), v)
        p.w("scalar", v)
        p.do("scalar", lambda E: E.activation(sb["t512a"][0:1, 0:512],
                                              rows[0:1, 512:1024], AF.Sqrt), a)
        p.w("vector", a)
        p.do("vector", lambda E: E.reciprocal(rows[0:1, 512:1024],
                                              sb["t512a"][0:1, 0:512]), v)
        p.do("vector", lambda E: E.tensor_copy(sb["rows_b"][0:1, 0:512],
                                               rows[0:1, 512:1024]), v)
        p.do("vector", lambda E: E.scalar_tensor_tensor(
            sb["rows_b"][0:1, 512:1024], rows[0:1, 0:512], -1.0,
            rows[0:1, 512:1024], ALU.mult, ALU.mult), v)
        rowsb_ready = v.n
        for pn, r0 in (("p4", 0), ("p5", 512)):
            pwait(pn)
            p.w("tensor", v, rowsb_ready)
            p.do("tensor", lambda E, pn=pn, r0=r0: E.matmul(
                p.ps[pn][:, 0:512], sb["rows_b"][0:1, 1024:1152],
                sb["rows_b"][0:1, r0:r0 + 512], start=True, stop=True), t)
        bc_done = t.n
        p.w("vector", t, bc_done)
        for c in range(16):
            p.do("vector", lambda E, c=c, th=th: E.tensor_mul(
                sb["t512a"][:, 0:512],
                sb[x][:, c * 1024 + th * 512:c * 1024 + (th + 1) * 512],
                p.ps["p4"][:, 0:512]), v)
            p.do("vector", lambda E, c=c, th=th: E.tensor_add(
                sb[xn][:, c * 1024 + th * 512:c * 1024 + (th + 1) * 512],
                sb["t512a"][:, 0:512], p.ps["p5"][:, 0:512]), v)
        psf["p4"] = psf["p5"] = (v, v.n)
        psf[pa] = psf[pb] = (v, v.n)
    S["xn_ready"] = v.n




def emit_ffn0(p, S, *, gdec, udec, ddec, x, xn, inter, n_sup=8,
              single_buf=False, deps=None, pre_super=None, tick=None):
    sb = p.sb
    v, a, t = S["v"], S["a"], S["t"]
    psf = S.setdefault("psfree", {})
    lg = (S["lg0"], S["lg1"])
    lu = (S["lu0"], S["lu1"])
    wfree = S.setdefault("wfree", {})

    def pwait(pn):
        if pn in psf:
            p.w("tensor", psf[pn][0], psf[pn][1])

    def load_win(buf, lsem, q, dram, n0, row_len, dep=None):
        key = f"{buf}{q}"
        if key in wfree:
            p.w("sync", wfree[key][0], wfree[key][1])
        if dep is not None:
            for ds, dn in dep:
                p.w("sync", ds, dn)
        p.do("sync", lambda E, buf=buf, q=q, dram=dram, n0=n0, row_len=row_len:
             E.dma_start(
                 out=sb[f"{buf}{q}"][:, 0:8192],
                 in_=bass.AP(dram.tensor, n0,
                             [[row_len, 128], [128 * row_len, 16], [1, 512]])),
             lsem, 16)
        return lsem.n

    xn_ready = S["xn_ready"]
    first_group = [True]

    def mm_group(wbuf, q, pn, rhs_name, rhs_off, ld_sem, ld_cnt, mi):
        pwait(pn)
        p.w("tensor", ld_sem, ld_cnt)
        if first_group[0]:
            p.w("tensor", v, xn_ready)
            first_group[0] = False
        for kc in range(16):
            p.do("tensor", lambda E, wbuf=wbuf, q=q, pn=pn, kc=kc, mi=mi,
                 rhs_name=rhs_name, rhs_off=rhs_off: E.matmul(
                     p.ps[pn][:, 0:512],
                     sb[f"{wbuf}{q}"][:, kc * 512 + mi * 128:kc * 512 + (mi + 1) * 128],
                     sb[rhs_name][:, kc * 1024 + rhs_off:kc * 1024 + rhs_off + 512],
                     start=(kc == 0), stop=(kc == 15)),
                 t if kc == 15 else None)
        return t.n

    for sup in range(n_sup):
        if pre_super is not None:
            pre_super(sup)
        depg = depu = depd = None
        if deps is not None:
            depg = deps.get(("g", sup))
            depu = deps.get(("u", sup))
            depd = deps.get(("d", sup))
        for wi in range(4):
            q = 0 if single_buf else wi % 2
            n0 = sup * 2048 + wi * 512
            ld_g = load_win("wg", lg[q], q, gdec, n0, 16384, dep=depg)
            ld_u = load_win("wu", lu[q], q, udec, n0, 16384, dep=depu)
            for mi in range(4):
                for th in range(2):
                    png = f"p{(mi * 2 + th) % 2}"
                    pnu = f"p{2 + (mi * 2 + th) % 2}"
                    g_done = mm_group("wg", q, png, xn, th * 512, lg[q], ld_g, mi)
                    u_done = mm_group("wu", q, pnu, xn, th * 512, lu[q], ld_u, mi)
                    p.w("scalar", t, g_done)
                    if "t512a_free" in S:
                        p.w("scalar", v, S["t512a_free"])
                    p.do("scalar", lambda E, png=png: E.activation(
                        sb["t512a"][:, 0:512], p.ps[png][0:128, 0:512],
                        AF.Gelu_apprx_tanh), a)
                    gl_done = a.n
                    psf[png] = (a, gl_done)
                    nch = wi * 4 + mi
                    p.w("vector", a, gl_done)
                    p.w("vector", t, u_done)
                    p.do("vector", lambda E, nch=nch, th=th, pnu=pnu: E.tensor_mul(
                        sb[inter][:, nch * 1024 + th * 512:nch * 1024 + (th + 1) * 512],
                        sb["t512a"][:, 0:512], p.ps[pnu][0:128, 0:512]), v)
                    psf[pnu] = (v, v.n)
                    S["t512a_free"] = v.n
            wfree[f"wg{q}"] = (t, g_done)
            wfree[f"wu{q}"] = (t, u_done)
            if tick is not None:
                tick()
        inter_done = v.n
        for wi in range(4):
            q = 0 if single_buf else wi % 2
            d0 = sup * 16 * 128 * 2048 + wi * 512
            ld_d = load_win("wg", lg[q], q, ddec, d0, 2048, dep=depd)
            for mi in range(4):
                for th in range(2):
                    pnd = f"p{4 + (mi * 2 + th) % 2}"
                    pwait(pnd)
                    p.w("tensor", lg[q], ld_d)
                    p.w("tensor", v, inter_done)
                    for kc in range(16):
                        p.do("tensor", lambda E, q=q, pnd=pnd, kc=kc, mi=mi, th=th:
                             E.matmul(
                                 p.ps[pnd][:, 0:512],
                                 sb[f"wg{q}"][:, kc * 512 + mi * 128:kc * 512 + (mi + 1) * 128],
                                 sb[inter][:, kc * 1024 + th * 512:kc * 1024 + (th + 1) * 512],
                                 start=(kc == 0), stop=(kc == 15)),
                             t if kc == 15 else None)
                    d_done = t.n
                    dch = wi * 4 + mi
                    p.w("vector", t, d_done)
                    p.do("vector", lambda E, dch=dch, th=th, pnd=pnd: E.tensor_add(
                        sb[x][:, dch * 1024 + th * 512:dch * 1024 + (th + 1) * 512],
                        sb[x][:, dch * 1024 + th * 512:dch * 1024 + (th + 1) * 512],
                        p.ps[pnd][0:128, 0:512]), v)
                    psf[pnd] = (v, v.n)
            wfree[f"wg{q}"] = (t, d_done)
            if tick is not None:
                tick()
    S["x_ready"] = v.n


def alloc_main_bufs(p):
    p.sbuf("x", [128, 16 * 1024], F32)
    p.sbuf("bufA", [128, 16 * 1024], BF16)
    p.sbuf("bufB", [128, 16 * 1024], BF16)
    p.sbuf("wg0", [128, 8192], BF16)
    p.sbuf("wg1", [128, 8192], BF16)
    p.sbuf("wu0", [128, 8192], BF16)
    p.sbuf("wu1", [128, 8192], BF16)
    p.sbuf("rows", [1, 1024], F32)
    p.sbuf("rows_b", [1, 1024 + 128], BF16)
    p.sbuf("t512a", [128, 512], F32)
    p.sbuf("ones_col", [128, 1], BF16)
    for i in range(8):
        p.psum(f"p{i}", [128, 512])


def emit_consts(p, S):
    v = S["v"]
    p.do("vector", lambda E: E.memset(p.sb["ones_col"][:], 1.0), v)
    p.do("vector", lambda E: E.memset(p.sb["rows_b"][0:1, 1024:1152], 1.0), v)
    if "c16k" in p.sb:
        p.do("vector", lambda E: E.memset(p.sb["c16k"][:, 0:1], 16128.0), v)


def build_ffn0(n_sup=8):
    nc = _new_nc()
    xin = nc.dram_tensor("xin", [2048, 1024], F32, kind="ExternalInput").ap()
    gdec = nc.dram_tensor("gdec", [2048, 16384], BF16, kind="ExternalInput").ap()
    udec = nc.dram_tensor("udec", [2048, 16384], BF16, kind="ExternalInput").ap()
    ddec = nc.dram_tensor("ddec", [16384, 2048], BF16, kind="ExternalInput").ap()
    xo = nc.dram_tensor("xo", [2048, 1024], F32, kind="ExternalOutput").ap()
    with contextlib.ExitStack() as ctx:
        p = Prog(nc, ctx)
        alloc_main_bufs(p)
        S = {k: p.sem(k) for k in ("v", "a", "t", "lg0", "lg1", "lu0", "lu1",
                                   "lx", "sx")}
        emit_consts(p, S)
        p.do("scalar", lambda E: E.dma_start(
            out=p.sb["x"][:, :],
            in_=bass.AP(xin.tensor, 0, [[1024, 128], [128 * 1024, 16], [1, 1024]])),
            S["lx"], 16)
        p.w("vector", S["lx"])
        p.w("scalar", S["lx"])
        emit_ln(p, S, x="x", xn="bufA", scr="bufB")
        emit_ffn0(p, S, gdec=gdec, udec=udec, ddec=ddec, x="x", xn="bufA",
                  inter="bufB", n_sup=n_sup)
        p.w("scalar", S["v"], S["x_ready"])
        p.do("scalar", lambda E: E.dma_start(
            out=bass.AP(xo.tensor, 0, [[1024, 128], [128 * 1024, 16], [1, 1024]]),
            in_=p.sb["x"][:, :]), S["sx"], 16)
        p.w("scalar", S["sx"])
        p.emit()
    return nc



_REG_UID = [0]


def _rn(base):
    _REG_UID[0] += 1
    return f"{base}{_REG_UID[0]}"


def emit_xchg_write(p, S, *, shared_t, off_idx, sbuf_ap, nrows, rowlen,
                    flag_t, flagval_row, wsem):
    def body(E):
        with E.register(_rn("r_off")) as r_off:
            E.reg_load(r_off, p.sb["xoffs_sb"][0:1, off_idx:off_idx + 1])
            E.dma_start(out=bass.AP(shared_t.tensor, r_off, [[rowlen, nrows], [1, rowlen]]),
                        in_=sbuf_ap).then_inc(wsem.h, 16)
            wsem.reserve(16)
            E.wait_ge(wsem.h, wsem.n)
            E.reg_load(r_off, p.sb["xoffs_sb"][0:1, 2:3])
            E.dma_start(out=bass.AP(flag_t.tensor, r_off, [[16, 1], [1, 16]]),
                        in_=p.sb["flagv_sb"][flagval_row:flagval_row + 1, 0:16]
                        ).then_inc(wsem.h, 16)
            wsem.reserve(16)
            E.wait_ge(wsem.h, wsem.n)
    p.do("gpsimd", body)


def emit_xchg_poll_read(p, S, *, shared_t, off_idx, sbuf_ap, nrows, rowlen,
                        flag_t, flagval_row, psem, rsem, roleint_col=0,
                        maxpoll=2000000):
    def body(E):
        with (
            E.register(_rn("rf")) as rf,
            E.register(_rn("rc")) as rc,
            E.register(_rn("rcont")) as rcont,
            E.register(_rn("rtmp")) as rtmp,
            E.register(_rn("r_off")) as r_off,
        ):
            E.reg_load(rcont, p.sb["roleint_sb"][0:1, roleint_col:roleint_col + 1])
            E.reg_load(r_off, p.sb["xoffs_sb"][0:1, 3:4])
            E.reg_mov(rc, 0)
            with E.While(rcont):
                E.dma_start(out=p.sb["flagsc"][0:1, 0:16],
                            in_=bass.AP(flag_t.tensor, r_off, [[16, 1], [1, 16]])
                            ).then_inc(psem.h, 16)
                E.reg_add(rc, rc, 16)
                E.wait_ge(psem.h, rc)
                E.reg_load(rf, p.sb["flagsc"][0:1, 0:1])
                E.reg_load(rtmp, p.sb["flagv_sb"][flagval_row:flagval_row + 1, 0:1])
                E.reg_alu(rcont, rf, rtmp, ALU.not_equal)
                E.reg_alu(rtmp, rc, 16 * maxpoll, ALU.is_lt)
                E.reg_alu(rcont, rcont, rtmp, ALU.bitwise_and)
            E.reg_load(r_off, p.sb["xoffs_sb"][0:1, off_idx:off_idx + 1])
            E.dma_start(out=sbuf_ap,
                        in_=bass.AP(shared_t.tensor, r_off, [[rowlen, nrows], [1, rowlen]])
                        ).then_inc(rsem.h, 16)
            rsem.reserve(16)
            E.wait_ge(rsem.h, rsem.n)
    p.do("gpsimd", body)


def alloc_xchg_bufs(p):
    p.sbuf("xoffs_sb", [1, 8], I32)
    p.sbuf("flagv_sb", [4, 16], I32)
    p.sbuf("roleint_sb", [1, 16], I32)
    p.sbuf("flagsc", [1, 16], I32)
    p.sbuf("rolem_sb", [128, 1], I32)
    p.sbuf("decay_pt_sb", [128, 16], F32)
    p.sbuf("bias_pt_sb", [128, 16], F32)
    p.sbuf("dsig", [128, 16], F32)
    p.sbuf("omd", [128, 16], F32)
    p.sbuf("fin", [128, 16], F32)
    p.sbuf("pfin", [128, 16], F32)



def emit_attn0(p, S, *, winb, woutb, x, xn, prj, xfin, xflag):
    sb = p.sb
    v, a, t = S["v"], S["a"], S["t"]
    psf = S.setdefault("psfree", {})
    wfree = S.setdefault("wfree", {})
    wsems = (S["lg0"], S["lg1"], S["lu0"], S["lu1"])
    wnames = ("wg0", "wg1", "wu0", "wu1")

    def pwait(pn):
        if pn in psf:
            p.w("tensor", psf[pn][0], psf[pn][1])

    p.do("scalar", lambda E: E.activation(sb["dsig"][:, 0:16], sb["decay_pt_sb"][:, 0:16],
                                          AF.Sigmoid), a)
    p.w("vector", a)
    p.do("vector", lambda E: E.tensor_scalar(
        sb["omd"][:, 0:16], sb["dsig"][:, 0:16], -1.0, 1.0, ALU.mult, ALU.add), v)
    omd_ready = v.n

    def load_w_bf16(dram):
        for b in range(4):
            key = wnames[b]
            if key in wfree:
                p.w("sync", wfree[key][0], wfree[key][1])
            p.do("sync", lambda E, b=b, dram=dram: E.dma_start(
                out=sb[wnames[b]][:, 0:8192],
                in_=bass.AP(dram.tensor, b * 4 * 128 * 2048,
                            [[2048, 128], [128 * 2048, 4], [1, 2048]])),
                wsems[b], 16)
        return [ws.n for ws in wsems]

    w_ready = load_w_bf16(winb)

    def wmm(mi, th, pn, rdy):
        pwait(pn)
        for b in range(4):
            p.w("tensor", wsems[b], rdy[b])
        for kc in range(16):
            wb = sb[wnames[kc // 4]]
            p.do("tensor", lambda E, mi=mi, th=th, kc=kc, pn=pn, wb=wb: E.matmul(
                p.ps[pn][:, 0:512],
                wb[:, (kc % 4) * 2048 + mi * 128:(kc % 4) * 2048 + (mi + 1) * 128],
                sb[xn][:, kc * 1024 + th * 512:kc * 1024 + (th + 1) * 512],
                start=(kc == 0), stop=(kc == 15)), t if kc == 15 else None)
        return t.n

    p.w("tensor", v, S["xn_ready"])
    for mi in range(16):
        for th in range(2):
            pn = f"p{(mi * 2 + th) % 2}"
            mmd = wmm(mi, th, pn, w_ready)
            p.w("vector", t, mmd)
            p.w("vector", v, omd_ready)
            p.do("vector", lambda E, mi=mi, th=th, pn=pn: E.tensor_scalar(
                sb[prj][:, mi * 1024 + th * 512:mi * 1024 + (th + 1) * 512],
                p.ps[pn][0:128, 0:512], sb["omd"][:, mi:mi + 1], None,
                ALU.mult), v)
            psf[pn] = (v, v.n)
    prj_ready = v.n
    win_consumed = t.n

    emit_xchg_poll_read(p, S, shared_t=xfin, off_idx=1, sbuf_ap=sb["pfin"][:, 0:16],
                        nrows=128, rowlen=16, flag_t=xflag, flagval_row=0,
                        psem=S["poll0"], rsem=S["xls"], roleint_col=0)
    xch = S["xch"]
    xch.reserve(1)
    p.on("gpsimd", lambda E, s=xch.h: E.nop().then_inc(s, 1))
    p.w("vector", xch)
    p.do("vector", lambda E: E.tensor_tensor(
        sb["pfin"][:, 0:16].bitcast(I32), sb["pfin"][:, 0:16].bitcast(I32),
        sb["rolem_sb"][:, 0:1].bitcast(I32).broadcast_to([128, 16]),
        ALU.bitwise_and), v)
    p.w("vector", v, prj_ready)
    for c in range(16):
        p.do("vector", lambda E, c=c: E.tensor_tensor_scan(
            sb[xn][:, c * 1024:(c + 1) * 1024],
            sb["dsig"][:, c:c + 1].broadcast_to([128, 1024]),
            sb[prj][:, c * 1024:(c + 1) * 1024],
            sb["pfin"][:, c:c + 1], ALU.mult, ALU.add), v)
    scans_done = v.n
    p.do("vector", lambda E: E.tensor_copy(
        sb["fin"][:, 0:16],
        bass.AP(sb[xn], 1023, [[16384, 128], [1024, 16]])), v)
    fin_ready = v.n
    p.w("gpsimd", v, fin_ready)
    emit_xchg_write(p, S, shared_t=xfin, off_idx=0, sbuf_ap=sb["fin"][:, 0:16],
                    nrows=128, rowlen=16, flag_t=xflag, flagval_row=0,
                    wsem=S["xws"])

    for b in range(4):
        wfree[wnames[b]] = (t, win_consumed)
    w2_ready = load_w_bf16(woutb)
    p.w("tensor", v, scans_done)
    for mi in range(16):
        for th in range(2):
            pn = f"p{2 + (mi * 2 + th) % 2}"
            mmd = wmm(mi, th, pn, w2_ready)
            p.w("vector", t, mmd)
            p.do("vector", lambda E, mi=mi, th=th, pn=pn: E.scalar_tensor_tensor(
                sb[x][:, mi * 1024 + th * 512:mi * 1024 + (th + 1) * 512],
                p.ps[pn][0:128, 0:512], sb["bias_pt_sb"][:, mi:mi + 1],
                sb[x][:, mi * 1024 + th * 512:mi * 1024 + (th + 1) * 512],
                ALU.add, ALU.add), v)
            psf[pn] = (v, v.n)
    for b in range(4):
        wfree[wnames[b]] = (t, t.n)
    S["x_ready"] = v.n


def build_attn0():
    nc = _new_nc()
    xin = nc.dram_tensor("xin", [2048, 1024], F32, kind="ExternalInput").ap()
    winb = nc.dram_tensor("winb", [2048, 2048], BF16, kind="ExternalInput").ap()
    woutb = nc.dram_tensor("woutb", [2048, 2048], BF16, kind="ExternalInput").ap()
    decay_pt = nc.dram_tensor("decay_pt", [128, 16], F32, kind="ExternalInput").ap()
    bias_pt = nc.dram_tensor("bias_pt", [128, 16], F32, kind="ExternalInput").ap()
    xoffs = nc.dram_tensor("xoffs", [1, 8], I32, kind="ExternalInput").ap()
    flagvals = nc.dram_tensor("flagvals", [4, 16], I32, kind="ExternalInput").ap()
    roleint = nc.dram_tensor("roleint", [1, 2], I32, kind="ExternalInput").ap()
    rolem = nc.dram_tensor("rolem", [128, 1], I32, kind="ExternalInput").ap()
    xo = nc.dram_tensor("xo", [2048, 1024], F32, kind="ExternalOutput").ap()
    xfin = nc.dram_tensor("xfin", [256, 16], F32, kind="Internal",
                          addr_space="Shared").ap()
    xflag = nc.dram_tensor("xflag", [2, 16], I32, kind="Internal",
                           addr_space="Shared").ap()
    with contextlib.ExitStack() as ctx:
        p = Prog(nc, ctx)
        alloc_main_bufs(p)
        alloc_xchg_bufs(p)
        S = {k: p.sem(k) for k in ("v", "a", "t", "lg0", "lg1", "lu0", "lu1",
                                   "lx", "sx", "poll0", "xls", "xws", "xch")}
        emit_consts(p, S)
        for nm, dst in [("decay_pt", "decay_pt_sb"), ("bias_pt", "bias_pt_sb"),
                        ("flagvals", "flagv_sb"), ("roleint", "roleint_sb"),
                        ("rolem", "rolem_sb"), ("xoffs", "xoffs_sb")]:
            ap_in = {"decay_pt": decay_pt, "bias_pt": bias_pt, "flagvals": flagvals,
                     "roleint": roleint, "rolem": rolem, "xoffs": xoffs}[nm]
            shp = ap_in.shape
            p.do("scalar", lambda E, ap_in=ap_in, dst=dst, shp=shp: E.dma_start(
                out=p.sb[dst][0:shp[0], 0:shp[1]], in_=ap_in), S["lx"], 16)
        p.do("scalar", lambda E: E.dma_start(
            out=p.sb["x"][:, :],
            in_=bass.AP(xin.tensor, 0, [[1024, 128], [128 * 1024, 16], [1, 1024]])),
            S["lx"], 16)
        p.w("vector", S["lx"])
        p.w("scalar", S["lx"])
        p.w("gpsimd", S["lx"])
        emit_ln(p, S, x="x", xn="bufA", scr="bufB")
        emit_attn0(p, S, winb=winb, woutb=woutb, x="x", xn="bufA", prj="bufB",
                   xfin=xfin, xflag=xflag)
        p.w("scalar", S["v"], S["x_ready"])
        p.do("scalar", lambda E: E.dma_start(
            out=bass.AP(xo.tensor, 0, [[1024, 128], [128 * 1024, 16], [1, 1024]]),
            in_=p.sb["x"][:, :]), S["sx"], 16)
        p.w("scalar", S["sx"])
        p.w("gpsimd", S["xws"])
        p.emit()
    return nc


def host_xchg_inputs(c, nonce):
    role = c % 2
    return {
        "xoffs": np.array([[role * 128 * 16, (1 - role) * 128 * 16,
                            role * 16, (1 - role) * 16,
                            role * 1024, 128 * 8192 + role * 4096, 0, 0]], np.int32),
        "flagvals": np.tile(np.arange(4, dtype=np.int32)[:, None] + nonce, (1, 16)),
        "roleint": np.array([[role, 1, 1 - role, role]], np.int32),
        "rolem": np.full((128, 1), -role, np.int32),
    }



def emit_ffn1(p, S, *, l1gb, l1ub, l1db, x, xn, inter):
    sb = p.sb
    v, a, t = S["v"], S["a"], S["t"]
    psf = S.setdefault("psfree", {})
    wfree = S.setdefault("wfree", {})
    lg = (S["lg0"], S["lg1"])
    lu = (S["lu0"], S["lu1"])

    def pwait(pn):
        if pn in psf:
            p.w("tensor", psf[pn][0], psf[pn][1])

    rt = sb["rt"]

    pwait("p6")
    p.w("tensor", v, S["xn_ready"])
    p.w("tensor", S["lmisc"])
    if "lmiscg" in S:
        p.w("tensor", S["lmiscg"])
    for tt in range(8):
        for kc in range(16):
            p.do("tensor", lambda E, tt=tt, kc=kc: E.matmul(
                p.ps["p6"][:, 16 * tt:16 * tt + 12],
                sb[xn][:, kc * 1024 + tt * 128:kc * 1024 + (tt + 1) * 128],
                sb["clfb_sb"][:, kc * 12:(kc + 1) * 12],
                start=(kc == 0), stop=(kc == 15)), t if kc == 15 else None)
    clf_done = t.n
    pwait("p7")
    p.do("tensor", lambda E: E.matmul(
        p.ps["p7"][:, 0:16], sb["rows_b"][0:1, 1024:1152],
        sb["escb_sb"][0:1, 0:16], start=True, stop=True), t)
    escbc_done = t.n

    p.w("vector", t, escbc_done)
    for tt in range(8):
        base = tt * 48
        col = lambda o, w=1: rt[:, base + o:base + o + w]
        ps6 = p.ps["p6"][:, 16 * tt:16 * tt + 12]
        p.do("vector", lambda E, ps6=ps6, base=base: E.tensor_copy(
            rt[:, base:base + 12], ps6), v)
        p.do("vector", lambda E, base=base: E.tensor_reduce(
            sb["rcol"][:, 0:1], rt[:, base:base + 12].rearrange("p (s n) -> p s n", s=1),
            axis=AXX, op=ALU.max, negate=True), v)
        nm_ready = v.n
        p.w("scalar", v, nm_ready)
        p.do("scalar", lambda E, base=base: E.activation(
            rt[:, base + 16:base + 28], rt[:, base:base + 12], AF.Exp,
            bias=sb["rcol"][:, 0:1]), a)
        p.w("vector", a)
        p.do("vector", lambda E, base=base: E.tensor_reduce(
            sb["rcol"][:, 1:2], rt[:, base + 16:base + 28].rearrange("p (s n) -> p s n", s=1),
            axis=AXX, op=ALU.add), v)
        p.do("vector", lambda E: E.reciprocal(sb["rcol"][:, 2:3], sb["rcol"][:, 1:2]), v)
        p.do("vector", lambda E, base=base: E.tensor_scalar(
            rt[:, base:base + 12], rt[:, base + 16:base + 28], sb["rcol"][:, 2:3],
            None, ALU.mult), v)
        p.do("vector", lambda E, base=base: E.tensor_tensor(
            rt[:, base + 16:base + 28], rt[:, base:base + 12],
            p.ps["p7"][:, 0:12], ALU.add), v)
        p.do("vector", lambda E, base=base: E.tensor_reduce(
            sb["rcol"][:, 3:4], rt[:, base + 16:base + 28].rearrange("p (s n) -> p s n", s=1),
            axis=AXX, op=ALU.max), v)
        p.do("vector", lambda E, base=base: E.tensor_scalar(
            rt[:, base + 32:base + 44], rt[:, base + 16:base + 28], sb["rcol"][:, 3:4],
            None, ALU.is_equal), v)
        p.do("vector", lambda E, base=base: E.scalar_tensor_tensor(
            rt[:, base + 32:base + 44], rt[:, base + 32:base + 44], -1e9,
            rt[:, base + 16:base + 28], ALU.mult, ALU.add), v)
        p.do("vector", lambda E, base=base: E.tensor_reduce(
            sb["rcol"][:, 4:5], rt[:, base + 32:base + 44].rearrange("p (s n) -> p s n", s=1),
            axis=AXX, op=ALU.max), v)
        p.do("vector", lambda E, base=base: E.tensor_scalar(
            rt[:, base + 32:base + 44], rt[:, base + 16:base + 28], sb["rcol"][:, 4:5],
            None, ALU.is_ge), v)
        p.do("vector", lambda E, base=base: E.scalar_tensor_tensor(
            rt[:, base + 16:base + 28], rt[:, base:base + 12], 0.75,
            rt[:, base + 32:base + 44], ALU.mult, ALU.mult), v)
        p.do("vector", lambda E, base=base: E.tensor_reduce(
            rt[:, base + 24:base + 25],
            rt[:, base + 24:base + 28].rearrange("p (s n) -> p s n", s=1),
            axis=AXX, op=ALU.add), v)
    route_done = v.n
    psf["p6"] = psf["p7"] = (v, route_done)

    def bcast_expert(e):
        pwait("p6")
        pwait("p7")
        p.w("tensor", v, route_done)
        for tt in range(8):
            pnm = "p6" if tt < 4 else "p7"
            p.do("tensor", lambda E, tt=tt, e=e, pnm=pnm: E.transpose(
                p.ps[pnm][0:1, (tt % 4) * 128:(tt % 4 + 1) * 128],
                rt[:, tt * 48 + 16 + e:tt * 48 + 16 + e + 1],
                sb["identf_sb"][:, 0:128]), t)
        trd = t.n
        p.w("vector", t, trd)
        for tt in range(8):
            pnm = "p6" if tt < 4 else "p7"
            p.do("vector", lambda E, tt=tt, pnm=pnm: E.tensor_copy(
                sb["rows_b"][0:1, tt * 128:(tt + 1) * 128],
                p.ps[pnm][0:1, (tt % 4) * 128:(tt % 4 + 1) * 128]), v)
        rowdone = v.n
        psf["p6"] = (v, rowdone)
        p.w("tensor", v, rowdone)
        for th, pn in ((0, "p6"), (1, "p7")):
            p.do("tensor", lambda E, th=th, pn=pn: E.matmul(
                p.ps[pn][:, 0:512], sb["rows_b"][0:1, 1024:1152],
                sb["rows_b"][0:1, th * 512:(th + 1) * 512], start=True, stop=True), t)
        return t.n

    bz = bcast_expert(8)
    p.w("vector", t, bz)
    for c in range(16):
        for th, pn in ((0, "p6"), (1, "p7")):
            p.do("vector", lambda E, c=c, th=th, pn=pn: E.tensor_mul(
                sb["t512a"][:, 0:512],
                sb[xn][:, c * 1024 + th * 512:c * 1024 + (th + 1) * 512],
                p.ps[pn][:, 0:512]), v)
            p.do("vector", lambda E, c=c, th=th: E.tensor_add(
                sb[x][:, c * 1024 + th * 512:c * 1024 + (th + 1) * 512],
                sb[x][:, c * 1024 + th * 512:c * 1024 + (th + 1) * 512],
                sb["t512a"][:, 0:512]), v)
    psf["p6"] = psf["p7"] = (v, v.n)

    for e in range(8):
        bdone = bcast_expert(e)
        web_free_at = None
        for wi in range(2):
            q = wi % 2
            i0 = e * 1024 + wi * 512
            for stream, dram, lsem, wbuf in (("g", l1gb, lg[q], f"wg{q}"),
                                             ("u", l1ub, lu[q], f"wu{q}")):
                if wbuf in wfree:
                    p.w("sync", wfree[wbuf][0], wfree[wbuf][1])
                p.do("sync", lambda E, dram=dram, i0=i0, wbuf=wbuf: E.dma_start(
                    out=sb[wbuf][:, 0:8192],
                    in_=bass.AP(dram.tensor, i0,
                                [[8192, 128], [128 * 8192, 16], [1, 512]])),
                    lsem, 16)
            ldg, ldu = lg[q].n, lu[q].n
            for mi in range(4):
                for th in range(2):
                    png = f"p{(mi * 2 + th) % 2}"
                    pnu = f"p{2 + (mi * 2 + th) % 2}"
                    pwait(png)
                    pwait(pnu)
                    p.w("tensor", lg[q], ldg)
                    p.w("tensor", lu[q], ldu)
                    for kc in range(16):
                        p.do("tensor", lambda E, q=q, png=png, kc=kc, mi=mi, th=th:
                             E.matmul(
                                 p.ps[png][:, 0:512],
                                 sb[f"wg{q}"][:, kc * 512 + mi * 128:kc * 512 + (mi + 1) * 128],
                                 sb[xn][:, kc * 1024 + th * 512:kc * 1024 + (th + 1) * 512],
                                 start=(kc == 0), stop=(kc == 15)),
                             t if kc == 15 else None)
                    g_done = t.n
                    for kc in range(16):
                        p.do("tensor", lambda E, q=q, pnu=pnu, kc=kc, mi=mi, th=th:
                             E.matmul(
                                 p.ps[pnu][:, 0:512],
                                 sb[f"wu{q}"][:, kc * 512 + mi * 128:kc * 512 + (mi + 1) * 128],
                                 sb[xn][:, kc * 1024 + th * 512:kc * 1024 + (th + 1) * 512],
                                 start=(kc == 0), stop=(kc == 15)),
                             t if kc == 15 else None)
                    u_done = t.n
                    p.w("scalar", t, g_done)
                    if "t512a_free" in S:
                        p.w("scalar", v, S["t512a_free"])
                    p.do("scalar", lambda E, png=png: E.activation(
                        sb["t512a"][:, 0:512], p.ps[png][0:128, 0:512], AF.Silu), a)
                    sl_done = a.n
                    psf[png] = (a, sl_done)
                    ich = wi * 4 + mi
                    p.w("vector", a, sl_done)
                    p.w("vector", t, u_done)
                    p.w("vector", t, bdone)
                    wepn = "p6" if th == 0 else "p7"
                    p.do("vector", lambda E, wepn=wepn: E.tensor_mul(
                        sb["t512a"][:, 0:512], sb["t512a"][:, 0:512],
                        p.ps[wepn][:, 0:512]), v)
                    p.do("vector", lambda E, ich=ich, th=th, pnu=pnu: E.tensor_mul(
                        sb[inter][:, ich * 1024 + th * 512:ich * 1024 + (th + 1) * 512],
                        sb["t512a"][:, 0:512], p.ps[pnu][0:128, 0:512]), v)
                    psf[pnu] = (v, v.n)
                    S["t512a_free"] = v.n
                    web_free_at = v.n
            wfree[f"wg{q}"] = (t, g_done)
            wfree[f"wu{q}"] = (t, u_done)
        psf["p6"] = psf["p7"] = (v, web_free_at)
        inter_done = v.n
        for wi in range(4):
            q = wi % 2
            key = f"wg{q}"
            if key in wfree:
                p.w("sync", wfree[key][0], wfree[key][1])
            p.do("sync", lambda E, e=e, wi=wi, q=q: E.dma_start(
                out=sb[f"wg{q}"][:, 0:4096],
                in_=bass.AP(l1db.tensor, e * 1024 * 2048 + wi * 512,
                            [[2048, 128], [128 * 2048, 8], [1, 512]])), lg[q], 16)
            ldd = lg[q].n
            for mi in range(4):
                for th in range(2):
                    pnd = f"p{4 + (mi * 2 + th) % 2}"
                    pwait(pnd)
                    p.w("tensor", lg[q], ldd)
                    p.w("tensor", v, inter_done)
                    for kc in range(8):
                        p.do("tensor", lambda E, q=q, pnd=pnd, kc=kc, mi=mi, th=th:
                             E.matmul(
                                 p.ps[pnd][:, 0:512],
                                 sb[f"wg{q}"][:, kc * 512 + mi * 128:kc * 512 + (mi + 1) * 128],
                                 sb[inter][:, kc * 1024 + th * 512:kc * 1024 + (th + 1) * 512],
                                 start=(kc == 0), stop=(kc == 7)),
                             t if kc == 7 else None)
                    d_done = t.n
                    dch = wi * 4 + mi
                    p.w("vector", t, d_done)
                    p.do("vector", lambda E, dch=dch, th=th, pnd=pnd: E.tensor_add(
                        sb[x][:, dch * 1024 + th * 512:dch * 1024 + (th + 1) * 512],
                        sb[x][:, dch * 1024 + th * 512:dch * 1024 + (th + 1) * 512],
                        p.ps[pnd][0:128, 0:512]), v)
                    psf[pnd] = (v, v.n)
            wfree[f"wg{q}"] = (t, d_done)
    S["x_ready"] = v.n


def alloc_attn1_bufs(p):
    p.sbuf("negc", [128, 1], F32)
    p.sbuf("tb512", [128, 512], BF16)


def alloc_ffn1_bufs(p):
    p.sbuf("rt", [128, 8 * 48], F32)
    p.sbuf("rcol", [128, 8], F32)
    p.sbuf("clfb_sb", [128, 16 * 12], BF16)
    p.sbuf("escb_sb", [1, 16], BF16)
    p.sbuf("escf_sb", [1, 16], F32)
    p.sbuf("identf_sb", [128, 128], F32)


def build_ffn1():
    nc = _new_nc()
    xin = nc.dram_tensor("xin", [2048, 1024], F32, kind="ExternalInput").ap()
    l1gb = nc.dram_tensor("l1gb", [2048, 8192], BF16, kind="ExternalInput").ap()
    l1ub = nc.dram_tensor("l1ub", [2048, 8192], BF16, kind="ExternalInput").ap()
    l1db = nc.dram_tensor("l1db", [8192, 2048], BF16, kind="ExternalInput").ap()
    clfb = nc.dram_tensor("clfb", [2048, 12], BF16, kind="ExternalInput").ap()
    escb = nc.dram_tensor("escb", [1, 12], F32, kind="ExternalInput").ap()
    identf = nc.dram_tensor("identf", [128, 128], F32, kind="ExternalInput").ap()
    xo = nc.dram_tensor("xo", [2048, 1024], F32, kind="ExternalOutput").ap()
    with contextlib.ExitStack() as ctx:
        p = Prog(nc, ctx)
        alloc_main_bufs(p)
        alloc_ffn1_bufs(p)
        S = {k: p.sem(k) for k in ("v", "a", "t", "lg0", "lg1", "lu0", "lu1",
                                   "lx", "sx", "lmisc")}
        emit_consts(p, S)
        p.do("scalar", lambda E: E.dma_start(
            out=p.sb["clfb_sb"][:, :],
            in_=bass.AP(clfb.tensor, 0, [[12, 128], [128 * 12, 16], [1, 12]])),
            S["lmisc"], 16)
        p.do("scalar", lambda E: E.dma_start(out=p.sb["escf_sb"][0:1, 0:12], in_=escb),
             S["lmisc"], 16)
        p.do("scalar", lambda E: E.dma_start(out=p.sb["identf_sb"][:, :], in_=identf),
             S["lmisc"], 16)
        p.do("vector", lambda E: E.memset(p.sb["escb_sb"][0:1, 0:16], 0.0), S["v"])
        p.w("vector", S["lmisc"])
        p.do("vector", lambda E: E.tensor_copy(p.sb["escb_sb"][0:1, 0:12],
                                               p.sb["escf_sb"][0:1, 0:12]), S["v"])
        p.do("scalar", lambda E: E.dma_start(
            out=p.sb["x"][:, :],
            in_=bass.AP(xin.tensor, 0, [[1024, 128], [128 * 1024, 16], [1, 1024]])),
            S["lx"], 16)
        p.w("vector", S["lx"])
        p.w("scalar", S["lx"])
        p.w("vector", S["lmisc"])
        emit_ln(p, S, x="x", xn="bufA", scr="bufB")
        emit_ffn1(p, S, l1gb=l1gb, l1ub=l1ub, l1db=l1db, x="x", xn="bufA",
                  inter="bufB")
        p.w("scalar", S["v"], S["x_ready"])
        p.do("scalar", lambda E: E.dma_start(
            out=bass.AP(xo.tensor, 0, [[1024, 128], [128 * 1024, 16], [1, 1024]]),
            in_=p.sb["x"][:, :]), S["sx"], 16)
        p.w("scalar", S["sx"])
        p.emit()
    return nc


def emit_xchg_write2(p, S, *, shared_t, flag_t, flagval_row, wsem, parts):
    def body(E):
        with E.register(_rn("r_off")) as r_off:
            for off_idx, pat, sap in parts:
                E.reg_load(r_off, p.sb["xoffs_sb"][0:1, off_idx:off_idx + 1])
                E.dma_start(out=bass.AP(shared_t.tensor, r_off, pat),
                            in_=sap).then_inc(wsem.h, 16)
                wsem.reserve(16)
            E.wait_ge(wsem.h, wsem.n)
            E.reg_load(r_off, p.sb["xoffs_sb"][0:1, 2:3])
            E.dma_start(out=bass.AP(flag_t.tensor, r_off, [[16, 1], [1, 16]]),
                        in_=p.sb["flagv_sb"][flagval_row:flagval_row + 1, 0:16]
                        ).then_inc(wsem.h, 16)
            wsem.reserve(16)
            E.wait_ge(wsem.h, wsem.n)
    p.do("gpsimd", body)


def emit_xchg_poll_read2(p, S, *, shared_t, flag_t, flagval_row, psem, rsem,
                         reads, roleint_col, maxpoll=2000000):
    def body(E):
        with (
            E.register(_rn("rf")) as rf,
            E.register(_rn("rc")) as rc,
            E.register(_rn("rcont")) as rcont,
            E.register(_rn("rtmp")) as rtmp,
            E.register(_rn("r_off")) as r_off,
        ):
            E.reg_load(rcont, p.sb["roleint_sb"][0:1, roleint_col:roleint_col + 1])
            E.reg_load(r_off, p.sb["xoffs_sb"][0:1, 3:4])
            E.reg_mov(rc, 0)
            with E.While(rcont):
                E.dma_start(out=p.sb["flagsc"][0:1, 0:16],
                            in_=bass.AP(flag_t.tensor, r_off, [[16, 1], [1, 16]])
                            ).then_inc(psem.h, 16)
                E.reg_add(rc, rc, 16)
                E.wait_ge(psem.h, rc)
                E.reg_load(rf, p.sb["flagsc"][0:1, 0:1])
                E.reg_load(rtmp, p.sb["flagv_sb"][flagval_row:flagval_row + 1, 0:1])
                E.reg_alu(rcont, rf, rtmp, ALU.not_equal)
                E.reg_alu(rtmp, rc, 16 * maxpoll, ALU.is_lt)
                E.reg_alu(rcont, rcont, rtmp, ALU.bitwise_and)
            for off, sap, pat in reads:
                E.dma_start(out=sap, in_=bass.AP(shared_t.tensor, off, pat)
                            ).then_inc(rsem.h, 16)
                rsem.reserve(16)
            E.wait_ge(rsem.h, rsem.n)
    p.do("gpsimd", body)



def emit_attn1(p, S, *, wqb, wkb, wvb, wob, x, xn, qt, xkv, xflag):
    sb = p.sb
    v, a, t = S["v"], S["a"], S["t"]
    psf = S.setdefault("psfree", {})
    wfree = S.setdefault("wfree", {})
    lg1, lu1 = S["lg1"], S["lu1"]
    lkv = (S["lg0"], S["lu0"])

    def pwait(pn):
        if pn in psf:
            p.w("tensor", psf[pn][0], psf[pn][1])

    def wwait(*keys):
        for key in keys:
            if key in wfree:
                p.w("sync", wfree[key][0], wfree[key][1])


    p.do("vector", lambda E: E.memset(sb["negc"][:, 0:1], -11.4), v)
    negc_ready = v.n

    def proj_group(wsrc, colbase, pn, th, rdy_sem, rdy_cnt):
        pwait(pn)
        p.w("tensor", rdy_sem, rdy_cnt)
        for kc in range(16):
            p.do("tensor", lambda E, kc=kc, colbase=colbase, pn=pn, th=th, wsrc=wsrc:
                 E.matmul(
                     p.ps[pn][:, 0:512],
                     sb[wsrc][:, kc * 512 + colbase:kc * 512 + colbase + 128],
                     sb[xn][:, kc * 1024 + th * 512:kc * 1024 + (th + 1) * 512],
                     start=(kc == 0), stop=(kc == 15)), t if kc == 15 else None)
        return t.n

    def qknorm_evac(pn, dst_ap_fn, fold_sqrt_d):
        p.do("vector", lambda E: E.tensor_copy(dst_ap_fn(), p.ps[pn][0:128, 0:512]), v)
        raw_done = v.n
        psf[pn] = (v, raw_done)
        p.w("scalar", v, raw_done)
        p.do("scalar", lambda E: E.activation(sb["tb512"][:, 0:512], dst_ap_fn(),
                                              AF.Square), a)
        sq_done = a.n
        pwait("p2")
        p.w("tensor", a, sq_done)
        p.do("tensor", lambda E: E.matmul(p.ps["p2"][0:1, 0:512],
                                          sb["ones_col"][:, 0:1],
                                          sb["tb512"][:, 0:512],
                                          start=True, stop=True), t)
        ssd = t.n
        p.w("vector", t, ssd)
        if fold_sqrt_d:
            p.do("vector", lambda E: E.tensor_single_scalar(
                sb["rows"][0:1, 0:512], p.ps["p2"][0:1, 0:512], 1.28e-4, ALU.add), v)
        else:
            p.do("vector", lambda E: E.tensor_scalar(
                sb["rows"][0:1, 0:512], p.ps["p2"][0:1, 0:512], 1.0 / 128.0, 1e-6,
                ALU.mult, ALU.add), v)
        psf["p2"] = (v, v.n)
        p.w("scalar", v)
        p.do("scalar", lambda E: E.activation(sb["t512a"][0:1, 0:512],
                                              sb["rows"][0:1, 0:512], AF.Sqrt), a)
        p.w("vector", a)
        p.do("vector", lambda E: E.reciprocal(sb["rows"][0:1, 0:512],
                                              sb["t512a"][0:1, 0:512]), v)
        p.do("vector", lambda E: E.tensor_copy(sb["rows_b"][0:1, 0:512],
                                               sb["rows"][0:1, 0:512]), v)
        rb = v.n
        pwait("p3")
        p.w("tensor", v, rb)
        p.do("tensor", lambda E: E.matmul(p.ps["p3"][:, 0:512],
                                          sb["rows_b"][0:1, 1024:1152],
                                          sb["rows_b"][0:1, 0:512],
                                          start=True, stop=True), t)
        bcd = t.n
        p.w("vector", t, bcd)
        p.do("vector", lambda E: E.tensor_mul(dst_ap_fn(), dst_ap_fn(),
                                              p.ps["p3"][:, 0:512]), v)
        psf["p3"] = (v, v.n)

    wwait("wg1", "wg0")
    p.do("sync", lambda E: E.dma_start(
        out=sb["wg1"][:, 0:8192],
        in_=bass.AP(wkb.tensor, 0, [[512, 128], [128 * 512, 16], [1, 512]])), lg1, 16)
    ldk = lg1.n
    p.w("tensor", v, S["xn_ready"])
    for kvh in range(4):
        for th in range(2):
            pn = f"p{(kvh * 2 + th) % 2}"
            mmd = proj_group("wg1", kvh * 128, pn, th, lg1, ldk)
            p.w("vector", t, mmd)
            qknorm_evac(pn, lambda kvh=kvh, th=th: sb["wg0"][:, kvh * 2048 + th * 512:
                                                            kvh * 2048 + (th + 1) * 512],
                        fold_sqrt_d=False)
    k_done = v.n
    wwait("wu1", "wu0")
    p.do("sync", lambda E: E.dma_start(
        out=sb["wu1"][:, 0:8192],
        in_=bass.AP(wvb.tensor, 0, [[512, 128], [128 * 512, 16], [1, 512]])), lu1, 16)
    ldv = lu1.n
    for tt in range(8):
        pn = f"p{4 + tt % 2}"
        pwait(pn)
        p.w("tensor", lu1, ldv)
        for kc in range(16):
            p.do("tensor", lambda E, kc=kc, tt=tt, pn=pn: E.matmul(
                p.ps[pn][:, 0:512],
                sb[xn][:, kc * 1024 + tt * 128:kc * 1024 + (tt + 1) * 128],
                sb["wu1"][:, kc * 512:(kc + 1) * 512],
                start=(kc == 0), stop=(kc == 15)), t if kc == 15 else None)
        mmd = t.n
        p.w("vector", t, mmd)
        p.do("vector", lambda E, tt=tt, pn=pn: E.tensor_copy(
            sb["wu0"][:, tt * 512:(tt + 1) * 512], p.ps[pn][0:128, 0:512]), v)
        psf[pn] = (v, v.n)
    v_done = v.n

    qbufs = ("wg1", "wu1")
    qsems = (lg1, lu1)
    kv_consumed = t.n
    qwin_used = {0: kv_consumed, 1: kv_consumed}
    for w in range(4):
        q = w % 2
        p.w("sync", t, qwin_used[q])
        p.do("sync", lambda E, w=w, q=q: E.dma_start(
            out=sb[qbufs[q]][:, 0:8192],
            in_=bass.AP(wqb.tensor, w * 512,
                        [[2048, 128], [128 * 2048, 16], [1, 512]])), qsems[q], 16)
        ldq = qsems[q].n
        for hh in range(4):
            h = w * 4 + hh
            for th in range(2):
                pn = f"p{(hh * 2 + th) % 2}"
                mmd = proj_group(qbufs[q], hh * 128, pn, th, qsems[q], ldq)
                p.w("vector", t, mmd)
                qknorm_evac(pn, lambda h=h, th=th: sb[qt][:, h * 1024 + th * 512:
                                                        h * 1024 + (th + 1) * 512],
                            fold_sqrt_d=True)
        qwin_used[q] = t.n
    q_done = v.n

    p.w("gpsimd", v, max(k_done, v_done))
    emit_xchg_write2(p, S, shared_t=xkv, flag_t=xflag, flagval_row=1, wsem=S["xws"],
                     parts=[
                         (4, [[8192, 128], [2048, 4], [1, 1024]],
                          bass.AP(sb["wg0"], 0, [[8192, 128], [2048, 4], [1, 1024]])),
                         (5, [[8192, 128], [512, 8], [1, 512]],
                          bass.AP(sb["wu0"], 0, [[8192, 128], [512, 8], [1, 512]])),
                     ])
    emit_xchg_poll_read2(p, S, shared_t=xkv, flag_t=xflag, flagval_row=1,
                         psem=S["poll1"], rsem=S["xls2"], roleint_col=1,
                         reads=[(0, sb["wg0"][:, 0:8192], [[8192, 128], [1, 8192]]),
                                (128 * 8192, sb["wu0"][:, 0:8192],
                                 [[8192, 128], [1, 8192]])])
    xch2 = S["xch2"]
    xch2.reserve(1)
    p.on("gpsimd", lambda E, s=xch2.h: E.nop().then_inc(s, 1))
    p.w("tensor", xch2)
    p.w("vector", xch2)

    p.w("tensor", v, q_done)
    for h in range(16):
        kvh = h // 4
        for m in range(16):
            for th in range(2):
                pn = f"p{(m * 2 + th) % 2}"
                pwait(pn)
                p.do("tensor", lambda E, m=m, th=th, pn=pn, kvh=kvh, h=h: E.matmul(
                    p.ps[pn][:, 0:512],
                    sb["wg0"][:, kvh * 2048 + m * 128:kvh * 2048 + (m + 1) * 128],
                    sb[qt][:, h * 1024 + th * 512:h * 1024 + (th + 1) * 512],
                    start=True, stop=True), t)
                mmd = t.n
                p.w("scalar", t, mmd)
                p.w("scalar", v, negc_ready)
                p.do("scalar", lambda E, m=m, th=th, pn=pn: E.activation(
                    sb[xn][:, m * 1024 + th * 512:m * 1024 + (th + 1) * 512],
                    p.ps[pn][0:128, 0:512], AF.Exp, bias=sb["negc"][:, 0:1]), a)
                psf[pn] = (a, a.n)
        probs_done = a.n
        for th in range(2):
            pwait("p2")
            p.w("tensor", a, probs_done)
            for m in range(16):
                p.do("tensor", lambda E, m=m, th=th: E.matmul(
                    p.ps["p2"][0:1, 0:512], sb["ones_col"][:, 0:1],
                    sb[xn][:, m * 1024 + th * 512:m * 1024 + (th + 1) * 512],
                    start=(m == 0), stop=(m == 15)), t if m == 15 else None)
            dsum = t.n
            p.w("vector", t, dsum)
            p.do("vector", lambda E: E.reciprocal(sb["rows"][0:1, 0:512],
                                                  p.ps["p2"][0:1, 0:512]), v)
            psf["p2"] = (v, v.n)
            p.do("vector", lambda E: E.tensor_copy(sb["rows_b"][0:1, 0:512],
                                                   sb["rows"][0:1, 0:512]), v)
            rb = v.n
            pwait("p3")
            p.w("tensor", v, rb)
            p.do("tensor", lambda E: E.matmul(p.ps["p3"][:, 0:512],
                                              sb["rows_b"][0:1, 1024:1152],
                                              sb["rows_b"][0:1, 0:512],
                                              start=True, stop=True), t)
            bcd = t.n
            p.w("vector", t, bcd)
            p.do("vector", lambda E: E.tensor_copy(sb["t512a"][:, 0:512],
                                                   p.ps["p3"][:, 0:512]), v)
            dinv_ready = v.n
            psf["p3"] = (v, dinv_ready)
            pn = f"p{4 + th}"
            pwait(pn)
            for ch in range(16):
                p.do("tensor", lambda E, ch=ch, th=th, pn=pn, kvh=kvh: E.matmul(
                    p.ps[pn][:, 0:512],
                    sb["wu0"][:, ch * 512 + kvh * 128:ch * 512 + (kvh + 1) * 128],
                    sb[xn][:, ch * 1024 + th * 512:ch * 1024 + (th + 1) * 512],
                    start=(ch == 0), stop=(ch == 15)), t if ch == 15 else None)
            avd = t.n
            p.w("vector", t, avd)
            p.w("vector", v, dinv_ready)
            p.do("vector", lambda E, h=h, th=th, pn=pn: E.tensor_mul(
                sb[qt][:, h * 1024 + th * 512:h * 1024 + (th + 1) * 512],
                sb["t512a"][:, 0:512], p.ps[pn][0:128, 0:512]), v)
            psf[pn] = (v, v.n)
            S["t512a_free"] = v.n
    attn_done = v.n

    for w in range(4):
        q = w % 2
        p.w("sync", t, qwin_used[q])
        p.do("sync", lambda E, w=w, q=q: E.dma_start(
            out=sb[qbufs[q]][:, 0:8192],
            in_=bass.AP(wob.tensor, w * 512,
                        [[2048, 128], [128 * 2048, 16], [1, 512]])), qsems[q], 16)
        ldo = qsems[q].n
        for mi in range(4):
            for th in range(2):
                pn = f"p{2 + (mi * 2 + th) % 2}"
                pwait(pn)
                p.w("tensor", qsems[q], ldo)
                p.w("tensor", v, attn_done)
                for kc in range(16):
                    p.do("tensor", lambda E, kc=kc, mi=mi, th=th, pn=pn, q=q: E.matmul(
                        p.ps[pn][:, 0:512],
                        sb[qbufs[q]][:, kc * 512 + mi * 128:kc * 512 + (mi + 1) * 128],
                        sb[qt][:, kc * 1024 + th * 512:kc * 1024 + (th + 1) * 512],
                        start=(kc == 0), stop=(kc == 15)), t if kc == 15 else None)
                mmd = t.n
                dch = w * 4 + mi
                p.w("vector", t, mmd)
                p.do("vector", lambda E, dch=dch, th=th, pn=pn: E.tensor_add(
                    sb[x][:, dch * 1024 + th * 512:dch * 1024 + (th + 1) * 512],
                    sb[x][:, dch * 1024 + th * 512:dch * 1024 + (th + 1) * 512],
                    p.ps[pn][0:128, 0:512]), v)
                psf[pn] = (v, v.n)
        qwin_used[q] = t.n
    for key in ("wg0", "wg1", "wu0", "wu1"):
        wfree[key] = (t, t.n)
    S["x_ready"] = v.n


def build_attn1():
    nc = _new_nc()
    xin = nc.dram_tensor("xin", [2048, 1024], F32, kind="ExternalInput").ap()
    wqb = nc.dram_tensor("wqb", [2048, 2048], BF16, kind="ExternalInput").ap()
    wkb = nc.dram_tensor("wkb", [2048, 512], BF16, kind="ExternalInput").ap()
    wvb = nc.dram_tensor("wvb", [2048, 512], BF16, kind="ExternalInput").ap()
    wob = nc.dram_tensor("wob", [2048, 2048], BF16, kind="ExternalInput").ap()
    xoffs = nc.dram_tensor("xoffs", [1, 8], I32, kind="ExternalInput").ap()
    flagvals = nc.dram_tensor("flagvals", [4, 16], I32, kind="ExternalInput").ap()
    roleint = nc.dram_tensor("roleint", [1, 2], I32, kind="ExternalInput").ap()
    rolem = nc.dram_tensor("rolem", [128, 1], I32, kind="ExternalInput").ap()
    xo = nc.dram_tensor("xo", [2048, 1024], F32, kind="ExternalOutput").ap()
    xkv = nc.dram_tensor("xkv", [256, 8192], BF16, kind="Internal",
                         addr_space="Shared").ap()
    xflag = nc.dram_tensor("xflag", [2, 16], I32, kind="Internal",
                           addr_space="Shared").ap()
    with contextlib.ExitStack() as ctx:
        p = Prog(nc, ctx)
        alloc_main_bufs(p)
        alloc_xchg_bufs(p)
        alloc_attn1_bufs(p)
        S = {k: p.sem(k) for k in ("v", "a", "t", "lg0", "lg1", "lu0", "lu1",
                                   "lx", "sx", "poll1", "xls2", "xws", "xch2")}
        emit_consts(p, S)
        for nm, dst, ap_in in [("flagvals", "flagv_sb", flagvals),
                               ("roleint", "roleint_sb", roleint),
                               ("rolem", "rolem_sb", rolem),
                               ("xoffs", "xoffs_sb", xoffs)]:
            shp = ap_in.shape
            p.do("scalar", lambda E, ap_in=ap_in, dst=dst, shp=shp: E.dma_start(
                out=p.sb[dst][0:shp[0], 0:shp[1]], in_=ap_in), S["lx"], 16)
        p.do("scalar", lambda E: E.dma_start(
            out=p.sb["x"][:, :],
            in_=bass.AP(xin.tensor, 0, [[1024, 128], [128 * 1024, 16], [1, 1024]])),
            S["lx"], 16)
        p.w("vector", S["lx"])
        p.w("scalar", S["lx"])
        p.w("gpsimd", S["lx"])
        emit_ln(p, S, x="x", xn="bufA", scr="bufB")
        emit_attn1(p, S, wqb=wqb, wkb=wkb, wvb=wvb, wob=wob, x="x", xn="bufA",
                   qt="bufB", xkv=xkv, xflag=xflag)
        p.w("scalar", S["v"], S["x_ready"])
        p.do("scalar", lambda E: E.dma_start(
            out=bass.AP(xo.tensor, 0, [[1024, 128], [128 * 1024, 16], [1, 1024]]),
            in_=p.sb["x"][:, :]), S["sx"], 16)
        p.w("scalar", S["sx"])
        p.w("gpsimd", S["xws"])
        p.emit()
    return nc



def emit_decode_v2(p, S, bufs, *, n_bt, width, pk_dram, scT_dram, out_dram,
                   ind, progress_key=None):
    v, a, t, g = S["v"], S["a"], S["t"], S["g"]
    lds = (S["ld0"], S["ld1"])
    lsc = S["lsc"]
    sts = (S["st0"], S["st1"])
    psf = S.setdefault("psfree", {})
    arena = S.setdefault("arena", {})
    W = width

    def pwait(pn):
        if pn in psf:
            p.w("tensor", psf[pn][0], psf[pn][1])

    for bt in range(n_bt):
        q = bt % 2
        pkt = bufs[f"pk{q}"]
        plb = bufs[f"pl{q}"]
        scb = bufs[f"scr{q}"]
        scev = bufs[f"sce{q}"]
        if ("pk", q) in arena:
            p.w("sync", *arena[("pk", q)])
        p.do("sync", lambda E, bt=bt, pkt=pkt: E.dma_start(
            out=pkt[:, 0:W], in_=pk_dram[128 * bt:128 * (bt + 1), 0:W]), lds[q], 16)
        ld_pk = lds[q].n
        if ("sc", q) in arena:
            p.w("gpsimd", *arena[("sc", q)])
        p.do("gpsimd", lambda E, bt=bt, scb=scb: E.dma_start(
            out=scb[0:8, 0:W], in_=scT_dram[8 * bt:8 * (bt + 1), 0:W]), lsc, 16)
        ld_sc = lsc.n
        NW = (W + 511) // 512
        scpn = []
        for win in range(NW):
            w0, w1 = win * 512, min(win * 512 + 512, W)
            pn = ("p6", "p7")[win % 2]
            pwait(pn)
            p.w("tensor", lsc, ld_sc)
            p.do("tensor", lambda E, pn=pn, w0=w0, w1=w1, scb=scb, ind=ind: E.matmul(
                p.ps[pn][:, 0:w1 - w0], ind[0:8, :], scb[0:8, w0:w1],
                start=True, stop=True), t)
            scpn.append((pn, w0, w1, t.n))
        arena[("sc", q)] = (t, t.n)
        if ("sce", q) in arena:
            p.w("scalar", *arena[("sce", q)])
        for pn, w0, w1, mmc in scpn:
            p.w("scalar", t, mmc)
            p.do("scalar", lambda E, pn=pn, w0=w0, w1=w1, scev=scev: E.copy(
                scev[:, w0:w1], p.ps[pn][0:128, 0:w1 - w0]), a)
            psf[pn] = (a, a.n)
        sc_ready = a.n
        p.w("scalar", lds[q], ld_pk)
        t16 = bufs["t16"]
        p.do("scalar", lambda E, pkt=pkt: E.activation(
            t16[:, 0:W], pkt[:, 0:W], AF.Identity), a)
        cvt_done = a.n
        arena[("pk", q)] = (a, cvt_done)
        for pl, (mA, shA, mS, shS) in enumerate([(7, 6, 8, 12), (0x70, 2, 0x80, 8)]):
            dst = bufs["lo16"] if pl == 0 else bufs["hi16"]
            a16, s16 = bufs["a16"], bufs["s16"]
            p.w("vector", a, cvt_done)
            p.do("vector", lambda E, mA=mA, shA=shA, a16=a16: E.tensor_scalar(
                a16[:, 0:W], t16[:, 0:W], mA, shA,
                ALU.bitwise_and, ALU.logical_shift_left), v)
            m6_done = v.n
            p.do("vector", lambda E, mS=mS, shS=shS, s16=s16: E.tensor_scalar(
                s16[:, 0:W], t16[:, 0:W], mS, shS,
                ALU.bitwise_and, ALU.logical_shift_left), v)
            s15_done = v.n
            p.w("scalar", v, m6_done)
            p.do("scalar", lambda E, a16=a16: E.activation(
                a16[:, 0:W], a16[:, 0:W], AF.Identity,
                bias=p.sb["c16k"][:, 0:1]), a)
            p.do("scalar", lambda E, a16=a16, dst=dst: E.activation(
                dst[:, 0:W].bitcast(BF16), a16[:, 0:W].bitcast(BF16), AF.Relu,
                scale=-1.0, bias=1.0), a)
            r_done = a.n
            p.w("vector", a, r_done)
            p.do("vector", lambda E, a16=a16, dst=dst: E.scalar_tensor_tensor(
                dst[:, 0:W].bitcast(BF16), dst[:, 0:W].bitcast(BF16), -1.0,
                a16[:, 0:W].bitcast(BF16), ALU.mult, ALU.add), v)
            g_done = v.n
            p.do("vector", lambda E, dst=dst, s16=s16: E.tensor_tensor(
                dst[:, 0:W], dst[:, 0:W], s16[:, 0:W], ALU.bitwise_xor), v)
            sg_done = v.n
            if ("pl", q) in arena:
                p.w("vector", *arena[("pl", q)])
            p.w("vector", a, sc_ready)
            p.do("vector", lambda E, dst=dst, plb=plb, pl=pl, scev=scev: E.tensor_tensor(
                plb[:, pl * W:(pl + 1) * W], dst[:, 0:W].bitcast(BF16),
                scev[:, 0:W], ALU.mult), v)
        planes_done = v.n
        arena[("sce", q)] = (v, planes_done)
        p.w("sync", v, planes_done)
        for pl in range(2):
            kap = 2 * bt + pl
            p.do("sync", lambda E, kap=kap, pl=pl, plb=plb: E.dma_start(
                out=out_dram[128 * kap:128 * (kap + 1), 0:W],
                in_=plb[:, pl * W:(pl + 1) * W]), sts[q], 16)
        arena[("pl", q)] = (sts[q], sts[q].n)
    if progress_key is not None:
        S.setdefault("decoded", {})[progress_key] = (sts[0], sts[0].n, sts[1], sts[1].n)



def build_fused():
    nc = _new_nc()
    ins = {}
    for nm, shape, dt in [
        ("xin", [2048, 1024], F32),
        ("gpkT", [1024, 8192], U8), ("gscT", [64, 8192], F32),
        ("upkT", [1024, 8192], U8), ("uscT", [64, 8192], F32),
        ("dpkT", [4096, 2048], U8), ("dscT", [256, 2048], F32),
        ("winT", [1024, 2048], F32), ("woutT", [1024, 2048], F32),
        ("wq", [1024, 2048], F32), ("wk", [1024, 512], F32),
        ("wv", [1024, 512], F32), ("wo", [1024, 2048], F32),
        ("l1g", [1024, 8192], F32), ("l1u", [1024, 8192], F32),
        ("l1d", [4096, 2048], F32), ("clfT", [2048, 12], F32),
        ("escb", [1, 12], F32),
        ("decay_pt", [128, 16], F32), ("bias_pt", [128, 16], F32),
        ("ind16", [8, 128], BF16), ("ind16h", [8, 128], BF16),
        ("identf", [128, 128], F32),
        ("xoffs", [1, 8], I32), ("flagvals", [4, 16], I32),
        ("roleint", [1, 4], I32), ("rolem", [128, 1], I32),
    ]:
        ins[nm] = nc.dram_tensor(nm, shape, dt, kind="ExternalInput").ap()
    xo = nc.dram_tensor("xo", [2048, 1024], F32, kind="ExternalOutput").ap()
    scr = {}
    for nm, shape in [("sgate", [2048, 16384]), ("sup", [2048, 16384]),
                      ("sdown", [16384, 2048]), ("swin", [2048, 2048]),
                      ("swout", [2048, 2048]), ("swq", [2048, 2048]),
                      ("swk", [2048, 512]), ("swv", [2048, 512]),
                      ("swo", [2048, 2048]), ("sl1g", [2048, 8192]),
                      ("sl1u", [2048, 8192]), ("sl1d", [8192, 2048])]:
        scr[nm] = nc.dram_tensor(nm, shape, BF16, kind="Internal",
                                 addr_space="Shared").ap()
    xdflag = nc.dram_tensor("xdflag", [2, 16], I32, kind="Internal",
                            addr_space="Shared").ap()
    xcflag = nc.dram_tensor("xcflag", [2, 16], I32, kind="Internal",
                            addr_space="Shared").ap()
    xfin = nc.dram_tensor("xfin", [256, 16], F32, kind="Internal",
                          addr_space="Shared").ap()
    xkv = nc.dram_tensor("xkv", [256, 8192], BF16, kind="Internal",
                         addr_space="Shared").ap()
    xflag = nc.dram_tensor("xflag", [2, 16], I32, kind="Internal",
                           addr_space="Shared").ap()

    with contextlib.ExitStack() as ctx:
        p = Prog(nc, ctx)
        alloc_main_bufs(p)
        alloc_xchg_bufs(p)
        alloc_ffn1_bufs(p)
        alloc_attn1_bufs(p)
        p.sbuf("ind16_sb", [8, 128], BF16)
        p.sbuf("ind16h_sb", [8, 128], BF16)
        p.sbuf("c16k", [128, 1], F32)
        sb = p.sb
        wg1, wu1 = sb["wg1"], sb["wu1"]
        arena_bufs = {
            "pkb0": wg1[:, 0:1024].bitcast(U8),
            "pkb1": wg1[:, 1024:2048].bitcast(U8),
            "t16_0": wg1[:, 2048:2560].bitcast(U16),
            "t16_1": wg1[:, 2560:3072].bitcast(U16),
            "a16lo_0": wg1[:, 3072:3584].bitcast(U16),
            "a16lo_1": wg1[:, 3584:4096].bitcast(U16),
            "a16hi_0": wg1[:, 4096:4608].bitcast(U16),
            "a16hi_1": wg1[:, 4608:5120].bitcast(U16),
            "lo16_0": wg1[:, 5120:5632].bitcast(U16),
            "lo16_1": wg1[:, 5632:6144].bitcast(U16),
            "hi16_0": wg1[:, 6144:6656].bitcast(U16),
            "hi16_1": wg1[:, 6656:7168].bitcast(U16),
            "scr0": wg1[0:8, 7168:7680],
            "scr1": wg1[0:8, 7680:8192],
            "pla0": wu1[:, 0:4096],
            "pla1": wu1[:, 4096:8192],
        }
        p.sbuf("dflagsc", [2, 16], I32)
        S = {k: p.sem(k) for k in (
            "v", "a", "t", "g", "lg0", "lg1", "lu0", "lu1", "lx", "sx", "lmisc",
            "cvt", "ld0", "ld1", "lsc0", "lsc1", "st0", "st1", "poll0", "poll1",
            "xls", "xls2", "xws", "xch", "xch2", "lmiscg",
            "pd", "pdp0", "pdp1", "pdp2", "pdp3", "pc", "pcp")}
        v, a, t = S["v"], S["a"], S["t"]
        emit_consts(p, S)
        p.do("vector", lambda E: E.memset(sb["escb_sb"][0:1, 0:16], 0.0), v)

        for nm, dst in [("decay_pt", "decay_pt_sb"), ("bias_pt", "bias_pt_sb"),
                        ("flagvals", "flagv_sb"), ("roleint", "roleint_sb"),
                        ("rolem", "rolem_sb"), ("xoffs", "xoffs_sb"),
                        ("identf", "identf_sb"), ("ind16", "ind16_sb"),
                        ("ind16h", "ind16h_sb")]:
            ap_in = ins[nm]
            shp = ap_in.shape
            p.do("scalar", lambda E, ap_in=ap_in, dst=dst, shp=shp: E.dma_start(
                out=sb[dst][0:shp[0], 0:shp[1]], in_=ap_in), S["lmisc"], 16)
        p.do("gpsimd", lambda E: E.dma_start(
            out=sb["clfb_sb"][:, :],
            in_=bass.AP(ins["clfT"].tensor, 0, [[12, 128], [128 * 12, 16], [1, 12]])),
            S["lmiscg"], 16)
        p.w("gpsimd", v, 4)
        p.do("gpsimd", lambda E: E.dma_start(
            out=sb["escb_sb"][0:1, 0:12], in_=ins["escb"]), S["lmiscg"], 16)
        p.do("scalar", lambda E: E.dma_start(
            out=sb["x"][:, :],
            in_=bass.AP(ins["xin"].tensor, 0, [[1024, 128], [128 * 1024, 16], [1, 1024]])),
            S["lx"], 16)
        p.w("vector", S["lx"])
        p.w("scalar", S["lx"])
        p.w("tensor", S["lmisc"])
        p.w("gpsimd", S["lmiscg"])

        conv_jobs = []
        for nm, src, half in [("swin", "winT", 1024), ("swout", "woutT", 1024),
                              ("sl1g", "l1g", 1024), ("sl1u", "l1u", 1024),
                              ("sl1d", "l1d", 4096), ("swq", "wq", 1024),
                              ("swk", "wk", 1024), ("swv", "wv", 1024),
                              ("swo", "wo", 1024)]:
            nchunk = 2 if nm in ("sl1g", "sl1u", "sl1d") else 1
            step = half // nchunk
            for i in range(nchunk):
                conv_jobs.append((nm, src, i * step, step, half))
        cvt = S["cvt"]
        cvt_need = {}

        def emit_conv(n=3):
            for _ in range(n):
                if not conv_jobs:
                    return
                nm, src, r0, nr, half = conv_jobs.pop(0)
                cvt.reserve(16)

                def body(E, nm=nm, src=src, r0=r0, nr=nr, half=half):
                    for br in (0, 1):
                        with E.register(_rn("cvb")) as rc:
                            E.reg_load(rc, sb["roleint_sb"][0:1, 2 + br:3 + br])
                            with E.While(rc):
                                E.dma_start(
                                    out=scr[nm][br * half + r0:br * half + r0 + nr, :],
                                    in_=ins[src][r0:r0 + nr, :]).then_inc(cvt.h, 16)
                                E.reg_mov(rc, 0)
                p.do("gpsimd", body)
                cvt_need[nm] = cvt.n

        deps = {}
        all_bts = []
        for it in range(4):
            all_bts.extend(dec_bts_for_iter(ins, scr, sb, it))
        dec_pos = [0]
        dec_flags = [0]
        tick_ct = [0]

        def dec_emit(n_bts):
            i0 = dec_pos[0]
            i1 = min(i0 + n_bts, len(all_bts))
            if i1 > i0:
                emit_decode_v5(p, S, arena_bufs, all_bts[i0:i1], sb["ind16_sb"])
                dec_pos[0] = i1
            if i1 == len(all_bts):
                emit_decode_v5_flush(p, S, arena_bufs)
                it_done = 4
            else:
                it_done = max(0, (i1 - 1) // 24)
            while dec_flags[0] < it_done:
                it = dec_flags[0]
                emit_dec_flag_poll(p, S, it, xdflag)
                for ssup in (2 * it, 2 * it + 1):
                    for key in ("g", "u", "d"):
                        deps[(key, ssup)] = [(S["pd"], 16 * (it + 1))]
                dec_flags[0] = it + 1
            emit_conv(1)

        emit_ln(p, S, x="x", xn="bufA", scr="bufB")
        dec_emit(36)

        def pre_super(sup):
            pass

        def tick():
            tick_ct[0] += 1
            dec_emit(2 if tick_ct[0] % 2 == 0 else 1)

        emit_ffn0(p, S, gdec=scr["sgate"], udec=scr["sup"], ddec=scr["sdown"],
                  x="x", xn="bufA", inter="bufB", single_buf=True, deps=deps,
                  pre_super=pre_super, tick=tick)
        dec_emit(len(all_bts))
        emit_conv(99)
        pc, pcp = S["pc"], S["pcp"]
        cvt_full = cvt.n
        cvt.reserve(16)
        cvt_flagged = cvt.n
        pc.reserve(16)

        def conv_flag_body(E, cvt_full=cvt_full, cvt_flagged=cvt_flagged):
            E.wait_ge(cvt.h, cvt_full)
            for br in (0, 1):
                with E.register(_rn("cfw")) as rc:
                    E.reg_load(rc, sb["roleint_sb"][0:1, 2 + br:3 + br])
                    with E.While(rc):
                        E.dma_start(out=bass.AP(xcflag.tensor, br * 16,
                                                [[16, 1], [1, 16]]),
                                    in_=sb["flagv_sb"][2:3, 0:16]
                                    ).then_inc(cvt.h, 16)
                        E.reg_mov(rc, 0)
            E.wait_ge(cvt.h, cvt_flagged)
            with (
                E.register(_rn("cf0")) as rf0,
                E.register(_rn("cf1")) as rf1,
                E.register(_rn("cc")) as rc2,
                E.register(_rn("ck")) as rcont,
                E.register(_rn("ct")) as rtmp,
            ):
                E.reg_mov(rc2, 0)
                E.reg_mov(rcont, 1)
                with E.While(rcont):
                    E.dma_start(out=sb["dflagsc"][0:2, 0:16],
                                in_=bass.AP(xcflag.tensor, 0, [[16, 2], [1, 16]])
                                ).then_inc(pcp.h, 16)
                    E.reg_add(rc2, rc2, 16)
                    E.wait_ge(pcp.h, rc2)
                    E.reg_load(rtmp, sb["flagv_sb"][2:3, 0:1])
                    E.reg_load(rf0, sb["dflagsc"][0:1, 2:3])
                    E.reg_load(rf1, sb["dflagsc"][1:2, 2:3])
                    E.reg_alu(rf0, rf0, rtmp, ALU.not_equal)
                    E.reg_alu(rf1, rf1, rtmp, ALU.not_equal)
                    E.reg_alu(rcont, rf0, rf1, ALU.bitwise_or)
            E.dma_start(out=sb["dflagsc"][0:2, 0:16],
                        in_=bass.AP(xcflag.tensor, 0, [[16, 2], [1, 16]])
                        ).then_inc(pc.h, 16)
        p.do("gpsimd", conv_flag_body)
        dec_end_v = (v, v.n)
        S.setdefault("wfree", {})["wg1"] = dec_end_v
        S["wfree"]["wu1"] = (S["st0"], S["st0"].n)
        p.w("sync", S["st0"])
        p.w("sync", S["st1"])

        emit_ln(p, S, x="x", xn="bufA", scr="bufB",
                xn_guard=[(t, t.n)], scr_guard=[(t, t.n)])
        p.w("sync", S["pc"], 16)
        emit_attn0(p, S, winb=scr["swin"], woutb=scr["swout"], x="x", xn="bufA",
                   prj="bufB", xfin=xfin, xflag=xflag)

        emit_ln(p, S, x="x", xn="bufA", scr="bufB",
                xn_guard=[(t, t.n)], scr_guard=[(t, t.n), (v, v.n)])
        emit_ffn0(p, S, gdec=scr["sgate"], udec=scr["sup"], ddec=scr["sdown"],
                  x="x", xn="bufA", inter="bufB")

        emit_ln(p, S, x="x", xn="bufA", scr="bufB",
                xn_guard=[(t, t.n)], scr_guard=[(t, t.n)])
        p.w("sync", S["pc"], 16)
        emit_ffn1(p, S, l1gb=scr["sl1g"], l1ub=scr["sl1u"], l1db=scr["sl1d"],
                  x="x", xn="bufA", inter="bufB")

        emit_ln(p, S, x="x", xn="bufA", scr="bufB",
                xn_guard=[(t, t.n), (v, v.n)], scr_guard=[(t, t.n)])
        p.w("sync", S["pc"], 16)
        emit_attn1(p, S, wqb=scr["swq"], wkb=scr["swk"], wvb=scr["swv"],
                   wob=scr["swo"], x="x", xn="bufA", qt="bufB", xkv=xkv,
                   xflag=xflag)

        emit_ln(p, S, x="x", xn="bufA", scr="bufB",
                xn_guard=[(t, t.n)], scr_guard=[(t, t.n)])
        emit_ffn1(p, S, l1gb=scr["sl1g"], l1ub=scr["sl1u"], l1db=scr["sl1d"],
                  x="x", xn="bufA", inter="bufB")

        p.w("scalar", v, S["x_ready"])
        p.do("scalar", lambda E: E.dma_start(
            out=bass.AP(xo.tensor, 0, [[1024, 128], [128 * 1024, 16], [1, 1024]]),
            in_=sb["x"][:, :]), S["sx"], 16)
        p.w("scalar", S["sx"])
        p.w("gpsimd", S["xws"])
        p.emit()
    return nc


def host_fused_inputs(inputs, g, nonce):
    maps = []
    xs = shard_x(inputs)
    cs = np.concatenate
    halves = {}
    for role in (0, 1):
        mys = [2 * i + role for i in range(4)]
        halves[role] = {
            "gpkT": cs([g["gate_pkT"][:, s * 2048:(s + 1) * 2048] for s in mys], 1),
            "gscT": cs([g["gate_scT"][:, s * 2048:(s + 1) * 2048] for s in mys], 1),
            "upkT": cs([g["up_pkT"][:, s * 2048:(s + 1) * 2048] for s in mys], 1),
            "uscT": cs([g["up_scT"][:, s * 2048:(s + 1) * 2048] for s in mys], 1),
            "dpkT": cs([g["down_pkT"][1024 * s:1024 * (s + 1), :] for s in mys], 0),
            "dscT": cs([g["down_scT"][64 * s:64 * (s + 1), :] for s in mys], 0),
            "winT": g["winT"][role * 1024:(role + 1) * 1024],
            "woutT": g["woutT"][role * 1024:(role + 1) * 1024],
            "wq": g["wq"][role * 1024:(role + 1) * 1024],
            "wk": g["wk"][role * 1024:(role + 1) * 1024],
            "wv": g["wv"][role * 1024:(role + 1) * 1024],
            "wo": g["wo"][role * 1024:(role + 1) * 1024],
            "l1g": g["l1gT"][role * 1024:(role + 1) * 1024],
            "l1u": g["l1uT"][role * 1024:(role + 1) * 1024],
            "l1d": g["l1dT"][role * 4096:(role + 1) * 4096],
        }
    bias_pt = np.ascontiguousarray(
        inputs["L0_attn_bias"][PI2048].reshape(16, 128).astype(np.float32).T)
    escb = np.ascontiguousarray(
        inputs["L1_ffn_e_score_correction_bias"].reshape(1, 12).astype(np.float32))
    identf = np.eye(128, dtype=np.float32)
    for c in range(8):
        h = halves[c % 2]
        m = {
            "xin": xs[c],
            "gpkT": h["gpkT"], "gscT": h["gscT"],
            "upkT": h["upkT"], "uscT": h["uscT"],
            "dpkT": h["dpkT"], "dscT": h["dscT"],
            "winT": h["winT"], "woutT": h["woutT"],
            "wq": h["wq"], "wk": h["wk"], "wv": h["wv"], "wo": h["wo"],
            "l1g": h["l1g"], "l1u": h["l1u"], "l1d": h["l1d"],
            "clfT": g["clfT"], "escb": escb,
            "decay_pt": g["decay_pt"], "bias_pt": bias_pt,
            "ind16": g["ind16"], "ind16h": g["ind16h"], "identf": identf,
        }
        m.update(host_xchg_inputs(c, nonce))
        maps.append(m)
    return maps



def emit_decode_v3(p, S, bufs, tiles, ind_default):
    v, a, t = S["v"], S["a"], S["t"]
    lds = (S["ld0"], S["ld1"])
    lscs = (S["lsc0"], S["lsc1"])
    sts = (S["st0"], S["st1"])
    psf = S.setdefault("psfree", {})
    st8 = S.setdefault("dv3", {"cvt": {}, "v1": {}, "v2": {}, "a1": {}, "scev": {},
                               "mm": {}, "store": {}, "ld": {}, "lsc": {}})
    W = 512
    n = len(tiles)
    base = st8.setdefault("n", 0)

    def emit_front(gi):
        i = gi - base
        pi = gi % 2
        ti = tiles[i]
        pn = ("p6", "p7")[pi]
        if gi - 2 in st8["cvt"]:
            p.w("sync", a, st8["cvt"][gi - 2])
        p.do("sync", lambda E, i=i, pi=pi: E.dma_start(
            out=bufs[f"pk{pi}"], in_=tiles[i]["pk"]), lds[pi], 16)
        st8["ld"][gi] = lds[pi].n
        if gi - 2 in st8["mm"]:
            p.w("gpsimd", t, st8["mm"][gi - 2])
        p.do("gpsimd", lambda E, i=i, pi=pi: E.dma_start(
            out=bufs[f"scr{pi}"], in_=tiles[i]["sc"]), lscs[pi], 16)
        st8["lsc"][gi] = lscs[pi].n
        if pn in psf:
            p.w("tensor", psf[pn][0], psf[pn][1])
        p.w("tensor", lscs[pi], st8["lsc"][gi])
        ind = ti.get("ind", ind_default)
        p.do("tensor", lambda E, pi=pi, pn=pn, ind=ind: E.matmul(
            p.ps[pn][:, 0:512], ind[0:8, :], bufs[f"scr{pi}"][0:8, 0:512],
            start=True, stop=True), t)
        st8["mm"][gi] = t.n
        p.w("scalar", lds[pi], st8["ld"][gi])
        if gi - 2 in st8["v2"]:
            p.w("scalar", v, st8["v2"][gi - 2])
        p.do("scalar", lambda E, pi=pi: E.activation(
            bufs[f"t16_{pi}"], bufs[f"pk{pi}"], AF.Identity), a)
        st8["cvt"][gi] = a.n
        p.w("scalar", t, st8["mm"][gi])
        p.do("scalar", lambda E, pi=pi, pn=pn: E.copy(
            bufs[f"sce{pi}"], p.ps[pn][0:128, 0:512]), a)
        st8["scev"][gi] = a.n
        psf[pn] = (a, a.n)
        p.w("vector", a, st8["cvt"][gi])
        p.do("vector", lambda E, pi=pi: E.tensor_scalar(
            bufs[f"a16lo_{pi}"], bufs[f"t16_{pi}"], 7, 6,
            ALU.bitwise_and, ALU.logical_shift_left), v)
        p.do("vector", lambda E, pi=pi: E.tensor_scalar(
            bufs[f"a16hi_{pi}"], bufs[f"t16_{pi}"], 0x70, 2,
            ALU.bitwise_and, ALU.logical_shift_left), v)
        st8["v1"][gi] = v.n
        p.w("scalar", v, st8["v1"][gi])
        if gi - 2 in st8["v2"]:
            p.w("scalar", v, st8["v2"][gi - 2])
        for h in ("lo", "hi"):
            p.do("scalar", lambda E, pi=pi, h=h: E.activation(
                bufs[f"a16{h}_{pi}"], bufs[f"a16{h}_{pi}"], AF.Identity,
                bias=p.sb["c16k"][:, 0:1]), a)
            p.do("scalar", lambda E, pi=pi, h=h: E.activation(
                bufs[f"{h}16_{pi}"].bitcast(BF16), bufs[f"a16{h}_{pi}"].bitcast(BF16),
                AF.Relu, scale=-1.0, bias=1.0), a)
        st8["a1"][gi] = a.n

    def emit_back(gi):
        i = gi - base
        pi = gi % 2
        p.w("vector", a, st8["a1"][gi])
        p.w("vector", a, st8["scev"][gi])
        if gi - 2 in st8["store"]:
            p.w("vector", sts[pi], st8["store"][gi - 2])
        for h, (mS, shS) in (("lo", (8, 12)), ("hi", (0x80, 8))):
            p.do("vector", lambda E, pi=pi, h=h, mS=mS, shS=shS: E.tensor_scalar(
                bufs["s16"], bufs[f"t16_{pi}"], mS, shS,
                ALU.bitwise_and, ALU.logical_shift_left), v)
            p.do("vector", lambda E, pi=pi, h=h: E.scalar_tensor_tensor(
                bufs[f"{h}16_{pi}"].bitcast(BF16), bufs[f"{h}16_{pi}"].bitcast(BF16),
                -1.0, bufs[f"a16{h}_{pi}"].bitcast(BF16), ALU.mult, ALU.add), v)
            p.do("vector", lambda E, pi=pi, h=h: E.tensor_tensor(
                bufs[f"{h}16_{pi}"], bufs[f"{h}16_{pi}"], bufs["s16"],
                ALU.bitwise_xor), v)
            p.do("vector", lambda E, pi=pi, h=h: E.tensor_tensor(
                bufs[f"pl{pi}"][:, (0 if h == "lo" else 512):(512 if h == "lo" else 1024)],
                bufs[f"{h}16_{pi}"].bitcast(BF16), bufs[f"sce{pi}"], ALU.mult), v)
        st8["v2"][gi] = v.n
        p.w("sync", v, st8["v2"][gi])
        for h, off in (("lo", 0), ("hi", 512)):
            p.do("sync", lambda E, i=i, pi=pi, h=h, off=off: E.dma_start(
                out=tiles[i][f"out_{h}"],
                in_=bufs[f"pl{pi}"][:, off:off + 512]), sts[pi], 16)
        st8["store"][gi] = sts[pi].n

    if st8.pop("pending", None) is not None:
        pass
    for gi in range(base, base + n):
        emit_front(gi)
        if gi - 1 >= 0 and gi - 1 >= base:
            emit_back(gi - 1)
        elif gi - 1 == base - 1 and base > 0:
            pass
    emit_back(base + n - 1)
    st8["n"] = base + n


def dec_tiles_for_super(ins, scr, sb, sup):
    tiles = []
    for pk, sct, out, ind in (("gpkT", "gscT", "sgate", "ind16_sb"),
                              ("upkT", "uscT", "sup", "ind16_sb")):
        for bt in range(8):
            for w in range(4):
                c0 = sup * 2048 + w * 512
                tiles.append({
                    "pk": ins[pk][128 * bt:128 * (bt + 1), c0:c0 + 512],
                    "sc": ins[sct][8 * bt:8 * (bt + 1), c0:c0 + 512],
                    "out_lo": scr[out][128 * (2 * bt):128 * (2 * bt + 1), c0:c0 + 512],
                    "out_hi": scr[out][128 * (2 * bt + 1):128 * (2 * bt + 2), c0:c0 + 512],
                    "ind": sb[ind],
                })
    for bt in range(8):
        gbt = 8 * sup + bt
        for w in range(4):
            d0 = w * 512
            tiles.append({
                "pk": ins["dpkT"][128 * gbt:128 * (gbt + 1), d0:d0 + 512],
                "sc": ins["dscT"][8 * gbt:8 * (gbt + 1), d0:d0 + 512],
                "out_lo": scr["sdown"][128 * (2 * gbt):128 * (2 * gbt + 1), d0:d0 + 512],
                "out_hi": scr["sdown"][128 * (2 * gbt + 1):128 * (2 * gbt + 2), d0:d0 + 512],
                "ind": sb["ind16h_sb"],
            })
    return tiles



def emit_decode_v4(p, S, bufs, bts, ind_default):
    v, a, t = S["v"], S["a"], S["t"]
    lds = (S["ld0"], S["ld1"])
    lscs = (S["lsc0"], S["lsc1"])
    sts = (S["st0"], S["st1"])
    psf = S.setdefault("psfree", {})
    st8 = S.setdefault("dv4", {"cvt": {}, "v1": {}, "v2": {}, "a1": {}, "scev": {},
                              "mm": {}, "store": {}, "pk": {}, "sc": {},
                              "cvtlast": {}, "mmlast": {}, "nb": 0})

    def pwait(pn):
        if pn in psf:
            p.w("tensor", psf[pn][0], psf[pn][1])

    nb0 = st8["nb"]
    subs = []
    for bi, bt in enumerate(bts):
        for w in range(4):
            subs.append((nb0 + bi, bt, w))

    def front(B, bt, w):
        bp = B % 2
        ind = bt.get("ind", ind_default)
        gs = 4 * B + w
        sp = gs % 2
        gp = gs // 2
        pp = gp % 2
        pn = ("p6", "p7")[sp]
        if w == 0:
            if B - 2 in st8["cvtlast"]:
                p.w("sync", a, st8["cvtlast"][B - 2])
            p.do("sync", lambda E, bt=bt, bp=bp: E.dma_start(
                out=bufs[f"pkb{bp}"], in_=bt["pk"]), lds[bp], 16)
            st8["pk"][B] = lds[bp].n
        if gs - 2 in st8["mm"]:
            p.w("gpsimd", t, st8["mm"][gs - 2])
        p.do("gpsimd", lambda E, bt=bt, w=w, sp=sp: E.dma_start(
            out=bufs[f"scr{sp}"], in_=bt["sc"][0:8, w * 512:(w + 1) * 512]),
            lscs[sp], 16)
        st8["sc"][gs] = lscs[sp].n
        pwait(pn)
        p.w("tensor", lscs[sp], st8["sc"][gs])
        p.do("tensor", lambda E, pn=pn, sp=sp, ind=ind: E.matmul(
            p.ps[pn][:, 0:512], ind[0:8, :],
            bufs[f"scr{sp}"][0:8, 0:512],
            start=True, stop=True), t)
        st8["mm"][gs] = t.n
        p.w("scalar", lds[bp], st8["pk"][B])
        if gs - 2 in st8["v2"]:
            p.w("scalar", v, st8["v2"][gs - 2])
        p.do("scalar", lambda E, sp=sp, bp=bp, w=w: E.activation(
            bufs[f"t16_{sp}"], bufs[f"pkb{bp}"][:, w * 512:(w + 1) * 512],
            AF.Identity), a)
        st8["cvt"][gs] = a.n
        st8["cvtlast"][B] = a.n
        p.w("vector", a, st8["cvt"][gs])
        p.do("vector", lambda E, sp=sp: E.tensor_scalar(
            bufs[f"a16lo_{sp}"], bufs[f"t16_{sp}"], 7, 6,
            ALU.bitwise_and, ALU.logical_shift_left), v)
        p.do("vector", lambda E, sp=sp: E.tensor_scalar(
            bufs[f"a16hi_{sp}"], bufs[f"t16_{sp}"], 0x70, 2,
            ALU.bitwise_and, ALU.logical_shift_left), v)
        st8["v1"][gs] = v.n
        p.w("scalar", v, st8["v1"][gs])
        if gs - 2 in st8["v2"]:
            p.w("scalar", v, st8["v2"][gs - 2])
        for h in ("lo", "hi"):
            p.do("scalar", lambda E, sp=sp, h=h: E.activation(
                bufs[f"a16{h}_{sp}"], bufs[f"a16{h}_{sp}"], AF.Identity,
                bias=p.sb["c16k"][:, 0:1]), a)
            p.do("scalar", lambda E, sp=sp, h=h: E.activation(
                bufs[f"{h}16_{sp}"].bitcast(BF16),
                bufs[f"a16{h}_{sp}"].bitcast(BF16),
                AF.Relu, scale=-1.0, bias=1.0), a)
        st8["a1"][gs] = a.n

    def back(B, bt, w):
        return _dec4_back(p, S, bufs, B, bt, w)

    pend = st8.pop("pend", None)
    for k, (B, bt, w) in enumerate(subs):
        front(B, bt, w)
        if pend is not None:
            back(*pend)
        pend = (B, bt, w)
    st8["pend"] = pend
    st8["nb"] = nb0 + len(bts)


def _dec4_back(p, S, bufs, B, bt, w):
    v, a, t = S["v"], S["a"], S["t"]
    sts = (S["st0"], S["st1"])
    st8 = S["dv4"]
    gs = 4 * B + w
    sp = gs % 2
    gp = gs // 2
    pp = gp % 2
    p.w("vector", a, st8["a1"][gs])
    p.w("vector", a, st8["scev"][gs])
    if gp - 2 in st8["store"]:
        p.w("vector", sts[pp], st8["store"][gp - 2])
    for h, hoff in (("lo", 0), ("hi", 1024)):
        p.do("vector", lambda E, sp=sp, h=h, mS=(8 if h == "lo" else 0x80),
             shS=(12 if h == "lo" else 8): E.tensor_scalar(
                 bufs["s16"], bufs[f"t16_{sp}"], mS, shS,
                 ALU.bitwise_and, ALU.logical_shift_left), v)
        p.do("vector", lambda E, sp=sp, h=h: E.scalar_tensor_tensor(
            bufs[f"{h}16_{sp}"].bitcast(BF16), bufs[f"{h}16_{sp}"].bitcast(BF16),
            -1.0, bufs[f"a16{h}_{sp}"].bitcast(BF16), ALU.mult, ALU.add), v)
        p.do("vector", lambda E, sp=sp, h=h: E.tensor_tensor(
            bufs[f"{h}16_{sp}"], bufs[f"{h}16_{sp}"], bufs["s16"],
            ALU.bitwise_xor), v)
        p.do("vector", lambda E, sp=sp, pp=pp, w=w, h=h, hoff=hoff:
             E.tensor_tensor(
                 bufs[f"pl{pp}"][:, hoff + (w % 2) * 512:
                                 hoff + (w % 2) * 512 + 512],
                 bufs[f"{h}16_{sp}"].bitcast(BF16), bufs[f"sce{sp}"],
                 ALU.mult), v)
    st8["v2"][gs] = v.n
    if w % 2 == 1:
        p.w("gpsimd", v, st8["v2"][gs])
        p.do("gpsimd", lambda E, bt=bt, w=w, pp=pp: E.dma_start(
            out=bt["store"][w // 2], in_=bufs[f"pl{pp}"][:, 0:2048]),
            sts[pp], 16)
        st8["store"][gp] = sts[pp].n


def emit_decode_v4_flush(p, S, bufs):
    pend = S.get("dv4", {}).pop("pend", None)
    if pend is not None:
        _dec4_back(p, S, bufs, *pend)



def dec_bts_for_iter(ins, scr_t, sb, it):
    bts = []
    for pk, sct, out in (("gpkT", "gscT", "sgate"), ("upkT", "uscT", "sup")):
        outt = scr_t[out].tensor
        W = outt.shape[1]
        for bt in range(8):
            c0 = it * 2048
            bts.append({
                "pk": ins[pk][128 * bt:128 * (bt + 1), c0:c0 + 2048],
                "sc": ins[sct][8 * bt:8 * (bt + 1), c0:c0 + 2048],
                "store_br": [bass.AP(outt,
                                     (256 * bt) * W + (2 * it + br) * 2048,
                                     [[W, 128], [128 * W, 2], [1, 2048]])
                             for br in range(2)],
                "ind": sb["ind16_sb"],
            })
    outt = scr_t["sdown"].tensor
    for bt in range(8):
        bts.append({
            "pk": ins["dpkT"][1024 * it + 128 * bt:1024 * it + 128 * (bt + 1),
                              0:2048],
            "sc": ins["dscT"][64 * it + 8 * bt:64 * it + 8 * (bt + 1), 0:2048],
            "store_br": [bass.AP(outt,
                                 (2048 * (2 * it + br) + 256 * bt) * 2048,
                                 [[2048, 128], [128 * 2048, 2], [1, 2048]])
                         for br in range(2)],
            "ind": sb["ind16h_sb"],
        })
    return bts


def emit_decode_v5(p, S, bufs, bts, ind_default):
    v, a, t = S["v"], S["a"], S["t"]
    lds = (S["ld0"], S["ld1"])
    lscs = (S["lsc0"], S["lsc1"])
    psf = S.setdefault("psfree", {})
    st8 = S.setdefault("dv5", {"cvt": {}, "v1": {}, "v2": {}, "a1": {},
                               "scev": {}, "mm": {}, "store": {}, "pk": {},
                               "sc": {}, "cvtlast": {}, "mmlast": {}, "nb": 0})

    def pwait(pn):
        if pn in psf:
            p.w("tensor", psf[pn][0], psf[pn][1])

    nb0 = st8["nb"]
    subs = []
    for bi, bt in enumerate(bts):
        for w in range(4):
            subs.append((nb0 + bi, bt, w))

    def front(B, bt, w):
        bp = B % 2
        ind = bt.get("ind", ind_default)
        gs = 4 * B + w
        sp = gs % 2
        gp = gs // 2
        pp = gp % 2
        pn = ("p6", "p7")[sp]
        if w == 0:
            if B - 2 in st8["cvtlast"]:
                p.w("sync", a, st8["cvtlast"][B - 2])
            p.do("sync", lambda E, bt=bt, bp=bp: E.dma_start(
                out=bufs[f"pkb{bp}"], in_=bt["pk"]), lds[bp], 16)
            st8["pk"][B] = lds[bp].n
        if gs - 2 in st8["mm"]:
            p.w("gpsimd", t, st8["mm"][gs - 2])
        p.do("gpsimd", lambda E, bt=bt, w=w, sp=sp: E.dma_start(
            out=bufs[f"scr{sp}"], in_=bt["sc"][0:8, w * 512:(w + 1) * 512]),
            lscs[sp], 16)
        st8["sc"][gs] = lscs[sp].n
        pwait(pn)
        p.w("tensor", lscs[sp], st8["sc"][gs])
        p.do("tensor", lambda E, pn=pn, sp=sp, ind=ind: E.matmul(
            p.ps[pn][:, 0:512], ind[0:8, :],
            bufs[f"scr{sp}"][0:8, 0:512],
            start=True, stop=True), t)
        st8["mm"][gs] = t.n
        p.w("scalar", lds[bp], st8["pk"][B])
        if gs - 2 in st8["v2"]:
            p.w("scalar", v, st8["v2"][gs - 2])
        p.do("scalar", lambda E, sp=sp, bp=bp, w=w: E.activation(
            bufs[f"t16_{sp}"], bufs[f"pkb{bp}"][:, w * 512:(w + 1) * 512],
            AF.Identity), a)
        st8["cvt"][gs] = a.n
        st8["cvtlast"][B] = a.n
        p.w("vector", a, st8["cvt"][gs])
        p.do("vector", lambda E, sp=sp: E.tensor_scalar(
            bufs[f"a16lo_{sp}"], bufs[f"t16_{sp}"], 7, 6,
            ALU.bitwise_and, ALU.logical_shift_left), v)
        p.do("vector", lambda E, sp=sp: E.tensor_scalar(
            bufs[f"a16hi_{sp}"], bufs[f"t16_{sp}"], 0x70, 2,
            ALU.bitwise_and, ALU.logical_shift_left), v)
        st8["v1"][gs] = v.n
        p.w("scalar", v, st8["v1"][gs])
        if gs - 2 in st8["v2"]:
            p.w("scalar", v, st8["v2"][gs - 2])
        for h in ("lo", "hi"):
            p.do("scalar", lambda E, sp=sp, h=h: E.activation(
                bufs[f"a16{h}_{sp}"], bufs[f"a16{h}_{sp}"], AF.Identity,
                bias=p.sb["c16k"][:, 0:1]), a)
            p.do("scalar", lambda E, sp=sp, h=h: E.activation(
                bufs[f"{h}16_{sp}"].bitcast(BF16),
                bufs[f"a16{h}_{sp}"].bitcast(BF16),
                AF.Relu, scale=-1.0, bias=1.0), a)
        st8["a1"][gs] = a.n

    pend = st8.pop("pend", None)
    for k, (B, bt, w) in enumerate(subs):
        front(B, bt, w)
        if pend is not None:
            _dec5_back(p, S, bufs, *pend)
        pend = (B, bt, w)
    st8["pend"] = pend
    st8["nb"] = nb0 + len(bts)


def _dec5_back(p, S, bufs, B, bt, w):
    v, a, t = S["v"], S["a"], S["t"]
    sts = (S["st0"], S["st1"])
    psf = S.setdefault("psfree", {})
    st8 = S["dv5"]
    gs = 4 * B + w
    sp = gs % 2
    pb = B % 2
    pn = ("p6", "p7")[sp]
    p.w("vector", a, st8["a1"][gs])
    p.w("vector", t, st8["mm"][gs])
    if w == 0 and B - 2 in st8["store"]:
        p.w("vector", sts[pb], st8["store"][B - 2])
    for h, hoff in (("lo", 0), ("hi", 2048)):
        p.do("vector", lambda E, sp=sp, h=h: E.scalar_tensor_tensor(
            bufs[f"{h}16_{sp}"].bitcast(BF16), bufs[f"{h}16_{sp}"].bitcast(BF16),
            -1.0, bufs[f"a16{h}_{sp}"].bitcast(BF16), ALU.mult, ALU.add), v)
        p.do("vector", lambda E, sp=sp, h=h, mS=(8 if h == "lo" else 0x80),
             shS=(12 if h == "lo" else 8): E.tensor_scalar(
                 bufs[f"a16lo_{sp}"], bufs[f"t16_{sp}"], mS, shS,
                 ALU.bitwise_and, ALU.logical_shift_left), v)
        p.do("vector", lambda E, sp=sp, h=h: E.tensor_tensor(
            bufs[f"{h}16_{sp}"], bufs[f"{h}16_{sp}"], bufs[f"a16lo_{sp}"],
            ALU.bitwise_xor), v)
        p.do("vector", lambda E, sp=sp, pb=pb, w=w, h=h, hoff=hoff, pn=pn:
             E.tensor_tensor(
                 bufs[f"pla{pb}"][:, hoff + 512 * w:hoff + 512 * w + 512],
                 bufs[f"{h}16_{sp}"].bitcast(BF16), p.ps[pn][0:128, 0:512],
                 ALU.mult), v)
    st8["v2"][gs] = v.n
    psf[pn] = (v, v.n)
    if w == 3:
        sem = sts[pb]
        p.w("gpsimd", v, st8["v2"][gs])
        sem.reserve(16)

        def body(E, bt=bt, pb=pb, sem=sem):
            for br in (0, 1):
                with E.register(_rn("dsb")) as rc:
                    E.reg_load(rc, p.sb["roleint_sb"][0:1, 2 + br:3 + br])
                    with E.While(rc):
                        E.dma_start(out=bt["store_br"][br],
                                    in_=bufs[f"pla{pb}"]).then_inc(sem.h, 16)
                        E.reg_mov(rc, 0)
        p.do("gpsimd", body)
        st8["store"][B] = sem.n


def emit_decode_v5_flush(p, S, bufs):
    pend = S.get("dv5", {}).pop("pend", None)
    if pend is not None:
        _dec5_back(p, S, bufs, *pend)


def emit_dec_flag_poll(p, S, it, xdflag):
    sts = (S["st0"], S["st1"])
    pd = S["pd"]
    pp = S[f"pdp{it}"]
    n0, n1 = sts[0].n, sts[1].n
    sts[0].reserve(16)
    nf = sts[0].n
    pd.reserve(16)

    def body(E, it=it, n0=n0, n1=n1, nf=nf):
        E.wait_ge(sts[0].h, n0)
        E.wait_ge(sts[1].h, n1)
        for br in (0, 1):
            with E.register(_rn("dfw")) as rc:
                E.reg_load(rc, p.sb["roleint_sb"][0:1, 2 + br:3 + br])
                with E.While(rc):
                    E.dma_start(
                        out=bass.AP(xdflag.tensor, br * 16, [[16, 1], [1, 16]]),
                        in_=p.sb["flagv_sb"][it:it + 1, 0:16]
                    ).then_inc(sts[0].h, 16)
                    E.reg_mov(rc, 0)
        E.wait_ge(sts[0].h, nf)
        with (
            E.register(_rn("qf0")) as rf0,
            E.register(_rn("qf1")) as rf1,
            E.register(_rn("qc")) as rc2,
            E.register(_rn("qk")) as rcont,
            E.register(_rn("qt")) as rtmp,
        ):
            E.reg_mov(rc2, 0)
            E.reg_mov(rcont, 1)
            with E.While(rcont):
                E.dma_start(out=p.sb["dflagsc"][0:2, 0:16],
                            in_=bass.AP(xdflag.tensor, 0, [[16, 2], [1, 16]])
                            ).then_inc(pp.h, 16)
                E.reg_add(rc2, rc2, 16)
                E.wait_ge(pp.h, rc2)
                E.reg_load(rtmp, p.sb["flagv_sb"][it:it + 1, 0:1])
                E.reg_load(rf0, p.sb["dflagsc"][0:1, it:it + 1])
                E.reg_load(rf1, p.sb["dflagsc"][1:2, it:it + 1])
                E.reg_alu(rf0, rf0, rtmp, ALU.not_equal)
                E.reg_alu(rf1, rf1, rtmp, ALU.not_equal)
                E.reg_alu(rcont, rf0, rf1, ALU.bitwise_or)
        E.dma_start(out=p.sb["dflagsc"][0:2, 0:16],
                    in_=bass.AP(xdflag.tensor, 0, [[16, 2], [1, 16]])
                    ).then_inc(pd.h, 16)
    p.do("gpsimd", body)


def dec_bts_for_super(ins, scr_t, sb, sup):
    bts = []
    for pk, sct, out, ind in (("gpkT", "gscT", "sgate", "ind16_sb"),
                              ("upkT", "uscT", "sup", "ind16_sb")):
        for bt in range(8):
            c0 = sup * 2048
            outt = scr_t[out].tensor
            W = outt.shape[1]
            bts.append({
                "pk": ins[pk][128 * bt:128 * (bt + 1), c0:c0 + 2048],
                "sc": ins[sct][8 * bt:8 * (bt + 1), c0:c0 + 2048],
                "store": [bass.AP(outt, (128 * 2 * bt) * W + c0 + 1024 * pr,
                                  [[W, 128], [128 * W, 2], [1, 1024]])
                          for pr in range(2)],
                "ind": sb[ind],
            })
    for bt in range(8):
        gbt = 8 * sup + bt
        outt = scr_t["sdown"].tensor
        W = outt.shape[1]
        bts.append({
            "pk": ins["dpkT"][128 * gbt:128 * (gbt + 1), 0:2048],
            "sc": ins["dscT"][8 * gbt:8 * (gbt + 1), 0:2048],
            "store": [bass.AP(outt, (128 * 2 * gbt) * W + 1024 * pr,
                              [[W, 128], [128 * W, 2], [1, 1024]])
                      for pr in range(2)],
            "ind": sb["ind16h_sb"],
        })
    return bts


# version 08
# speedup vs baseline: 1.1469x; 1.1469x over previous
import contextlib
import numpy as np
import ml_dtypes
import concourse.bass as bass
import concourse.mybir as mybir
from concourse.bass_utils import run_bass_kernel_spmd

F32 = mybir.dt.float32
BF16 = mybir.dt.bfloat16
U8 = mybir.dt.uint8
U16 = mybir.dt.uint16
I32 = mybir.dt.int32
AF = mybir.ActivationFunctionType
ALU = mybir.AluOpType
AXX = mybir.AxisListType.X

D = 2048
T = 1024
NMLP = 16384
BF = ml_dtypes.bfloat16
RACE_DETECT = True


def _new_nc():
    return bass.Bass("TRN2", target_bir_lowering=False,
                     detect_race_conditions=RACE_DETECT)


def _pi(n):
    kap = np.arange(n) // 128
    p = np.arange(n) % 128
    return (256 * (kap // 2) + 2 * p + (kap % 2)).astype(np.int64)


PI2048 = _pi(2048)
PI16384 = _pi(16384)
INV2048 = np.argsort(PI2048)


def _bytes_of_packed(w_int32):
    return np.ascontiguousarray(w_int32.view(np.uint8).reshape(*w_int32.shape, 4)[..., 0])


ENGS = ("sync", "scalar", "vector", "tensor", "gpsimd")


class Sem:
    __slots__ = ("h", "n", "name")

    def __init__(self, ctx, nc, name):
        self.h = ctx.enter_context(nc.semaphore(name))
        self.n = 0
        self.name = name

    def reserve(self, by=1):
        self.n += by
        return self.n


class Prog:
    def __init__(self, nc, ctx):
        self.nc = nc
        self.ctx = ctx
        self.q = {e: [] for e in ENGS}
        self.sb = {}
        self.ps = {}

    def sem(self, name):
        return Sem(self.ctx, self.nc, name)

    def sbuf(self, name, shape, dt):
        t = self.ctx.enter_context(self.nc.sbuf_tensor(name, list(shape), dt))
        self.sb[name] = t
        return t

    def psum(self, name, shape, dt=F32):
        t = self.ctx.enter_context(self.nc.psum_tensor(name, list(shape), dt))
        self.ps[name] = t
        return t

    def on(self, eng, fn):
        self.q[eng].append(fn)

    def w(self, eng, sem, n=None):
        v = sem.n if n is None else n
        if v <= 0:
            return
        self.q[eng].append(lambda E, s=sem.h, v=v: E.wait_ge(s, v))

    CHAINED = ("vector", "scalar", "gpsimd")

    def do(self, eng, fn, sem=None, by=1, chain=True):
        if chain and eng in self.CHAINED:
            prev = getattr(self, "_chain", {}).get(eng)
            if prev is not None:
                self.q[eng].append(lambda E, s=prev[0], v=prev[1]: E.wait_ge(s, v))
        if sem is None:
            self.q[eng].append(fn)
            return None
        sem.reserve(by)
        self.q[eng].append(lambda E, f=fn, s=sem.h, b=by: f(E).then_inc(s, b))
        if eng in self.CHAINED and by == 1:
            if not hasattr(self, "_chain"):
                self._chain = {}
            self._chain[eng] = (sem.h, sem.n)
        return sem.n

    def emit(self):
        with self.nc.Block() as block:
            for e in ENGS:
                fns = self.q[e]
                if not fns:
                    continue

                def run(E, fns=fns):
                    for f in fns:
                        f(E)
                getattr(block, e)(run)



def alloc_decode_bufs(p, wmax):
    p.sbuf("pk0", [128, wmax], U8)
    p.sbuf("pk1", [128, wmax], U8)
    p.sbuf("dq_t", [128, wmax], U16)
    p.sbuf("dq_a", [128, wmax], U16)
    p.sbuf("dq_s", [128, wmax], U16)
    p.sbuf("dq_lo", [128, wmax], U16)
    p.sbuf("dq_hi", [128, wmax], U16)
    p.sbuf("pl0", [128, 2 * wmax], BF16)
    p.sbuf("pl1", [128, 2 * wmax], BF16)
    p.sbuf("sc_f32", [8, 8 * wmax], F32)
    p.sbuf("sc_b", [8, 8 * wmax], BF16)
    p.psum("pdec0", [128, 512])
    p.psum("pdec1", [128, 512])


def emit_decode_matrix(p, st8, *, n_bt, width, pk_dram, scT_dram, out_dram, out_row0,
                       ind, eng="vector", dma="sync"):
    dec, mm = st8["dec"], st8["mm"]
    lds = (st8["ld0"], st8["ld1"])
    ldsc = st8["ldsc"]
    sts = (st8["st0"], st8["st1"])
    pk_free = st8.setdefault("pk_free", {})
    pl_free = st8.setdefault("pl_free", {})
    ps_free = st8.setdefault("ps_free", {})
    W = width
    NWIN = (W + 511) // 512
    sb = p.sb

    p.w(dma, mm)
    p.do(dma, lambda E: E.dma_start(
        out=sb["sc_f32"][0:8, 0:n_bt * W],
        in_=bass.AP(scT_dram.tensor, 0, [[W, 8], [8 * W, n_bt], [1, W]])), ldsc, 16)
    p.w(eng, ldsc)
    p.do(eng, lambda E: E.tensor_copy(sb["sc_b"][0:8, 0:n_bt * W],
                                      sb["sc_f32"][0:8, 0:n_bt * W]), dec)
    sc_ready = dec.n

    for bt in range(n_bt):
        buf = bt % 2
        pkt = sb[f"pk{buf}"]
        plb = sb[f"pl{buf}"]
        if buf in pk_free:
            p.w(dma, dec, pk_free[buf])
        p.do(dma, lambda E, bt=bt, pkt=pkt: E.dma_start(
            out=pkt[:, 0:W], in_=pk_dram[128 * bt:128 * bt + 128, 0:W]), lds[buf], 16)
        p.w(eng, lds[buf])
        p.do(eng, lambda E, pkt=pkt: E.tensor_copy(sb["dq_t"][:, 0:W], pkt[:, 0:W]), dec)
        pk_free[buf] = dec.n
        t16 = sb["dq_t"]

        for pl, (mA, shA, mS, shS) in enumerate([(7, 6, 8, 12), (0x70, 2, 0x80, 8)]):
            dst = sb["dq_lo"] if pl == 0 else sb["dq_hi"]
            p.do(eng, lambda E, mA=mA, shA=shA: E.tensor_scalar(
                sb["dq_a"][:, 0:W], t16[:, 0:W], mA, shA,
                ALU.bitwise_and, ALU.logical_shift_left), dec)
            p.do(eng, lambda E, mS=mS, shS=shS: E.tensor_scalar(
                sb["dq_s"][:, 0:W], t16[:, 0:W], mS, shS,
                ALU.bitwise_and, ALU.logical_shift_left), dec)
            p.do(eng, lambda E: E.tensor_single_scalar(
                sb["dq_a"][:, 0:W], sb["dq_a"][:, 0:W], 16128, ALU.add), dec)
            p.do(eng, lambda E, dst=dst: E.tensor_scalar(
                dst[:, 0:W].bitcast(BF16), sb["dq_a"][:, 0:W].bitcast(BF16),
                2.0, -1.0, ALU.mult, ALU.add), dec)
            p.do(eng, lambda E, dst=dst: E.tensor_tensor(
                dst[:, 0:W].bitcast(BF16), sb["dq_a"][:, 0:W].bitcast(BF16),
                dst[:, 0:W].bitcast(BF16), ALU.min), dec)
            p.do(eng, lambda E, dst=dst: E.tensor_tensor(
                dst[:, 0:W], dst[:, 0:W], sb["dq_s"][:, 0:W], ALU.bitwise_xor), dec)

        if buf in pl_free:
            p.w(eng, sts[buf], pl_free[buf])
        for win in range(NWIN):
            w0 = 512 * win
            w1 = min(w0 + 512, W)
            psq = win % 2
            pn = f"pdec{psq}"
            if psq in ps_free:
                p.w("tensor", dec, ps_free[psq])
            p.w("tensor", dec, sc_ready)
            p.do("tensor", lambda E, bt=bt, w0=w0, w1=w1, pn=pn, ind=ind: E.matmul(
                p.ps[pn][:, 0:w1 - w0], ind[0:8, :],
                sb["sc_b"][0:8, bt * W + w0:bt * W + w1], start=True, stop=True), mm)
            p.w(eng, mm)
            p.do(eng, lambda E, w0=w0, w1=w1, pn=pn, plb=plb: E.tensor_tensor(
                plb[:, w0:w1], sb["dq_lo"][:, w0:w1].bitcast(BF16),
                p.ps[pn][:, 0:w1 - w0], ALU.mult), dec)
            p.do(eng, lambda E, w0=w0, w1=w1, pn=pn, plb=plb: E.tensor_tensor(
                plb[:, W + w0:W + w1], sb["dq_hi"][:, w0:w1].bitcast(BF16),
                p.ps[pn][:, 0:w1 - w0], ALU.mult), dec)
            ps_free[psq] = dec.n
        p.w(dma, dec)
        for pl in range(2):
            kap = 2 * bt + pl
            p.do(dma, lambda E, kap=kap, pl=pl, plb=plb: E.dma_start(
                out=out_dram[out_row0 + 128 * kap: out_row0 + 128 * (kap + 1), 0:W],
                in_=plb[:, pl * W:(pl + 1) * W]), sts[buf], 16)
        pl_free[buf] = sts[buf].n



def host_prepare(inp):
    g = {}
    pi_n = PI16384
    pi_d = PI2048
    gpk = _bytes_of_packed(inp["L0_ffn_gate_W_packed"])
    upk = _bytes_of_packed(inp["L0_ffn_up_W_packed"])
    dpk = _bytes_of_packed(inp["L0_ffn_down_W_packed"])
    g["gate_pkT"] = np.ascontiguousarray(gpk[pi_n].T)
    g["up_pkT"] = np.ascontiguousarray(upk[pi_n].T)
    g["down_pkT"] = np.ascontiguousarray(dpk[pi_d].T)
    g["gate_scT"] = np.ascontiguousarray(inp["L0_ffn_gate_scales"][pi_n].T)
    g["up_scT"] = np.ascontiguousarray(inp["L0_ffn_up_scales"][pi_n].T)
    g["down_scT"] = np.ascontiguousarray(inp["L0_ffn_down_scales"][pi_d].T)
    g["winT"] = np.ascontiguousarray(inp["L0_attn_W_in"].T[pi_d])
    g["woutT"] = np.ascontiguousarray(inp["L0_attn_W_out"].T[:, pi_d])
    g["decay_pt"] = np.ascontiguousarray(
        inp["L0_attn_decay"].reshape(16, 128).T)
    g["bias_row"] = np.ascontiguousarray(
        inp["L0_attn_bias"][pi_d].reshape(1, 2048))
    g["wq"] = np.ascontiguousarray(
        inp["L1_attn_Wq"].reshape(2048, 2048)[pi_d])
    g["wk"] = np.ascontiguousarray(inp["L1_attn_Wk"].reshape(2048, 512)[pi_d])
    g["wv"] = np.ascontiguousarray(inp["L1_attn_Wv"].reshape(2048, 512)[pi_d])
    g["wo"] = np.ascontiguousarray(inp["L1_attn_Wo"][:, pi_d])
    g["clfT"] = np.ascontiguousarray(inp["L1_ffn_classifier_w"].T[pi_d])
    g["escb"] = np.ascontiguousarray(
        inp["L1_ffn_e_score_correction_bias"].reshape(1, 12))
    g["l1gT"] = np.ascontiguousarray(
        inp["L1_ffn_gate_w"].reshape(8192, 2048).T[pi_d])
    g["l1uT"] = np.ascontiguousarray(
        inp["L1_ffn_up_w"].reshape(8192, 2048).T[pi_d])
    g["l1dT"] = np.ascontiguousarray(
        inp["L1_ffn_down_w"].transpose(0, 2, 1)[:, :, pi_d].reshape(8192, 2048))
    ind = np.zeros((8, 128), BF)
    for j in range(8):
        ind[j, 16 * j:16 * (j + 1)] = 1.0
    g["ind16"] = ind
    g["ind16h"] = (ind.astype(np.float32) * 0.5).astype(BF)
    g["ident"] = np.eye(128).astype(BF)
    return g


def shard_x(inp):
    xs = []
    x = inp["L0_attn_x"]
    for c in range(8):
        b, r = c // 2, c % 2
        xs.append(np.ascontiguousarray(x[b, r * T:(r + 1) * T, :].T[PI2048]))
    return xs


def unshard_out(shards):
    out = np.empty((4, 2048, 2048), np.float32)
    for c in range(8):
        b, r = c // 2, c % 2
        out[b, r * T:(r + 1) * T, :] = shards[c][INV2048].T
    return out




def build_prep():
    nc = _new_nc()
    ins = {}
    for nm, shape, dt in [
        ("gpk", [1024, 2048], U8), ("gsc", [64, 2048], F32),
        ("upk", [1024, 2048], U8), ("usc", [64, 2048], F32),
        ("dpk", [1024, 2048], U8), ("dsc", [64, 2048], F32),
        ("l1g", [2048, 1024], F32), ("l1u", [2048, 1024], F32),
        ("l1d", [1024, 2048], F32),
        ("winT_s", [256, 2048], F32), ("woutT_s", [256, 2048], F32),
        ("wq_s", [256, 2048], F32), ("wk_s", [256, 512], F32),
        ("wv_s", [256, 512], F32), ("wo_s", [256, 2048], F32),
        ("clf_s", [256, 12], F32),
        ("ind16", [8, 128], BF16), ("ind16h", [8, 128], BF16),
    ]:
        ins[nm] = nc.dram_tensor(nm, shape, dt, kind="ExternalInput").ap()
    outs = {}
    for nm, shape in [("gdec", [2048, 2048]), ("udec", [2048, 2048]),
                      ("ddec", [2048, 2048]), ("l1gb", [2048, 1024]),
                      ("l1ub", [2048, 1024]), ("l1db", [1024, 2048]),
                      ("winb_s", [256, 2048]), ("woutb_s", [256, 2048]),
                      ("wqb_s", [256, 2048]), ("wkb_s", [256, 512]),
                      ("wvb_s", [256, 512]), ("wob_s", [256, 2048]),
                      ("clfb_s", [256, 12])]:
        outs[nm] = nc.dram_tensor(nm, shape, BF16, kind="ExternalOutput").ap()

    with contextlib.ExitStack() as ctx:
        p = Prog(nc, ctx)
        alloc_decode_bufs(p, 2048)
        p.sbuf("c_ind16", [8, 128], BF16)
        p.sbuf("c_ind16h", [8, 128], BF16)
        p.sbuf("cv0", [128, 2048], F32)
        p.sbuf("cv1", [128, 2048], F32)
        p.sbuf("cb0", [128, 2048], BF16)
        p.sbuf("cb1", [128, 2048], BF16)
        sems = {k: p.sem(k) for k in ("ld", "ld0", "ld1", "ldsc", "dec", "mm",
                                      "st0", "st1", "c_ld0", "c_ld1", "c_dec",
                                      "c_st0", "c_st1")}
        p.do("sync", lambda E: E.dma_start(out=p.sb["c_ind16"][:], in_=ins["ind16"]), sems["ld"], 16)
        p.do("sync", lambda E: E.dma_start(out=p.sb["c_ind16h"][:], in_=ins["ind16h"]), sems["ld"], 16)
        p.w("tensor", sems["ld"])
        emit_decode_matrix(p, sems, n_bt=8, width=2048, pk_dram=ins["gpk"],
                           scT_dram=ins["gsc"], out_dram=outs["gdec"], out_row0=0,
                           ind=p.sb["c_ind16"])
        emit_decode_matrix(p, sems, n_bt=8, width=2048, pk_dram=ins["upk"],
                           scT_dram=ins["usc"], out_dram=outs["udec"], out_row0=0,
                           ind=p.sb["c_ind16"])
        emit_decode_matrix(p, sems, n_bt=8, width=2048, pk_dram=ins["dpk"],
                           scT_dram=ins["dsc"], out_dram=outs["ddec"], out_row0=0,
                           ind=p.sb["c_ind16h"])

        clds = (sems["c_ld0"], sems["c_ld1"])
        cdec = sems["c_dec"]
        csts = (sems["c_st0"], sems["c_st1"])
        conv_jobs = []
        for nm_in, nm_out, rows, cols in [("l1g", "l1gb", 2048, 1024),
                                          ("l1u", "l1ub", 2048, 1024),
                                          ("l1d", "l1db", 1024, 2048),
                                          ("winT_s", "winb_s", 256, 2048),
                                          ("woutT_s", "woutb_s", 256, 2048),
                                          ("wq_s", "wqb_s", 256, 2048),
                                          ("wk_s", "wkb_s", 256, 512),
                                          ("wv_s", "wvb_s", 256, 512),
                                          ("wo_s", "wob_s", 256, 2048),
                                          ("clf_s", "clfb_s", 256, 12)]:
            for t0 in range(0, rows, 128):
                conv_jobs.append((nm_in, nm_out, t0, cols))
        stored = {}
        for i, (nm_in, nm_out, t0, cols) in enumerate(conv_jobs):
            buf = i % 2
            cv, cb = p.sb[f"cv{buf}"], p.sb[f"cb{buf}"]
            if i - 2 >= 0:
                p.w("gpsimd", cdec, stored[i - 2][0])
            p.do("gpsimd", lambda E, nm_in=nm_in, t0=t0, cols=cols, cv=cv: E.dma_start(
                out=cv[:, 0:cols], in_=ins[nm_in][t0:t0 + 128, 0:cols]), clds[i % 2], 16)
            p.w("scalar", clds[i % 2])
            if i - 2 >= 0:
                p.w("scalar", csts[i % 2], stored[i - 2][1])
            p.do("scalar", lambda E, cols=cols, cv=cv, cb=cb: E.copy(
                cb[:, 0:cols], cv[:, 0:cols]), cdec)
            stored_dec = cdec.n
            p.w("gpsimd", cdec)
            p.do("gpsimd", lambda E, nm_out=nm_out, t0=t0, cols=cols, cb=cb: E.dma_start(
                out=outs[nm_out][t0:t0 + 128, 0:cols], in_=cb[:, 0:cols]), csts[i % 2], 16)
            stored[i] = (stored_dec, csts[i % 2].n)

        p.w("sync", sems["st0"])
        p.w("sync", sems["st1"])
        p.w("gpsimd", sems["c_st0"])
        p.w("gpsimd", sems["c_st1"])
        p.emit()
    return nc


def host_prep_inputs(g):
    maps = []
    for c in range(8):
        n0, n1 = 2048 * c, 2048 * (c + 1)
        maps.append({
            "gpk": np.ascontiguousarray(g["gate_pkT"][:, n0:n1]),
            "gsc": np.ascontiguousarray(g["gate_scT"][:, n0:n1]),
            "upk": np.ascontiguousarray(g["up_pkT"][:, n0:n1]),
            "usc": np.ascontiguousarray(g["up_scT"][:, n0:n1]),
            "dpk": np.ascontiguousarray(g["down_pkT"][1024 * c:1024 * (c + 1), :]),
            "dsc": np.ascontiguousarray(g["down_scT"][64 * c:64 * (c + 1), :]),
            "l1g": np.ascontiguousarray(g["l1gT"][:, 1024 * c:1024 * (c + 1)]),
            "l1u": np.ascontiguousarray(g["l1uT"][:, 1024 * c:1024 * (c + 1)]),
            "l1d": np.ascontiguousarray(g["l1dT"][1024 * c:1024 * (c + 1), :]),
            "winT_s": np.ascontiguousarray(g["winT"][256 * c:256 * (c + 1)]),
            "woutT_s": np.ascontiguousarray(g["woutT"][256 * c:256 * (c + 1)]),
            "wq_s": np.ascontiguousarray(g["wq"][256 * c:256 * (c + 1)]),
            "wk_s": np.ascontiguousarray(g["wk"][256 * c:256 * (c + 1)]),
            "wv_s": np.ascontiguousarray(g["wv"][256 * c:256 * (c + 1)]),
            "wo_s": np.ascontiguousarray(g["wo"][256 * c:256 * (c + 1)]),
            "clf_s": np.ascontiguousarray(g["clfT"][256 * c:256 * (c + 1)]),
            "ind16": g["ind16"], "ind16h": g["ind16h"],
        })
    return maps


def host_prep_assemble(results):
    w = {}
    w["gdec"] = np.concatenate([np.asarray(r["gdec"]) for r in results], axis=1)
    w["udec"] = np.concatenate([np.asarray(r["udec"]) for r in results], axis=1)
    w["ddec"] = np.concatenate([np.asarray(r["ddec"]) for r in results], axis=0)
    w["l1gb"] = np.concatenate([np.asarray(r["l1gb"]) for r in results], axis=1)
    w["l1ub"] = np.concatenate([np.asarray(r["l1ub"]) for r in results], axis=1)
    w["l1db"] = np.concatenate([np.asarray(r["l1db"]) for r in results], axis=0)
    for nm in ("winb", "woutb", "wqb", "wkb", "wvb", "wob", "clfb"):
        w[nm] = np.concatenate([np.asarray(r[nm + "_s"]) for r in results], axis=0)
    return w


def kernel(**inputs):
    global RACE_DETECT
    RACE_DETECT = False
    g = host_prepare(inputs)
    nc = build_fused()
    nonce = int(np.random.randint(1, 1 << 30))
    maps = host_fused_inputs(inputs, g, nonce)
    res = run_bass_kernel_spmd(nc, maps, core_ids=list(range(8)))
    return unshard_out([np.asarray(res.results[c]["xo"]) for c in range(8)])



def emit_ln(p, S, *, x, xn, scr, xn_guard=None, scr_guard=None):
    sb = p.sb
    v, a, t = S["v"], S["a"], S["t"]
    psf = S.setdefault("psfree", {})

    def pwait(pn):
        if pn in psf:
            p.w("tensor", psf[pn][0], psf[pn][1])

    for gd in (xn_guard or []):
        p.w("vector", gd[0], gd[1])
    for gd in (scr_guard or []):
        p.w("scalar", gd[0], gd[1])
    p.do("vector", lambda E: E.tensor_copy(sb[xn][:, :], sb[x][:, :]), v)
    xb_ready = v.n
    p.w("scalar", v, xb_ready)
    p.do("scalar", lambda E: E.activation(sb[scr][:, :], sb[xn][:, :], AF.Square), a)
    sq_ready = a.n
    for (src, pa, pb, rsem, rdy) in [(xn, "p0", "p1", v, xb_ready),
                                     (scr, "p2", "p3", a, sq_ready)]:
        for th, pn in ((0, pa), (1, pb)):
            pwait(pn)
            p.w("tensor", rsem, rdy)
            for kc in range(16):
                p.do("tensor", lambda E, src=src, th=th, pn=pn, kc=kc: E.matmul(
                    p.ps[pn][0:1, 0:512], sb["ones_col"][:, 0:1],
                    sb[src][:, kc * 1024 + th * 512: kc * 1024 + (th + 1) * 512],
                    start=(kc == 0), stop=(kc == 15)),
                    t if kc == 15 else None)
    mm_done = t.n
    S["ln_mm_done"] = mm_done
    rows = sb["rows"]
    p.w("vector", t, mm_done)
    for th in range(2):
        pa = ("p0", "p1")[th]
        pb = ("p2", "p3")[th]
        p.do("vector", lambda E, pa=pa: E.tensor_single_scalar(
            rows[0:1, 0:512], p.ps[pa][0:1, 0:512], 1.0 / 2048.0, ALU.mult), v)
        p.do("vector", lambda E, pb=pb: E.tensor_scalar(
            rows[0:1, 512:1024], p.ps[pb][0:1, 0:512], 1.0 / 2048.0, 1e-5,
            ALU.mult, ALU.add), v)
        p.do("vector", lambda E: E.tensor_mul(sb["t512a"][0:1, 0:512],
                                              rows[0:1, 0:512], rows[0:1, 0:512]), v)
        p.do("vector", lambda E: E.tensor_sub(rows[0:1, 512:1024], rows[0:1, 512:1024],
                                              sb["t512a"][0:1, 0:512]), v)
        p.w("scalar", v)
        p.do("scalar", lambda E: E.activation(sb["t512a"][0:1, 0:512],
                                              rows[0:1, 512:1024], AF.Sqrt), a)
        p.w("vector", a)
        p.do("vector", lambda E: E.reciprocal(rows[0:1, 512:1024],
                                              sb["t512a"][0:1, 0:512]), v)
        p.do("vector", lambda E: E.tensor_copy(sb["rows_b"][0:1, 0:512],
                                               rows[0:1, 512:1024]), v)
        p.do("vector", lambda E: E.scalar_tensor_tensor(
            sb["rows_b"][0:1, 512:1024], rows[0:1, 0:512], -1.0,
            rows[0:1, 512:1024], ALU.mult, ALU.mult), v)
        rowsb_ready = v.n
        for pn, r0 in (("p4", 0), ("p5", 512)):
            pwait(pn)
            p.w("tensor", v, rowsb_ready)
            p.do("tensor", lambda E, pn=pn, r0=r0: E.matmul(
                p.ps[pn][:, 0:512], sb["rows_b"][0:1, 1024:1152],
                sb["rows_b"][0:1, r0:r0 + 512], start=True, stop=True), t)
        bc_done = t.n
        p.w("vector", t, bc_done)
        for c in range(16):
            p.do("vector", lambda E, c=c, th=th: E.tensor_mul(
                sb["t512a"][:, 0:512],
                sb[x][:, c * 1024 + th * 512:c * 1024 + (th + 1) * 512],
                p.ps["p4"][:, 0:512]), v)
            p.do("vector", lambda E, c=c, th=th: E.tensor_add(
                sb[xn][:, c * 1024 + th * 512:c * 1024 + (th + 1) * 512],
                sb["t512a"][:, 0:512], p.ps["p5"][:, 0:512]), v)
        psf["p4"] = psf["p5"] = (v, v.n)
        psf[pa] = psf[pb] = (v, v.n)
    S["xn_ready"] = v.n




def emit_ffn0(p, S, *, gdec, udec, ddec, x, xn, inter, n_sup=8,
              single_buf=False, deps=None, pre_super=None, tick=None,
              dbl_from=99):
    sb = p.sb
    v, a, t = S["v"], S["a"], S["t"]
    psf = S.setdefault("psfree", {})
    lg = (S["lg0"], S["lg1"])
    lu = (S["lu0"], S["lu1"])
    wfree = S.setdefault("wfree", {})

    def pwait(pn):
        if pn in psf:
            p.w("tensor", psf[pn][0], psf[pn][1])

    def load_win(buf, lsem, q, dram, n0, row_len, dep=None):
        key = f"{buf}{q}"
        if key in wfree:
            p.w("sync", wfree[key][0], wfree[key][1])
        if dep is not None:
            for ds, dn in dep:
                p.w("sync", ds, dn)
        p.do("sync", lambda E, buf=buf, q=q, dram=dram, n0=n0, row_len=row_len:
             E.dma_start(
                 out=sb[f"{buf}{q}"][:, 0:8192],
                 in_=bass.AP(dram.tensor, n0,
                             [[row_len, 128], [128 * row_len, 16], [1, 512]])),
             lsem, 16)
        return lsem.n

    xn_ready = S["xn_ready"]
    first_group = [True]

    def mm_group(wbuf, q, pn, rhs_name, rhs_off, ld_sem, ld_cnt, mi):
        pwait(pn)
        p.w("tensor", ld_sem, ld_cnt)
        if first_group[0]:
            p.w("tensor", v, xn_ready)
            first_group[0] = False
        for kc in range(16):
            p.do("tensor", lambda E, wbuf=wbuf, q=q, pn=pn, kc=kc, mi=mi,
                 rhs_name=rhs_name, rhs_off=rhs_off: E.matmul(
                     p.ps[pn][:, 0:512],
                     sb[f"{wbuf}{q}"][:, kc * 512 + mi * 128:kc * 512 + (mi + 1) * 128],
                     sb[rhs_name][:, kc * 1024 + rhs_off:kc * 1024 + rhs_off + 512],
                     start=(kc == 0), stop=(kc == 15)),
                 t if kc == 15 else None)
        return t.n

    for sup in range(n_sup):
        if pre_super is not None:
            pre_super(sup)
        depg = depu = depd = None
        if deps is not None:
            depg = deps.get(("g", sup))
            depu = deps.get(("u", sup))
            depd = deps.get(("d", sup))
        for wi in range(4):
            q = (wi % 2) if (not single_buf or sup >= dbl_from) else 0
            n0 = sup * 2048 + wi * 512
            ld_g = load_win("wg", lg[q], q, gdec, n0, 16384, dep=depg)
            ld_u = load_win("wu", lu[q], q, udec, n0, 16384, dep=depu)
            for mi in range(4):
                for th in range(2):
                    png = f"p{(mi * 2 + th) % 2}"
                    pnu = f"p{2 + (mi * 2 + th) % 2}"
                    g_done = mm_group("wg", q, png, xn, th * 512, lg[q], ld_g, mi)
                    u_done = mm_group("wu", q, pnu, xn, th * 512, lu[q], ld_u, mi)
                    p.w("scalar", t, g_done)
                    if "t512a_free" in S:
                        p.w("scalar", v, S["t512a_free"])
                    p.do("scalar", lambda E, png=png: E.activation(
                        sb["t512a"][:, 0:512], p.ps[png][0:128, 0:512],
                        AF.Gelu_apprx_tanh), a)
                    gl_done = a.n
                    psf[png] = (a, gl_done)
                    nch = wi * 4 + mi
                    p.w("vector", a, gl_done)
                    p.w("vector", t, u_done)
                    p.do("vector", lambda E, nch=nch, th=th, pnu=pnu: E.tensor_mul(
                        sb[inter][:, nch * 1024 + th * 512:nch * 1024 + (th + 1) * 512],
                        sb["t512a"][:, 0:512], p.ps[pnu][0:128, 0:512]), v)
                    psf[pnu] = (v, v.n)
                    S["t512a_free"] = v.n
            wfree[f"wg{q}"] = (t, g_done)
            wfree[f"wu{q}"] = (t, u_done)
            if tick is not None:
                tick()
        inter_done = v.n
        for wi in range(4):
            q = (wi % 2) if (not single_buf or sup >= dbl_from) else 0
            d0 = sup * 16 * 128 * 2048 + wi * 512
            ld_d = load_win("wg", lg[q], q, ddec, d0, 2048, dep=depd)
            for mi in range(4):
                for th in range(2):
                    pnd = f"p{4 + (mi * 2 + th) % 2}"
                    pwait(pnd)
                    p.w("tensor", lg[q], ld_d)
                    p.w("tensor", v, inter_done)
                    for kc in range(16):
                        p.do("tensor", lambda E, q=q, pnd=pnd, kc=kc, mi=mi, th=th:
                             E.matmul(
                                 p.ps[pnd][:, 0:512],
                                 sb[f"wg{q}"][:, kc * 512 + mi * 128:kc * 512 + (mi + 1) * 128],
                                 sb[inter][:, kc * 1024 + th * 512:kc * 1024 + (th + 1) * 512],
                                 start=(kc == 0), stop=(kc == 15)),
                             t if kc == 15 else None)
                    d_done = t.n
                    dch = wi * 4 + mi
                    p.w("vector", t, d_done)
                    p.do("vector", lambda E, dch=dch, th=th, pnd=pnd: E.tensor_add(
                        sb[x][:, dch * 1024 + th * 512:dch * 1024 + (th + 1) * 512],
                        sb[x][:, dch * 1024 + th * 512:dch * 1024 + (th + 1) * 512],
                        p.ps[pnd][0:128, 0:512]), v)
                    psf[pnd] = (v, v.n)
            wfree[f"wg{q}"] = (t, d_done)
            if tick is not None:
                tick()
    S["x_ready"] = v.n


def alloc_main_bufs(p):
    p.sbuf("x", [128, 16 * 1024], F32)
    p.sbuf("bufA", [128, 16 * 1024], BF16)
    p.sbuf("bufB", [128, 16 * 1024], BF16)
    p.sbuf("wg0", [128, 8192], BF16)
    p.sbuf("wg1", [128, 8192], BF16)
    p.sbuf("wu0", [128, 8192], BF16)
    p.sbuf("wu1", [128, 8192], BF16)
    p.sbuf("rows", [1, 1024], F32)
    p.sbuf("rows_b", [1, 1024 + 128], BF16)
    p.sbuf("t512a", [128, 512], F32)
    p.sbuf("ones_col", [128, 1], BF16)
    for i in range(8):
        p.psum(f"p{i}", [128, 512])


def emit_consts(p, S):
    v = S["v"]
    p.do("vector", lambda E: E.memset(p.sb["ones_col"][:], 1.0), v)
    p.do("vector", lambda E: E.memset(p.sb["rows_b"][0:1, 1024:1152], 1.0), v)
    if "c16k" in p.sb:
        p.do("vector", lambda E: E.memset(p.sb["c16k"][:, 0:1], 16128.0), v)


def build_ffn0(n_sup=8, single_buf=False, dbl_from=99):
    nc = _new_nc()
    xin = nc.dram_tensor("xin", [2048, 1024], F32, kind="ExternalInput").ap()
    gdec = nc.dram_tensor("gdec", [2048, 16384], BF16, kind="ExternalInput").ap()
    udec = nc.dram_tensor("udec", [2048, 16384], BF16, kind="ExternalInput").ap()
    ddec = nc.dram_tensor("ddec", [16384, 2048], BF16, kind="ExternalInput").ap()
    xo = nc.dram_tensor("xo", [2048, 1024], F32, kind="ExternalOutput").ap()
    with contextlib.ExitStack() as ctx:
        p = Prog(nc, ctx)
        alloc_main_bufs(p)
        S = {k: p.sem(k) for k in ("v", "a", "t", "lg0", "lg1", "lu0", "lu1",
                                   "lx", "sx")}
        emit_consts(p, S)
        p.do("scalar", lambda E: E.dma_start(
            out=p.sb["x"][:, :],
            in_=bass.AP(xin.tensor, 0, [[1024, 128], [128 * 1024, 16], [1, 1024]])),
            S["lx"], 16)
        p.w("vector", S["lx"])
        p.w("scalar", S["lx"])
        emit_ln(p, S, x="x", xn="bufA", scr="bufB")
        emit_ffn0(p, S, gdec=gdec, udec=udec, ddec=ddec, x="x", xn="bufA",
                  inter="bufB", n_sup=n_sup, single_buf=single_buf,
                  dbl_from=dbl_from)
        p.w("scalar", S["v"], S["x_ready"])
        p.do("scalar", lambda E: E.dma_start(
            out=bass.AP(xo.tensor, 0, [[1024, 128], [128 * 1024, 16], [1, 1024]]),
            in_=p.sb["x"][:, :]), S["sx"], 16)
        p.w("scalar", S["sx"])
        p.emit()
    return nc



_REG_UID = [0]


def _rn(base):
    _REG_UID[0] += 1
    return f"{base}{_REG_UID[0]}"


def emit_xchg_write(p, S, *, shared_t, off_idx, sbuf_ap, nrows, rowlen,
                    flag_t, flagval_row, wsem):
    def body(E):
        with E.register(_rn("r_off")) as r_off:
            E.reg_load(r_off, p.sb["xoffs_sb"][0:1, off_idx:off_idx + 1])
            E.dma_start(out=bass.AP(shared_t.tensor, r_off, [[rowlen, nrows], [1, rowlen]]),
                        in_=sbuf_ap).then_inc(wsem.h, 16)
            wsem.reserve(16)
            E.wait_ge(wsem.h, wsem.n)
            E.reg_load(r_off, p.sb["xoffs_sb"][0:1, 2:3])
            E.dma_start(out=bass.AP(flag_t.tensor, r_off, [[16, 1], [1, 16]]),
                        in_=p.sb["flagv_sb"][flagval_row:flagval_row + 1, 0:16]
                        ).then_inc(wsem.h, 16)
            wsem.reserve(16)
            E.wait_ge(wsem.h, wsem.n)
    p.do("gpsimd", body)


def emit_xchg_poll_read(p, S, *, shared_t, off_idx, sbuf_ap, nrows, rowlen,
                        flag_t, flagval_row, psem, rsem, roleint_col=0,
                        maxpoll=2000000):
    def body(E):
        with (
            E.register(_rn("rf")) as rf,
            E.register(_rn("rc")) as rc,
            E.register(_rn("rcont")) as rcont,
            E.register(_rn("rtmp")) as rtmp,
            E.register(_rn("r_off")) as r_off,
        ):
            E.reg_load(rcont, p.sb["roleint_sb"][0:1, roleint_col:roleint_col + 1])
            E.reg_load(r_off, p.sb["xoffs_sb"][0:1, 3:4])
            E.reg_mov(rc, 0)
            with E.While(rcont):
                E.dma_start(out=p.sb["flagsc"][0:1, 0:16],
                            in_=bass.AP(flag_t.tensor, r_off, [[16, 1], [1, 16]])
                            ).then_inc(psem.h, 16)
                E.reg_add(rc, rc, 16)
                E.wait_ge(psem.h, rc)
                E.reg_load(rf, p.sb["flagsc"][0:1, 0:1])
                E.reg_load(rtmp, p.sb["flagv_sb"][flagval_row:flagval_row + 1, 0:1])
                E.reg_alu(rcont, rf, rtmp, ALU.not_equal)
                E.reg_alu(rtmp, rc, 16 * maxpoll, ALU.is_lt)
                E.reg_alu(rcont, rcont, rtmp, ALU.bitwise_and)
            E.reg_load(r_off, p.sb["xoffs_sb"][0:1, off_idx:off_idx + 1])
            E.dma_start(out=sbuf_ap,
                        in_=bass.AP(shared_t.tensor, r_off, [[rowlen, nrows], [1, rowlen]])
                        ).then_inc(rsem.h, 16)
            rsem.reserve(16)
            E.wait_ge(rsem.h, rsem.n)
    p.do("gpsimd", body)


def alloc_xchg_bufs(p):
    p.sbuf("xoffs_sb", [1, 8], I32)
    p.sbuf("flagv_sb", [4, 16], I32)
    p.sbuf("roleint_sb", [1, 16], I32)
    p.sbuf("flagsc", [1, 16], I32)
    p.sbuf("rolem_sb", [128, 1], I32)
    p.sbuf("decay_pt_sb", [128, 16], F32)
    p.sbuf("bias_pt_sb", [128, 16], F32)
    p.sbuf("dsig", [128, 16], F32)
    p.sbuf("omd", [128, 16], F32)
    p.sbuf("fin", [128, 16], F32)
    p.sbuf("pfin", [128, 16], F32)



def emit_attn0(p, S, *, winb, woutb, x, xn, prj, xfin, xflag):
    sb = p.sb
    v, a, t = S["v"], S["a"], S["t"]
    psf = S.setdefault("psfree", {})
    wfree = S.setdefault("wfree", {})
    wsems = (S["lg0"], S["lg1"], S["lu0"], S["lu1"])
    wnames = ("wg0", "wg1", "wu0", "wu1")

    def pwait(pn):
        if pn in psf:
            p.w("tensor", psf[pn][0], psf[pn][1])

    p.do("scalar", lambda E: E.activation(sb["dsig"][:, 0:16], sb["decay_pt_sb"][:, 0:16],
                                          AF.Sigmoid), a)
    p.w("vector", a)
    p.do("vector", lambda E: E.tensor_scalar(
        sb["omd"][:, 0:16], sb["dsig"][:, 0:16], -1.0, 1.0, ALU.mult, ALU.add), v)
    omd_ready = v.n

    def load_w_bf16(dram):
        for b in range(4):
            key = wnames[b]
            if key in wfree:
                p.w("sync", wfree[key][0], wfree[key][1])
            p.do("sync", lambda E, b=b, dram=dram: E.dma_start(
                out=sb[wnames[b]][:, 0:8192],
                in_=bass.AP(dram.tensor, b * 4 * 128 * 2048,
                            [[2048, 128], [128 * 2048, 4], [1, 2048]])),
                wsems[b], 16)
        return [ws.n for ws in wsems]

    w_ready = load_w_bf16(winb)

    def wmm(mi, th, pn, rdy):
        pwait(pn)
        for b in range(4):
            p.w("tensor", wsems[b], rdy[b])
        for kc in range(16):
            wb = sb[wnames[kc // 4]]
            p.do("tensor", lambda E, mi=mi, th=th, kc=kc, pn=pn, wb=wb: E.matmul(
                p.ps[pn][:, 0:512],
                wb[:, (kc % 4) * 2048 + mi * 128:(kc % 4) * 2048 + (mi + 1) * 128],
                sb[xn][:, kc * 1024 + th * 512:kc * 1024 + (th + 1) * 512],
                start=(kc == 0), stop=(kc == 15)), t if kc == 15 else None)
        return t.n

    p.w("tensor", v, S["xn_ready"])
    for mi in range(16):
        for th in range(2):
            pn = f"p{(mi * 2 + th) % 2}"
            mmd = wmm(mi, th, pn, w_ready)
            p.w("vector", t, mmd)
            p.w("vector", v, omd_ready)
            p.do("vector", lambda E, mi=mi, th=th, pn=pn: E.tensor_scalar(
                sb[prj][:, mi * 1024 + th * 512:mi * 1024 + (th + 1) * 512],
                p.ps[pn][0:128, 0:512], sb["omd"][:, mi:mi + 1], None,
                ALU.mult), v)
            psf[pn] = (v, v.n)
    prj_ready = v.n
    win_consumed = t.n

    emit_xchg_poll_read(p, S, shared_t=xfin, off_idx=1, sbuf_ap=sb["pfin"][:, 0:16],
                        nrows=128, rowlen=16, flag_t=xflag, flagval_row=0,
                        psem=S["poll0"], rsem=S["xls"], roleint_col=0)
    xch = S["xch"]
    xch.reserve(1)
    p.on("gpsimd", lambda E, s=xch.h: E.nop().then_inc(s, 1))
    p.w("vector", xch)
    p.do("vector", lambda E: E.tensor_tensor(
        sb["pfin"][:, 0:16].bitcast(I32), sb["pfin"][:, 0:16].bitcast(I32),
        sb["rolem_sb"][:, 0:1].bitcast(I32).broadcast_to([128, 16]),
        ALU.bitwise_and), v)
    p.w("vector", v, prj_ready)
    for c in range(16):
        p.do("vector", lambda E, c=c: E.tensor_tensor_scan(
            sb[xn][:, c * 1024:(c + 1) * 1024],
            sb["dsig"][:, c:c + 1].broadcast_to([128, 1024]),
            sb[prj][:, c * 1024:(c + 1) * 1024],
            sb["pfin"][:, c:c + 1], ALU.mult, ALU.add), v)
    scans_done = v.n
    p.do("vector", lambda E: E.tensor_copy(
        sb["fin"][:, 0:16],
        bass.AP(sb[xn], 1023, [[16384, 128], [1024, 16]])), v)
    fin_ready = v.n
    p.w("gpsimd", v, fin_ready)
    emit_xchg_write(p, S, shared_t=xfin, off_idx=0, sbuf_ap=sb["fin"][:, 0:16],
                    nrows=128, rowlen=16, flag_t=xflag, flagval_row=0,
                    wsem=S["xws"])

    for b in range(4):
        wfree[wnames[b]] = (t, win_consumed)
    w2_ready = load_w_bf16(woutb)
    p.w("tensor", v, scans_done)
    for mi in range(16):
        for th in range(2):
            pn = f"p{2 + (mi * 2 + th) % 2}"
            mmd = wmm(mi, th, pn, w2_ready)
            p.w("vector", t, mmd)
            p.do("vector", lambda E, mi=mi, th=th, pn=pn: E.scalar_tensor_tensor(
                sb[x][:, mi * 1024 + th * 512:mi * 1024 + (th + 1) * 512],
                p.ps[pn][0:128, 0:512], sb["bias_pt_sb"][:, mi:mi + 1],
                sb[x][:, mi * 1024 + th * 512:mi * 1024 + (th + 1) * 512],
                ALU.add, ALU.add), v)
            psf[pn] = (v, v.n)
    for b in range(4):
        wfree[wnames[b]] = (t, t.n)
    S["x_ready"] = v.n


def build_attn0():
    nc = _new_nc()
    xin = nc.dram_tensor("xin", [2048, 1024], F32, kind="ExternalInput").ap()
    winb = nc.dram_tensor("winb", [2048, 2048], BF16, kind="ExternalInput").ap()
    woutb = nc.dram_tensor("woutb", [2048, 2048], BF16, kind="ExternalInput").ap()
    decay_pt = nc.dram_tensor("decay_pt", [128, 16], F32, kind="ExternalInput").ap()
    bias_pt = nc.dram_tensor("bias_pt", [128, 16], F32, kind="ExternalInput").ap()
    xoffs = nc.dram_tensor("xoffs", [1, 8], I32, kind="ExternalInput").ap()
    flagvals = nc.dram_tensor("flagvals", [4, 16], I32, kind="ExternalInput").ap()
    roleint = nc.dram_tensor("roleint", [1, 2], I32, kind="ExternalInput").ap()
    rolem = nc.dram_tensor("rolem", [128, 1], I32, kind="ExternalInput").ap()
    xo = nc.dram_tensor("xo", [2048, 1024], F32, kind="ExternalOutput").ap()
    xfin = nc.dram_tensor("xfin", [256, 16], F32, kind="Internal",
                          addr_space="Shared").ap()
    xflag = nc.dram_tensor("xflag", [2, 16], I32, kind="Internal",
                           addr_space="Shared").ap()
    with contextlib.ExitStack() as ctx:
        p = Prog(nc, ctx)
        alloc_main_bufs(p)
        alloc_xchg_bufs(p)
        S = {k: p.sem(k) for k in ("v", "a", "t", "lg0", "lg1", "lu0", "lu1",
                                   "lx", "sx", "poll0", "xls", "xws", "xch")}
        emit_consts(p, S)
        for nm, dst in [("decay_pt", "decay_pt_sb"), ("bias_pt", "bias_pt_sb"),
                        ("flagvals", "flagv_sb"), ("roleint", "roleint_sb"),
                        ("rolem", "rolem_sb"), ("xoffs", "xoffs_sb")]:
            ap_in = {"decay_pt": decay_pt, "bias_pt": bias_pt, "flagvals": flagvals,
                     "roleint": roleint, "rolem": rolem, "xoffs": xoffs}[nm]
            shp = ap_in.shape
            p.do("scalar", lambda E, ap_in=ap_in, dst=dst, shp=shp: E.dma_start(
                out=p.sb[dst][0:shp[0], 0:shp[1]], in_=ap_in), S["lx"], 16)
        p.do("scalar", lambda E: E.dma_start(
            out=p.sb["x"][:, :],
            in_=bass.AP(xin.tensor, 0, [[1024, 128], [128 * 1024, 16], [1, 1024]])),
            S["lx"], 16)
        p.w("vector", S["lx"])
        p.w("scalar", S["lx"])
        p.w("gpsimd", S["lx"])
        emit_ln(p, S, x="x", xn="bufA", scr="bufB")
        emit_attn0(p, S, winb=winb, woutb=woutb, x="x", xn="bufA", prj="bufB",
                   xfin=xfin, xflag=xflag)
        p.w("scalar", S["v"], S["x_ready"])
        p.do("scalar", lambda E: E.dma_start(
            out=bass.AP(xo.tensor, 0, [[1024, 128], [128 * 1024, 16], [1, 1024]]),
            in_=p.sb["x"][:, :]), S["sx"], 16)
        p.w("scalar", S["sx"])
        p.w("gpsimd", S["xws"])
        p.emit()
    return nc


def host_xchg_inputs(c, nonce):
    role = c % 2
    return {
        "xoffs": np.array([[role * 128 * 16, (1 - role) * 128 * 16,
                            role * 16, (1 - role) * 16,
                            role * 1024, 128 * 8192 + role * 4096, 0, 0]], np.int32),
        "flagvals": np.tile(np.arange(4, dtype=np.int32)[:, None] + nonce, (1, 16)),
        "roleint": np.array([[role, 1, 1 - role, role]], np.int32),
        "rolem": np.full((128, 1), -role, np.int32),
    }



def emit_ffn1(p, S, *, l1gb, l1ub, l1db, x, xn, inter):
    sb = p.sb
    v, a, t = S["v"], S["a"], S["t"]
    psf = S.setdefault("psfree", {})
    wfree = S.setdefault("wfree", {})
    lg = (S["lg0"], S["lg1"])
    lu = (S["lu0"], S["lu1"])

    def pwait(pn):
        if pn in psf:
            p.w("tensor", psf[pn][0], psf[pn][1])

    rt = sb["rt"]

    pwait("p6")
    p.w("tensor", v, S["xn_ready"])
    p.w("tensor", S["lmisc"])
    if "lmiscg" in S:
        p.w("tensor", S["lmiscg"])
    for tt in range(8):
        for kc in range(16):
            p.do("tensor", lambda E, tt=tt, kc=kc: E.matmul(
                p.ps["p6"][:, 16 * tt:16 * tt + 12],
                sb[xn][:, kc * 1024 + tt * 128:kc * 1024 + (tt + 1) * 128],
                sb["clfb_sb"][:, kc * 12:(kc + 1) * 12],
                start=(kc == 0), stop=(kc == 15)), t if kc == 15 else None)
    clf_done = t.n
    pwait("p7")
    p.do("tensor", lambda E: E.matmul(
        p.ps["p7"][:, 0:16], sb["rows_b"][0:1, 1024:1152],
        sb["escb_sb"][0:1, 0:16], start=True, stop=True), t)
    escbc_done = t.n

    p.w("vector", t, escbc_done)
    for tt in range(8):
        base = tt * 48
        col = lambda o, w=1: rt[:, base + o:base + o + w]
        ps6 = p.ps["p6"][:, 16 * tt:16 * tt + 12]
        p.do("vector", lambda E, ps6=ps6, base=base: E.tensor_copy(
            rt[:, base:base + 12], ps6), v)
        p.do("vector", lambda E, base=base: E.tensor_reduce(
            sb["rcol"][:, 0:1], rt[:, base:base + 12].rearrange("p (s n) -> p s n", s=1),
            axis=AXX, op=ALU.max, negate=True), v)
        nm_ready = v.n
        p.w("scalar", v, nm_ready)
        p.do("scalar", lambda E, base=base: E.activation(
            rt[:, base + 16:base + 28], rt[:, base:base + 12], AF.Exp,
            bias=sb["rcol"][:, 0:1]), a)
        p.w("vector", a)
        p.do("vector", lambda E, base=base: E.tensor_reduce(
            sb["rcol"][:, 1:2], rt[:, base + 16:base + 28].rearrange("p (s n) -> p s n", s=1),
            axis=AXX, op=ALU.add), v)
        p.do("vector", lambda E: E.reciprocal(sb["rcol"][:, 2:3], sb["rcol"][:, 1:2]), v)
        p.do("vector", lambda E, base=base: E.tensor_scalar(
            rt[:, base:base + 12], rt[:, base + 16:base + 28], sb["rcol"][:, 2:3],
            None, ALU.mult), v)
        p.do("vector", lambda E, base=base: E.tensor_tensor(
            rt[:, base + 16:base + 28], rt[:, base:base + 12],
            p.ps["p7"][:, 0:12], ALU.add), v)
        p.do("vector", lambda E, base=base: E.tensor_reduce(
            sb["rcol"][:, 3:4], rt[:, base + 16:base + 28].rearrange("p (s n) -> p s n", s=1),
            axis=AXX, op=ALU.max), v)
        p.do("vector", lambda E, base=base: E.tensor_scalar(
            rt[:, base + 32:base + 44], rt[:, base + 16:base + 28], sb["rcol"][:, 3:4],
            None, ALU.is_equal), v)
        p.do("vector", lambda E, base=base: E.scalar_tensor_tensor(
            rt[:, base + 32:base + 44], rt[:, base + 32:base + 44], -1e9,
            rt[:, base + 16:base + 28], ALU.mult, ALU.add), v)
        p.do("vector", lambda E, base=base: E.tensor_reduce(
            sb["rcol"][:, 4:5], rt[:, base + 32:base + 44].rearrange("p (s n) -> p s n", s=1),
            axis=AXX, op=ALU.max), v)
        p.do("vector", lambda E, base=base: E.tensor_scalar(
            rt[:, base + 32:base + 44], rt[:, base + 16:base + 28], sb["rcol"][:, 4:5],
            None, ALU.is_ge), v)
        p.do("vector", lambda E, base=base: E.scalar_tensor_tensor(
            rt[:, base + 16:base + 28], rt[:, base:base + 12], 0.75,
            rt[:, base + 32:base + 44], ALU.mult, ALU.mult), v)
        p.do("vector", lambda E, base=base: E.tensor_reduce(
            rt[:, base + 24:base + 25],
            rt[:, base + 24:base + 28].rearrange("p (s n) -> p s n", s=1),
            axis=AXX, op=ALU.add), v)
    route_done = v.n
    psf["p6"] = psf["p7"] = (v, route_done)

    def bcast_expert(e):
        pwait("p6")
        pwait("p7")
        p.w("tensor", v, route_done)
        for tt in range(8):
            pnm = "p6" if tt < 4 else "p7"
            p.do("tensor", lambda E, tt=tt, e=e, pnm=pnm: E.transpose(
                p.ps[pnm][0:1, (tt % 4) * 128:(tt % 4 + 1) * 128],
                rt[:, tt * 48 + 16 + e:tt * 48 + 16 + e + 1],
                sb["identf_sb"][:, 0:128]), t)
        trd = t.n
        p.w("vector", t, trd)
        for tt in range(8):
            pnm = "p6" if tt < 4 else "p7"
            p.do("vector", lambda E, tt=tt, pnm=pnm: E.tensor_copy(
                sb["rows_b"][0:1, tt * 128:(tt + 1) * 128],
                p.ps[pnm][0:1, (tt % 4) * 128:(tt % 4 + 1) * 128]), v)
        rowdone = v.n
        psf["p6"] = (v, rowdone)
        p.w("tensor", v, rowdone)
        for th, pn in ((0, "p6"), (1, "p7")):
            p.do("tensor", lambda E, th=th, pn=pn: E.matmul(
                p.ps[pn][:, 0:512], sb["rows_b"][0:1, 1024:1152],
                sb["rows_b"][0:1, th * 512:(th + 1) * 512], start=True, stop=True), t)
        return t.n

    bz = bcast_expert(8)
    p.w("vector", t, bz)
    for c in range(16):
        for th, pn in ((0, "p6"), (1, "p7")):
            p.do("vector", lambda E, c=c, th=th, pn=pn: E.tensor_mul(
                sb["t512a"][:, 0:512],
                sb[xn][:, c * 1024 + th * 512:c * 1024 + (th + 1) * 512],
                p.ps[pn][:, 0:512]), v)
            p.do("vector", lambda E, c=c, th=th: E.tensor_add(
                sb[x][:, c * 1024 + th * 512:c * 1024 + (th + 1) * 512],
                sb[x][:, c * 1024 + th * 512:c * 1024 + (th + 1) * 512],
                sb["t512a"][:, 0:512]), v)
    psf["p6"] = psf["p7"] = (v, v.n)

    for e in range(8):
        bdone = bcast_expert(e)
        web_free_at = None
        for wi in range(2):
            q = wi % 2
            i0 = e * 1024 + wi * 512
            for stream, dram, lsem, wbuf in (("g", l1gb, lg[q], f"wg{q}"),
                                             ("u", l1ub, lu[q], f"wu{q}")):
                if wbuf in wfree:
                    p.w("sync", wfree[wbuf][0], wfree[wbuf][1])
                p.do("sync", lambda E, dram=dram, i0=i0, wbuf=wbuf: E.dma_start(
                    out=sb[wbuf][:, 0:8192],
                    in_=bass.AP(dram.tensor, i0,
                                [[8192, 128], [128 * 8192, 16], [1, 512]])),
                    lsem, 16)
            ldg, ldu = lg[q].n, lu[q].n
            for mi in range(4):
                for th in range(2):
                    png = f"p{(mi * 2 + th) % 2}"
                    pnu = f"p{2 + (mi * 2 + th) % 2}"
                    pwait(png)
                    pwait(pnu)
                    p.w("tensor", lg[q], ldg)
                    p.w("tensor", lu[q], ldu)
                    for kc in range(16):
                        p.do("tensor", lambda E, q=q, png=png, kc=kc, mi=mi, th=th:
                             E.matmul(
                                 p.ps[png][:, 0:512],
                                 sb[f"wg{q}"][:, kc * 512 + mi * 128:kc * 512 + (mi + 1) * 128],
                                 sb[xn][:, kc * 1024 + th * 512:kc * 1024 + (th + 1) * 512],
                                 start=(kc == 0), stop=(kc == 15)),
                             t if kc == 15 else None)
                    g_done = t.n
                    for kc in range(16):
                        p.do("tensor", lambda E, q=q, pnu=pnu, kc=kc, mi=mi, th=th:
                             E.matmul(
                                 p.ps[pnu][:, 0:512],
                                 sb[f"wu{q}"][:, kc * 512 + mi * 128:kc * 512 + (mi + 1) * 128],
                                 sb[xn][:, kc * 1024 + th * 512:kc * 1024 + (th + 1) * 512],
                                 start=(kc == 0), stop=(kc == 15)),
                             t if kc == 15 else None)
                    u_done = t.n
                    p.w("scalar", t, g_done)
                    if "t512a_free" in S:
                        p.w("scalar", v, S["t512a_free"])
                    p.do("scalar", lambda E, png=png: E.activation(
                        sb["t512a"][:, 0:512], p.ps[png][0:128, 0:512], AF.Silu), a)
                    sl_done = a.n
                    psf[png] = (a, sl_done)
                    ich = wi * 4 + mi
                    p.w("vector", a, sl_done)
                    p.w("vector", t, u_done)
                    p.w("vector", t, bdone)
                    wepn = "p6" if th == 0 else "p7"
                    p.do("vector", lambda E, wepn=wepn: E.tensor_mul(
                        sb["t512a"][:, 0:512], sb["t512a"][:, 0:512],
                        p.ps[wepn][:, 0:512]), v)
                    p.do("vector", lambda E, ich=ich, th=th, pnu=pnu: E.tensor_mul(
                        sb[inter][:, ich * 1024 + th * 512:ich * 1024 + (th + 1) * 512],
                        sb["t512a"][:, 0:512], p.ps[pnu][0:128, 0:512]), v)
                    psf[pnu] = (v, v.n)
                    S["t512a_free"] = v.n
                    web_free_at = v.n
            wfree[f"wg{q}"] = (t, g_done)
            wfree[f"wu{q}"] = (t, u_done)
        psf["p6"] = psf["p7"] = (v, web_free_at)
        inter_done = v.n
        for wi in range(4):
            q = wi % 2
            key = f"wg{q}"
            if key in wfree:
                p.w("sync", wfree[key][0], wfree[key][1])
            p.do("sync", lambda E, e=e, wi=wi, q=q: E.dma_start(
                out=sb[f"wg{q}"][:, 0:4096],
                in_=bass.AP(l1db.tensor, e * 1024 * 2048 + wi * 512,
                            [[2048, 128], [128 * 2048, 8], [1, 512]])), lg[q], 16)
            ldd = lg[q].n
            for mi in range(4):
                for th in range(2):
                    pnd = f"p{4 + (mi * 2 + th) % 2}"
                    pwait(pnd)
                    p.w("tensor", lg[q], ldd)
                    p.w("tensor", v, inter_done)
                    for kc in range(8):
                        p.do("tensor", lambda E, q=q, pnd=pnd, kc=kc, mi=mi, th=th:
                             E.matmul(
                                 p.ps[pnd][:, 0:512],
                                 sb[f"wg{q}"][:, kc * 512 + mi * 128:kc * 512 + (mi + 1) * 128],
                                 sb[inter][:, kc * 1024 + th * 512:kc * 1024 + (th + 1) * 512],
                                 start=(kc == 0), stop=(kc == 7)),
                             t if kc == 7 else None)
                    d_done = t.n
                    dch = wi * 4 + mi
                    p.w("vector", t, d_done)
                    p.do("vector", lambda E, dch=dch, th=th, pnd=pnd: E.tensor_add(
                        sb[x][:, dch * 1024 + th * 512:dch * 1024 + (th + 1) * 512],
                        sb[x][:, dch * 1024 + th * 512:dch * 1024 + (th + 1) * 512],
                        p.ps[pnd][0:128, 0:512]), v)
                    psf[pnd] = (v, v.n)
            wfree[f"wg{q}"] = (t, d_done)
    S["x_ready"] = v.n


def alloc_attn1_bufs(p):
    p.sbuf("negc", [128, 1], F32)
    p.sbuf("tb512", [128, 512], BF16)


def alloc_ffn1_bufs(p):
    p.sbuf("rt", [128, 8 * 48], F32)
    p.sbuf("rcol", [128, 8], F32)
    p.sbuf("clfb_sb", [128, 16 * 12], BF16)
    p.sbuf("escb_sb", [1, 16], BF16)
    p.sbuf("escf_sb", [1, 16], F32)
    p.sbuf("identf_sb", [128, 128], F32)


def build_ffn1():
    nc = _new_nc()
    xin = nc.dram_tensor("xin", [2048, 1024], F32, kind="ExternalInput").ap()
    l1gb = nc.dram_tensor("l1gb", [2048, 8192], BF16, kind="ExternalInput").ap()
    l1ub = nc.dram_tensor("l1ub", [2048, 8192], BF16, kind="ExternalInput").ap()
    l1db = nc.dram_tensor("l1db", [8192, 2048], BF16, kind="ExternalInput").ap()
    clfb = nc.dram_tensor("clfb", [2048, 12], BF16, kind="ExternalInput").ap()
    escb = nc.dram_tensor("escb", [1, 12], F32, kind="ExternalInput").ap()
    identf = nc.dram_tensor("identf", [128, 128], F32, kind="ExternalInput").ap()
    xo = nc.dram_tensor("xo", [2048, 1024], F32, kind="ExternalOutput").ap()
    with contextlib.ExitStack() as ctx:
        p = Prog(nc, ctx)
        alloc_main_bufs(p)
        alloc_ffn1_bufs(p)
        S = {k: p.sem(k) for k in ("v", "a", "t", "lg0", "lg1", "lu0", "lu1",
                                   "lx", "sx", "lmisc")}
        emit_consts(p, S)
        p.do("scalar", lambda E: E.dma_start(
            out=p.sb["clfb_sb"][:, :],
            in_=bass.AP(clfb.tensor, 0, [[12, 128], [128 * 12, 16], [1, 12]])),
            S["lmisc"], 16)
        p.do("scalar", lambda E: E.dma_start(out=p.sb["escf_sb"][0:1, 0:12], in_=escb),
             S["lmisc"], 16)
        p.do("scalar", lambda E: E.dma_start(out=p.sb["identf_sb"][:, :], in_=identf),
             S["lmisc"], 16)
        p.do("vector", lambda E: E.memset(p.sb["escb_sb"][0:1, 0:16], 0.0), S["v"])
        p.w("vector", S["lmisc"])
        p.do("vector", lambda E: E.tensor_copy(p.sb["escb_sb"][0:1, 0:12],
                                               p.sb["escf_sb"][0:1, 0:12]), S["v"])
        p.do("scalar", lambda E: E.dma_start(
            out=p.sb["x"][:, :],
            in_=bass.AP(xin.tensor, 0, [[1024, 128], [128 * 1024, 16], [1, 1024]])),
            S["lx"], 16)
        p.w("vector", S["lx"])
        p.w("scalar", S["lx"])
        p.w("vector", S["lmisc"])
        emit_ln(p, S, x="x", xn="bufA", scr="bufB")
        emit_ffn1(p, S, l1gb=l1gb, l1ub=l1ub, l1db=l1db, x="x", xn="bufA",
                  inter="bufB")
        p.w("scalar", S["v"], S["x_ready"])
        p.do("scalar", lambda E: E.dma_start(
            out=bass.AP(xo.tensor, 0, [[1024, 128], [128 * 1024, 16], [1, 1024]]),
            in_=p.sb["x"][:, :]), S["sx"], 16)
        p.w("scalar", S["sx"])
        p.emit()
    return nc


def emit_xchg_write2(p, S, *, shared_t, flag_t, flagval_row, wsem, parts):
    def body(E):
        with E.register(_rn("r_off")) as r_off:
            for off_idx, pat, sap in parts:
                E.reg_load(r_off, p.sb["xoffs_sb"][0:1, off_idx:off_idx + 1])
                E.dma_start(out=bass.AP(shared_t.tensor, r_off, pat),
                            in_=sap).then_inc(wsem.h, 16)
                wsem.reserve(16)
            E.wait_ge(wsem.h, wsem.n)
            E.reg_load(r_off, p.sb["xoffs_sb"][0:1, 2:3])
            E.dma_start(out=bass.AP(flag_t.tensor, r_off, [[16, 1], [1, 16]]),
                        in_=p.sb["flagv_sb"][flagval_row:flagval_row + 1, 0:16]
                        ).then_inc(wsem.h, 16)
            wsem.reserve(16)
            E.wait_ge(wsem.h, wsem.n)
    p.do("gpsimd", body)


def emit_xchg_poll_read2(p, S, *, shared_t, flag_t, flagval_row, psem, rsem,
                         reads, roleint_col, maxpoll=2000000):
    def body(E):
        with (
            E.register(_rn("rf")) as rf,
            E.register(_rn("rc")) as rc,
            E.register(_rn("rcont")) as rcont,
            E.register(_rn("rtmp")) as rtmp,
            E.register(_rn("r_off")) as r_off,
        ):
            E.reg_load(rcont, p.sb["roleint_sb"][0:1, roleint_col:roleint_col + 1])
            E.reg_load(r_off, p.sb["xoffs_sb"][0:1, 3:4])
            E.reg_mov(rc, 0)
            with E.While(rcont):
                E.dma_start(out=p.sb["flagsc"][0:1, 0:16],
                            in_=bass.AP(flag_t.tensor, r_off, [[16, 1], [1, 16]])
                            ).then_inc(psem.h, 16)
                E.reg_add(rc, rc, 16)
                E.wait_ge(psem.h, rc)
                E.reg_load(rf, p.sb["flagsc"][0:1, 0:1])
                E.reg_load(rtmp, p.sb["flagv_sb"][flagval_row:flagval_row + 1, 0:1])
                E.reg_alu(rcont, rf, rtmp, ALU.not_equal)
                E.reg_alu(rtmp, rc, 16 * maxpoll, ALU.is_lt)
                E.reg_alu(rcont, rcont, rtmp, ALU.bitwise_and)
            for off, sap, pat in reads:
                E.dma_start(out=sap, in_=bass.AP(shared_t.tensor, off, pat)
                            ).then_inc(rsem.h, 16)
                rsem.reserve(16)
            E.wait_ge(rsem.h, rsem.n)
    p.do("gpsimd", body)



def emit_attn1(p, S, *, wqb, wkb, wvb, wob, x, xn, qt, xkv, xflag):
    sb = p.sb
    v, a, t = S["v"], S["a"], S["t"]
    psf = S.setdefault("psfree", {})
    wfree = S.setdefault("wfree", {})
    lg1, lu1 = S["lg1"], S["lu1"]
    lkv = (S["lg0"], S["lu0"])

    def pwait(pn):
        if pn in psf:
            p.w("tensor", psf[pn][0], psf[pn][1])

    def wwait(*keys):
        for key in keys:
            if key in wfree:
                p.w("sync", wfree[key][0], wfree[key][1])


    p.do("vector", lambda E: E.memset(sb["negc"][:, 0:1], -11.4), v)
    negc_ready = v.n

    def proj_group(wsrc, colbase, pn, th, rdy_sem, rdy_cnt):
        pwait(pn)
        p.w("tensor", rdy_sem, rdy_cnt)
        for kc in range(16):
            p.do("tensor", lambda E, kc=kc, colbase=colbase, pn=pn, th=th, wsrc=wsrc:
                 E.matmul(
                     p.ps[pn][:, 0:512],
                     sb[wsrc][:, kc * 512 + colbase:kc * 512 + colbase + 128],
                     sb[xn][:, kc * 1024 + th * 512:kc * 1024 + (th + 1) * 512],
                     start=(kc == 0), stop=(kc == 15)), t if kc == 15 else None)
        return t.n

    def qknorm_evac(pn, dst_ap_fn, fold_sqrt_d):
        p.do("vector", lambda E: E.tensor_copy(dst_ap_fn(), p.ps[pn][0:128, 0:512]), v)
        raw_done = v.n
        psf[pn] = (v, raw_done)
        p.w("scalar", v, raw_done)
        p.do("scalar", lambda E: E.activation(sb["tb512"][:, 0:512], dst_ap_fn(),
                                              AF.Square), a)
        sq_done = a.n
        pwait("p2")
        p.w("tensor", a, sq_done)
        p.do("tensor", lambda E: E.matmul(p.ps["p2"][0:1, 0:512],
                                          sb["ones_col"][:, 0:1],
                                          sb["tb512"][:, 0:512],
                                          start=True, stop=True), t)
        ssd = t.n
        p.w("vector", t, ssd)
        if fold_sqrt_d:
            p.do("vector", lambda E: E.tensor_single_scalar(
                sb["rows"][0:1, 0:512], p.ps["p2"][0:1, 0:512], 1.28e-4, ALU.add), v)
        else:
            p.do("vector", lambda E: E.tensor_scalar(
                sb["rows"][0:1, 0:512], p.ps["p2"][0:1, 0:512], 1.0 / 128.0, 1e-6,
                ALU.mult, ALU.add), v)
        psf["p2"] = (v, v.n)
        p.w("scalar", v)
        p.do("scalar", lambda E: E.activation(sb["t512a"][0:1, 0:512],
                                              sb["rows"][0:1, 0:512], AF.Sqrt), a)
        p.w("vector", a)
        p.do("vector", lambda E: E.reciprocal(sb["rows"][0:1, 0:512],
                                              sb["t512a"][0:1, 0:512]), v)
        p.do("vector", lambda E: E.tensor_copy(sb["rows_b"][0:1, 0:512],
                                               sb["rows"][0:1, 0:512]), v)
        rb = v.n
        pwait("p3")
        p.w("tensor", v, rb)
        p.do("tensor", lambda E: E.matmul(p.ps["p3"][:, 0:512],
                                          sb["rows_b"][0:1, 1024:1152],
                                          sb["rows_b"][0:1, 0:512],
                                          start=True, stop=True), t)
        bcd = t.n
        p.w("vector", t, bcd)
        p.do("vector", lambda E: E.tensor_mul(dst_ap_fn(), dst_ap_fn(),
                                              p.ps["p3"][:, 0:512]), v)
        psf["p3"] = (v, v.n)

    wwait("wg1", "wg0")
    p.do("sync", lambda E: E.dma_start(
        out=sb["wg1"][:, 0:8192],
        in_=bass.AP(wkb.tensor, 0, [[512, 128], [128 * 512, 16], [1, 512]])), lg1, 16)
    ldk = lg1.n
    p.w("tensor", v, S["xn_ready"])
    for kvh in range(4):
        for th in range(2):
            pn = f"p{(kvh * 2 + th) % 2}"
            mmd = proj_group("wg1", kvh * 128, pn, th, lg1, ldk)
            p.w("vector", t, mmd)
            qknorm_evac(pn, lambda kvh=kvh, th=th: sb["wg0"][:, kvh * 2048 + th * 512:
                                                            kvh * 2048 + (th + 1) * 512],
                        fold_sqrt_d=False)
    k_done = v.n
    wwait("wu1", "wu0")
    p.do("sync", lambda E: E.dma_start(
        out=sb["wu1"][:, 0:8192],
        in_=bass.AP(wvb.tensor, 0, [[512, 128], [128 * 512, 16], [1, 512]])), lu1, 16)
    ldv = lu1.n
    for tt in range(8):
        pn = f"p{4 + tt % 2}"
        pwait(pn)
        p.w("tensor", lu1, ldv)
        for kc in range(16):
            p.do("tensor", lambda E, kc=kc, tt=tt, pn=pn: E.matmul(
                p.ps[pn][:, 0:512],
                sb[xn][:, kc * 1024 + tt * 128:kc * 1024 + (tt + 1) * 128],
                sb["wu1"][:, kc * 512:(kc + 1) * 512],
                start=(kc == 0), stop=(kc == 15)), t if kc == 15 else None)
        mmd = t.n
        p.w("vector", t, mmd)
        p.do("vector", lambda E, tt=tt, pn=pn: E.tensor_copy(
            sb["wu0"][:, tt * 512:(tt + 1) * 512], p.ps[pn][0:128, 0:512]), v)
        psf[pn] = (v, v.n)
    v_done = v.n

    qbufs = ("wg1", "wu1")
    qsems = (lg1, lu1)
    kv_consumed = t.n
    qwin_used = {0: kv_consumed, 1: kv_consumed}
    for w in range(4):
        q = w % 2
        p.w("sync", t, qwin_used[q])
        p.do("sync", lambda E, w=w, q=q: E.dma_start(
            out=sb[qbufs[q]][:, 0:8192],
            in_=bass.AP(wqb.tensor, w * 512,
                        [[2048, 128], [128 * 2048, 16], [1, 512]])), qsems[q], 16)
        ldq = qsems[q].n
        for hh in range(4):
            h = w * 4 + hh
            for th in range(2):
                pn = f"p{(hh * 2 + th) % 2}"
                mmd = proj_group(qbufs[q], hh * 128, pn, th, qsems[q], ldq)
                p.w("vector", t, mmd)
                qknorm_evac(pn, lambda h=h, th=th: sb[qt][:, h * 1024 + th * 512:
                                                        h * 1024 + (th + 1) * 512],
                            fold_sqrt_d=True)
        qwin_used[q] = t.n
    q_done = v.n

    p.w("gpsimd", v, max(k_done, v_done))
    emit_xchg_write2(p, S, shared_t=xkv, flag_t=xflag, flagval_row=1, wsem=S["xws"],
                     parts=[
                         (4, [[8192, 128], [2048, 4], [1, 1024]],
                          bass.AP(sb["wg0"], 0, [[8192, 128], [2048, 4], [1, 1024]])),
                         (5, [[8192, 128], [512, 8], [1, 512]],
                          bass.AP(sb["wu0"], 0, [[8192, 128], [512, 8], [1, 512]])),
                     ])
    emit_xchg_poll_read2(p, S, shared_t=xkv, flag_t=xflag, flagval_row=1,
                         psem=S["poll1"], rsem=S["xls2"], roleint_col=1,
                         reads=[(0, sb["wg0"][:, 0:8192], [[8192, 128], [1, 8192]]),
                                (128 * 8192, sb["wu0"][:, 0:8192],
                                 [[8192, 128], [1, 8192]])])
    xch2 = S["xch2"]
    xch2.reserve(1)
    p.on("gpsimd", lambda E, s=xch2.h: E.nop().then_inc(s, 1))
    p.w("tensor", xch2)
    p.w("vector", xch2)

    p.w("tensor", v, q_done)
    for h in range(16):
        kvh = h // 4
        for m in range(16):
            for th in range(2):
                pn = f"p{(m * 2 + th) % 2}"
                pwait(pn)
                p.do("tensor", lambda E, m=m, th=th, pn=pn, kvh=kvh, h=h: E.matmul(
                    p.ps[pn][:, 0:512],
                    sb["wg0"][:, kvh * 2048 + m * 128:kvh * 2048 + (m + 1) * 128],
                    sb[qt][:, h * 1024 + th * 512:h * 1024 + (th + 1) * 512],
                    start=True, stop=True), t)
                mmd = t.n
                p.w("scalar", t, mmd)
                p.w("scalar", v, negc_ready)
                p.do("scalar", lambda E, m=m, th=th, pn=pn: E.activation(
                    sb[xn][:, m * 1024 + th * 512:m * 1024 + (th + 1) * 512],
                    p.ps[pn][0:128, 0:512], AF.Exp, bias=sb["negc"][:, 0:1]), a)
                psf[pn] = (a, a.n)
        probs_done = a.n
        for th in range(2):
            pwait("p2")
            p.w("tensor", a, probs_done)
            for m in range(16):
                p.do("tensor", lambda E, m=m, th=th: E.matmul(
                    p.ps["p2"][0:1, 0:512], sb["ones_col"][:, 0:1],
                    sb[xn][:, m * 1024 + th * 512:m * 1024 + (th + 1) * 512],
                    start=(m == 0), stop=(m == 15)), t if m == 15 else None)
            dsum = t.n
            p.w("vector", t, dsum)
            p.do("vector", lambda E: E.reciprocal(sb["rows"][0:1, 0:512],
                                                  p.ps["p2"][0:1, 0:512]), v)
            psf["p2"] = (v, v.n)
            p.do("vector", lambda E: E.tensor_copy(sb["rows_b"][0:1, 0:512],
                                                   sb["rows"][0:1, 0:512]), v)
            rb = v.n
            pwait("p3")
            p.w("tensor", v, rb)
            p.do("tensor", lambda E: E.matmul(p.ps["p3"][:, 0:512],
                                              sb["rows_b"][0:1, 1024:1152],
                                              sb["rows_b"][0:1, 0:512],
                                              start=True, stop=True), t)
            bcd = t.n
            p.w("vector", t, bcd)
            p.do("vector", lambda E: E.tensor_copy(sb["t512a"][:, 0:512],
                                                   p.ps["p3"][:, 0:512]), v)
            dinv_ready = v.n
            psf["p3"] = (v, dinv_ready)
            pn = f"p{4 + th}"
            pwait(pn)
            for ch in range(16):
                p.do("tensor", lambda E, ch=ch, th=th, pn=pn, kvh=kvh: E.matmul(
                    p.ps[pn][:, 0:512],
                    sb["wu0"][:, ch * 512 + kvh * 128:ch * 512 + (kvh + 1) * 128],
                    sb[xn][:, ch * 1024 + th * 512:ch * 1024 + (th + 1) * 512],
                    start=(ch == 0), stop=(ch == 15)), t if ch == 15 else None)
            avd = t.n
            p.w("vector", t, avd)
            p.w("vector", v, dinv_ready)
            p.do("vector", lambda E, h=h, th=th, pn=pn: E.tensor_mul(
                sb[qt][:, h * 1024 + th * 512:h * 1024 + (th + 1) * 512],
                sb["t512a"][:, 0:512], p.ps[pn][0:128, 0:512]), v)
            psf[pn] = (v, v.n)
            S["t512a_free"] = v.n
    attn_done = v.n

    for w in range(4):
        q = w % 2
        p.w("sync", t, qwin_used[q])
        p.do("sync", lambda E, w=w, q=q: E.dma_start(
            out=sb[qbufs[q]][:, 0:8192],
            in_=bass.AP(wob.tensor, w * 512,
                        [[2048, 128], [128 * 2048, 16], [1, 512]])), qsems[q], 16)
        ldo = qsems[q].n
        for mi in range(4):
            for th in range(2):
                pn = f"p{2 + (mi * 2 + th) % 2}"
                pwait(pn)
                p.w("tensor", qsems[q], ldo)
                p.w("tensor", v, attn_done)
                for kc in range(16):
                    p.do("tensor", lambda E, kc=kc, mi=mi, th=th, pn=pn, q=q: E.matmul(
                        p.ps[pn][:, 0:512],
                        sb[qbufs[q]][:, kc * 512 + mi * 128:kc * 512 + (mi + 1) * 128],
                        sb[qt][:, kc * 1024 + th * 512:kc * 1024 + (th + 1) * 512],
                        start=(kc == 0), stop=(kc == 15)), t if kc == 15 else None)
                mmd = t.n
                dch = w * 4 + mi
                p.w("vector", t, mmd)
                p.do("vector", lambda E, dch=dch, th=th, pn=pn: E.tensor_add(
                    sb[x][:, dch * 1024 + th * 512:dch * 1024 + (th + 1) * 512],
                    sb[x][:, dch * 1024 + th * 512:dch * 1024 + (th + 1) * 512],
                    p.ps[pn][0:128, 0:512]), v)
                psf[pn] = (v, v.n)
        qwin_used[q] = t.n
    for key in ("wg0", "wg1", "wu0", "wu1"):
        wfree[key] = (t, t.n)
    S["x_ready"] = v.n


def build_attn1():
    nc = _new_nc()
    xin = nc.dram_tensor("xin", [2048, 1024], F32, kind="ExternalInput").ap()
    wqb = nc.dram_tensor("wqb", [2048, 2048], BF16, kind="ExternalInput").ap()
    wkb = nc.dram_tensor("wkb", [2048, 512], BF16, kind="ExternalInput").ap()
    wvb = nc.dram_tensor("wvb", [2048, 512], BF16, kind="ExternalInput").ap()
    wob = nc.dram_tensor("wob", [2048, 2048], BF16, kind="ExternalInput").ap()
    xoffs = nc.dram_tensor("xoffs", [1, 8], I32, kind="ExternalInput").ap()
    flagvals = nc.dram_tensor("flagvals", [4, 16], I32, kind="ExternalInput").ap()
    roleint = nc.dram_tensor("roleint", [1, 2], I32, kind="ExternalInput").ap()
    rolem = nc.dram_tensor("rolem", [128, 1], I32, kind="ExternalInput").ap()
    xo = nc.dram_tensor("xo", [2048, 1024], F32, kind="ExternalOutput").ap()
    xkv = nc.dram_tensor("xkv", [256, 8192], BF16, kind="Internal",
                         addr_space="Shared").ap()
    xflag = nc.dram_tensor("xflag", [2, 16], I32, kind="Internal",
                           addr_space="Shared").ap()
    with contextlib.ExitStack() as ctx:
        p = Prog(nc, ctx)
        alloc_main_bufs(p)
        alloc_xchg_bufs(p)
        alloc_attn1_bufs(p)
        S = {k: p.sem(k) for k in ("v", "a", "t", "lg0", "lg1", "lu0", "lu1",
                                   "lx", "sx", "poll1", "xls2", "xws", "xch2")}
        emit_consts(p, S)
        for nm, dst, ap_in in [("flagvals", "flagv_sb", flagvals),
                               ("roleint", "roleint_sb", roleint),
                               ("rolem", "rolem_sb", rolem),
                               ("xoffs", "xoffs_sb", xoffs)]:
            shp = ap_in.shape
            p.do("scalar", lambda E, ap_in=ap_in, dst=dst, shp=shp: E.dma_start(
                out=p.sb[dst][0:shp[0], 0:shp[1]], in_=ap_in), S["lx"], 16)
        p.do("scalar", lambda E: E.dma_start(
            out=p.sb["x"][:, :],
            in_=bass.AP(xin.tensor, 0, [[1024, 128], [128 * 1024, 16], [1, 1024]])),
            S["lx"], 16)
        p.w("vector", S["lx"])
        p.w("scalar", S["lx"])
        p.w("gpsimd", S["lx"])
        emit_ln(p, S, x="x", xn="bufA", scr="bufB")
        emit_attn1(p, S, wqb=wqb, wkb=wkb, wvb=wvb, wob=wob, x="x", xn="bufA",
                   qt="bufB", xkv=xkv, xflag=xflag)
        p.w("scalar", S["v"], S["x_ready"])
        p.do("scalar", lambda E: E.dma_start(
            out=bass.AP(xo.tensor, 0, [[1024, 128], [128 * 1024, 16], [1, 1024]]),
            in_=p.sb["x"][:, :]), S["sx"], 16)
        p.w("scalar", S["sx"])
        p.w("gpsimd", S["xws"])
        p.emit()
    return nc



def emit_decode_v2(p, S, bufs, *, n_bt, width, pk_dram, scT_dram, out_dram,
                   ind, progress_key=None):
    v, a, t, g = S["v"], S["a"], S["t"], S["g"]
    lds = (S["ld0"], S["ld1"])
    lsc = S["lsc"]
    sts = (S["st0"], S["st1"])
    psf = S.setdefault("psfree", {})
    arena = S.setdefault("arena", {})
    W = width

    def pwait(pn):
        if pn in psf:
            p.w("tensor", psf[pn][0], psf[pn][1])

    for bt in range(n_bt):
        q = bt % 2
        pkt = bufs[f"pk{q}"]
        plb = bufs[f"pl{q}"]
        scb = bufs[f"scr{q}"]
        scev = bufs[f"sce{q}"]
        if ("pk", q) in arena:
            p.w("sync", *arena[("pk", q)])
        p.do("sync", lambda E, bt=bt, pkt=pkt: E.dma_start(
            out=pkt[:, 0:W], in_=pk_dram[128 * bt:128 * (bt + 1), 0:W]), lds[q], 16)
        ld_pk = lds[q].n
        if ("sc", q) in arena:
            p.w("gpsimd", *arena[("sc", q)])
        p.do("gpsimd", lambda E, bt=bt, scb=scb: E.dma_start(
            out=scb[0:8, 0:W], in_=scT_dram[8 * bt:8 * (bt + 1), 0:W]), lsc, 16)
        ld_sc = lsc.n
        NW = (W + 511) // 512
        scpn = []
        for win in range(NW):
            w0, w1 = win * 512, min(win * 512 + 512, W)
            pn = ("p6", "p7")[win % 2]
            pwait(pn)
            p.w("tensor", lsc, ld_sc)
            p.do("tensor", lambda E, pn=pn, w0=w0, w1=w1, scb=scb, ind=ind: E.matmul(
                p.ps[pn][:, 0:w1 - w0], ind[0:8, :], scb[0:8, w0:w1],
                start=True, stop=True), t)
            scpn.append((pn, w0, w1, t.n))
        arena[("sc", q)] = (t, t.n)
        if ("sce", q) in arena:
            p.w("scalar", *arena[("sce", q)])
        for pn, w0, w1, mmc in scpn:
            p.w("scalar", t, mmc)
            p.do("scalar", lambda E, pn=pn, w0=w0, w1=w1, scev=scev: E.copy(
                scev[:, w0:w1], p.ps[pn][0:128, 0:w1 - w0]), a)
            psf[pn] = (a, a.n)
        sc_ready = a.n
        p.w("scalar", lds[q], ld_pk)
        t16 = bufs["t16"]
        p.do("scalar", lambda E, pkt=pkt: E.activation(
            t16[:, 0:W], pkt[:, 0:W], AF.Identity), a)
        cvt_done = a.n
        arena[("pk", q)] = (a, cvt_done)
        for pl, (mA, shA, mS, shS) in enumerate([(7, 6, 8, 12), (0x70, 2, 0x80, 8)]):
            dst = bufs["lo16"] if pl == 0 else bufs["hi16"]
            a16, s16 = bufs["a16"], bufs["s16"]
            p.w("vector", a, cvt_done)
            p.do("vector", lambda E, mA=mA, shA=shA, a16=a16: E.tensor_scalar(
                a16[:, 0:W], t16[:, 0:W], mA, shA,
                ALU.bitwise_and, ALU.logical_shift_left), v)
            m6_done = v.n
            p.do("vector", lambda E, mS=mS, shS=shS, s16=s16: E.tensor_scalar(
                s16[:, 0:W], t16[:, 0:W], mS, shS,
                ALU.bitwise_and, ALU.logical_shift_left), v)
            s15_done = v.n
            p.w("scalar", v, m6_done)
            p.do("scalar", lambda E, a16=a16: E.activation(
                a16[:, 0:W], a16[:, 0:W], AF.Identity,
                bias=p.sb["c16k"][:, 0:1]), a)
            p.do("scalar", lambda E, a16=a16, dst=dst: E.activation(
                dst[:, 0:W].bitcast(BF16), a16[:, 0:W].bitcast(BF16), AF.Relu,
                scale=-1.0, bias=1.0), a)
            r_done = a.n
            p.w("vector", a, r_done)
            p.do("vector", lambda E, a16=a16, dst=dst: E.scalar_tensor_tensor(
                dst[:, 0:W].bitcast(BF16), dst[:, 0:W].bitcast(BF16), -1.0,
                a16[:, 0:W].bitcast(BF16), ALU.mult, ALU.add), v)
            g_done = v.n
            p.do("vector", lambda E, dst=dst, s16=s16: E.tensor_tensor(
                dst[:, 0:W], dst[:, 0:W], s16[:, 0:W], ALU.bitwise_xor), v)
            sg_done = v.n
            if ("pl", q) in arena:
                p.w("vector", *arena[("pl", q)])
            p.w("vector", a, sc_ready)
            p.do("vector", lambda E, dst=dst, plb=plb, pl=pl, scev=scev: E.tensor_tensor(
                plb[:, pl * W:(pl + 1) * W], dst[:, 0:W].bitcast(BF16),
                scev[:, 0:W], ALU.mult), v)
        planes_done = v.n
        arena[("sce", q)] = (v, planes_done)
        p.w("sync", v, planes_done)
        for pl in range(2):
            kap = 2 * bt + pl
            p.do("sync", lambda E, kap=kap, pl=pl, plb=plb: E.dma_start(
                out=out_dram[128 * kap:128 * (kap + 1), 0:W],
                in_=plb[:, pl * W:(pl + 1) * W]), sts[q], 16)
        arena[("pl", q)] = (sts[q], sts[q].n)
    if progress_key is not None:
        S.setdefault("decoded", {})[progress_key] = (sts[0], sts[0].n, sts[1], sts[1].n)



def build_fused():
    nc = _new_nc()
    ins = {}
    for nm, shape, dt in [
        ("xin", [2048, 1024], F32),
        ("gpkT", [1024, 8192], U8), ("gscT", [64, 8192], F32),
        ("upkT", [1024, 8192], U8), ("uscT", [64, 8192], F32),
        ("dpkT", [4096, 2048], U8), ("dscT", [256, 2048], F32),
        ("winT", [1024, 2048], F32), ("woutT", [1024, 2048], F32),
        ("wq", [1024, 2048], F32), ("wk", [1024, 512], F32),
        ("wv", [1024, 512], F32), ("wo", [1024, 2048], F32),
        ("l1g", [1024, 8192], F32), ("l1u", [1024, 8192], F32),
        ("l1d", [4096, 2048], F32), ("clfT", [2048, 12], F32),
        ("escb", [1, 12], F32),
        ("decay_pt", [128, 16], F32), ("bias_pt", [128, 16], F32),
        ("ind16", [8, 128], BF16), ("ind16h", [8, 128], BF16),
        ("identf", [128, 128], F32),
        ("xoffs", [1, 8], I32), ("flagvals", [4, 16], I32),
        ("roleint", [1, 4], I32), ("rolem", [128, 1], I32),
    ]:
        ins[nm] = nc.dram_tensor(nm, shape, dt, kind="ExternalInput").ap()
    xo = nc.dram_tensor("xo", [2048, 1024], F32, kind="ExternalOutput").ap()
    scr = {}
    for nm, shape in [("sgate", [2048, 16384]), ("sup", [2048, 16384]),
                      ("sdown", [16384, 2048]), ("swin", [2048, 2048]),
                      ("swout", [2048, 2048]), ("swq", [2048, 2048]),
                      ("swk", [2048, 512]), ("swv", [2048, 512]),
                      ("swo", [2048, 2048]), ("sl1g", [2048, 8192]),
                      ("sl1u", [2048, 8192]), ("sl1d", [8192, 2048])]:
        scr[nm] = nc.dram_tensor(nm, shape, BF16, kind="Internal",
                                 addr_space="Shared").ap()
    xdflag = nc.dram_tensor("xdflag", [2, 16], I32, kind="Internal",
                            addr_space="Shared").ap()
    xcflag = nc.dram_tensor("xcflag", [2, 16], I32, kind="Internal",
                            addr_space="Shared").ap()
    xfin = nc.dram_tensor("xfin", [256, 16], F32, kind="Internal",
                          addr_space="Shared").ap()
    xkv = nc.dram_tensor("xkv", [256, 8192], BF16, kind="Internal",
                         addr_space="Shared").ap()
    xflag = nc.dram_tensor("xflag", [2, 16], I32, kind="Internal",
                           addr_space="Shared").ap()

    with contextlib.ExitStack() as ctx:
        p = Prog(nc, ctx)
        alloc_main_bufs(p)
        alloc_xchg_bufs(p)
        alloc_ffn1_bufs(p)
        alloc_attn1_bufs(p)
        p.sbuf("ind16_sb", [8, 128], BF16)
        p.sbuf("ind16h_sb", [8, 128], BF16)
        p.sbuf("c16k", [128, 1], F32)
        sb = p.sb
        wg1, wu1 = sb["wg1"], sb["wu1"]
        arena_bufs = {
            "pkb0": wg1[:, 0:1024].bitcast(U8),
            "pkb1": wg1[:, 1024:2048].bitcast(U8),
            "t16_0": wg1[:, 2048:2560].bitcast(U16),
            "t16_1": wg1[:, 2560:3072].bitcast(U16),
            "a16lo_0": wg1[:, 3072:3584].bitcast(U16),
            "a16lo_1": wg1[:, 3584:4096].bitcast(U16),
            "a16hi_0": wg1[:, 4096:4608].bitcast(U16),
            "a16hi_1": wg1[:, 4608:5120].bitcast(U16),
            "lo16_0": wg1[:, 5120:5632].bitcast(U16),
            "lo16_1": wg1[:, 5632:6144].bitcast(U16),
            "hi16_0": wg1[:, 6144:6656].bitcast(U16),
            "hi16_1": wg1[:, 6656:7168].bitcast(U16),
            "scr0": wg1[0:8, 7168:7680],
            "scr1": wg1[0:8, 7680:8192],
            "pla0": wu1[:, 0:4096],
            "pla1": wu1[:, 4096:8192],
        }
        p.sbuf("dflagsc", [2, 16], I32)
        S = {k: p.sem(k) for k in (
            "v", "a", "t", "g", "lg0", "lg1", "lu0", "lu1", "lx", "sx", "lmisc",
            "cvt", "ld0", "ld1", "lsc0", "lsc1", "st0", "st1", "poll0", "poll1",
            "xls", "xls2", "xws", "xch", "xch2", "lmiscg",
            "pd", "pdp0", "pdp1", "pdp2", "pdp3", "pc", "pcp")}
        v, a, t = S["v"], S["a"], S["t"]
        emit_consts(p, S)
        p.do("vector", lambda E: E.memset(sb["escb_sb"][0:1, 0:16], 0.0), v)

        for nm, dst in [("decay_pt", "decay_pt_sb"), ("bias_pt", "bias_pt_sb"),
                        ("flagvals", "flagv_sb"), ("roleint", "roleint_sb"),
                        ("rolem", "rolem_sb"), ("xoffs", "xoffs_sb"),
                        ("identf", "identf_sb"), ("ind16", "ind16_sb"),
                        ("ind16h", "ind16h_sb")]:
            ap_in = ins[nm]
            shp = ap_in.shape
            p.do("scalar", lambda E, ap_in=ap_in, dst=dst, shp=shp: E.dma_start(
                out=sb[dst][0:shp[0], 0:shp[1]], in_=ap_in), S["lmisc"], 16)
        p.do("gpsimd", lambda E: E.dma_start(
            out=sb["clfb_sb"][:, :],
            in_=bass.AP(ins["clfT"].tensor, 0, [[12, 128], [128 * 12, 16], [1, 12]])),
            S["lmiscg"], 16)
        p.w("gpsimd", v, 4)
        p.do("gpsimd", lambda E: E.dma_start(
            out=sb["escb_sb"][0:1, 0:12], in_=ins["escb"]), S["lmiscg"], 16)
        p.do("scalar", lambda E: E.dma_start(
            out=sb["x"][:, :],
            in_=bass.AP(ins["xin"].tensor, 0, [[1024, 128], [128 * 1024, 16], [1, 1024]])),
            S["lx"], 16)
        p.w("vector", S["lx"])
        p.w("scalar", S["lx"])
        p.w("tensor", S["lmisc"])
        p.w("gpsimd", S["lmiscg"])

        conv_jobs = []
        for nm, src, half in [("swin", "winT", 1024), ("swout", "woutT", 1024),
                              ("sl1g", "l1g", 1024), ("sl1u", "l1u", 1024),
                              ("sl1d", "l1d", 4096), ("swq", "wq", 1024),
                              ("swk", "wk", 1024), ("swv", "wv", 1024),
                              ("swo", "wo", 1024)]:
            nchunk = 2 if nm in ("sl1g", "sl1u", "sl1d") else 1
            step = half // nchunk
            for i in range(nchunk):
                conv_jobs.append((nm, src, i * step, step, half))
        cvt = S["cvt"]
        cvt_need = {}

        def emit_conv(n=3):
            for _ in range(n):
                if not conv_jobs:
                    return
                nm, src, r0, nr, half = conv_jobs.pop(0)
                cvt.reserve(16)

                def body(E, nm=nm, src=src, r0=r0, nr=nr, half=half):
                    for br in (0, 1):
                        with E.register(_rn("cvb")) as rc:
                            E.reg_load(rc, sb["roleint_sb"][0:1, 2 + br:3 + br])
                            with E.While(rc):
                                E.dma_start(
                                    out=scr[nm][br * half + r0:br * half + r0 + nr, :],
                                    in_=ins[src][r0:r0 + nr, :]).then_inc(cvt.h, 16)
                                E.reg_mov(rc, 0)
                p.do("gpsimd", body)
                cvt_need[nm] = cvt.n

        deps = {}
        all_bts = []
        for it in range(4):
            all_bts.extend(dec_bts_for_iter(ins, scr, sb, it))
        dec_pos = [0]
        dec_flags = [0]
        dec_done = [False]
        tick_ct = [0]

        def dec_emit(n_bts):
            i0 = dec_pos[0]
            i1 = min(i0 + n_bts, len(all_bts))
            if i1 > i0:
                emit_decode_v5(p, S, arena_bufs, all_bts[i0:i1], sb["ind16_sb"])
                dec_pos[0] = i1
            if i1 == len(all_bts):
                it_done = 4
                if not dec_done[0]:
                    dec_done[0] = True
                    emit_decode_v5_flush(p, S, arena_bufs)
                    S.setdefault("wfree", {})["wg1"] = (S["pd"], 64)
                    S["wfree"]["wu1"] = (S["pd"], 64)
            else:
                it_done = max(0, (i1 - 1) // 24)
            while dec_flags[0] < it_done:
                it = dec_flags[0]
                emit_dec_flag_poll(p, S, it, xdflag)
                for ssup in (2 * it, 2 * it + 1):
                    for key in ("g", "u", "d"):
                        deps[(key, ssup)] = [(S["pd"], 16 * (it + 1))]
                dec_flags[0] = it + 1
            emit_conv(1)

        emit_ln(p, S, x="x", xn="bufA", scr="bufB")
        dec_emit(48)

        def pre_super(sup):
            pass

        def tick():
            tick_ct[0] += 1
            dec_emit(2)

        emit_ffn0(p, S, gdec=scr["sgate"], udec=scr["sup"], ddec=scr["sdown"],
                  x="x", xn="bufA", inter="bufB", single_buf=True, deps=deps,
                  pre_super=pre_super, tick=tick, dbl_from=5)
        dec_emit(len(all_bts))
        emit_conv(99)
        pc, pcp = S["pc"], S["pcp"]
        cvt_full = cvt.n
        cvt.reserve(16)
        cvt_flagged = cvt.n
        pc.reserve(16)

        def conv_flag_body(E, cvt_full=cvt_full, cvt_flagged=cvt_flagged):
            E.wait_ge(cvt.h, cvt_full)
            for br in (0, 1):
                with E.register(_rn("cfw")) as rc:
                    E.reg_load(rc, sb["roleint_sb"][0:1, 2 + br:3 + br])
                    with E.While(rc):
                        E.dma_start(out=bass.AP(xcflag.tensor, br * 16,
                                                [[16, 1], [1, 16]]),
                                    in_=sb["flagv_sb"][2:3, 0:16]
                                    ).then_inc(cvt.h, 16)
                        E.reg_mov(rc, 0)
            E.wait_ge(cvt.h, cvt_flagged)
            with (
                E.register(_rn("cf0")) as rf0,
                E.register(_rn("cf1")) as rf1,
                E.register(_rn("cc")) as rc2,
                E.register(_rn("ck")) as rcont,
                E.register(_rn("ct")) as rtmp,
            ):
                E.reg_mov(rc2, 0)
                E.reg_mov(rcont, 1)
                with E.While(rcont):
                    E.dma_start(out=sb["dflagsc"][0:2, 0:16],
                                in_=bass.AP(xcflag.tensor, 0, [[16, 2], [1, 16]])
                                ).then_inc(pcp.h, 16)
                    E.reg_add(rc2, rc2, 16)
                    E.wait_ge(pcp.h, rc2)
                    E.reg_load(rtmp, sb["flagv_sb"][2:3, 0:1])
                    E.reg_load(rf0, sb["dflagsc"][0:1, 2:3])
                    E.reg_load(rf1, sb["dflagsc"][1:2, 2:3])
                    E.reg_alu(rf0, rf0, rtmp, ALU.not_equal)
                    E.reg_alu(rf1, rf1, rtmp, ALU.not_equal)
                    E.reg_alu(rcont, rf0, rf1, ALU.bitwise_or)
            E.dma_start(out=sb["dflagsc"][0:2, 0:16],
                        in_=bass.AP(xcflag.tensor, 0, [[16, 2], [1, 16]])
                        ).then_inc(pc.h, 16)
        p.do("gpsimd", conv_flag_body)
        p.w("sync", S["st0"])
        p.w("sync", S["st1"])

        emit_ln(p, S, x="x", xn="bufA", scr="bufB",
                xn_guard=[(t, t.n)], scr_guard=[(t, t.n)])
        p.w("sync", S["pc"], 16)
        emit_attn0(p, S, winb=scr["swin"], woutb=scr["swout"], x="x", xn="bufA",
                   prj="bufB", xfin=xfin, xflag=xflag)

        emit_ln(p, S, x="x", xn="bufA", scr="bufB",
                xn_guard=[(t, t.n)], scr_guard=[(t, t.n), (v, v.n)])
        emit_ffn0(p, S, gdec=scr["sgate"], udec=scr["sup"], ddec=scr["sdown"],
                  x="x", xn="bufA", inter="bufB")

        emit_ln(p, S, x="x", xn="bufA", scr="bufB",
                xn_guard=[(t, t.n)], scr_guard=[(t, t.n)])
        p.w("sync", S["pc"], 16)
        emit_ffn1(p, S, l1gb=scr["sl1g"], l1ub=scr["sl1u"], l1db=scr["sl1d"],
                  x="x", xn="bufA", inter="bufB")

        emit_ln(p, S, x="x", xn="bufA", scr="bufB",
                xn_guard=[(t, t.n), (v, v.n)], scr_guard=[(t, t.n)])
        p.w("sync", S["pc"], 16)
        emit_attn1(p, S, wqb=scr["swq"], wkb=scr["swk"], wvb=scr["swv"],
                   wob=scr["swo"], x="x", xn="bufA", qt="bufB", xkv=xkv,
                   xflag=xflag)

        emit_ln(p, S, x="x", xn="bufA", scr="bufB",
                xn_guard=[(t, t.n)], scr_guard=[(t, t.n)])
        emit_ffn1(p, S, l1gb=scr["sl1g"], l1ub=scr["sl1u"], l1db=scr["sl1d"],
                  x="x", xn="bufA", inter="bufB")

        p.w("scalar", v, S["x_ready"])
        p.do("scalar", lambda E: E.dma_start(
            out=bass.AP(xo.tensor, 0, [[1024, 128], [128 * 1024, 16], [1, 1024]]),
            in_=sb["x"][:, :]), S["sx"], 16)
        p.w("scalar", S["sx"])
        p.w("gpsimd", S["xws"])
        p.emit()
    return nc


def host_fused_inputs(inputs, g, nonce):
    maps = []
    xs = shard_x(inputs)
    cs = np.concatenate
    halves = {}
    for role in (0, 1):
        mys = [2 * i + role for i in range(4)]
        halves[role] = {
            "gpkT": cs([g["gate_pkT"][:, s * 2048:(s + 1) * 2048] for s in mys], 1),
            "gscT": cs([g["gate_scT"][:, s * 2048:(s + 1) * 2048] for s in mys], 1),
            "upkT": cs([g["up_pkT"][:, s * 2048:(s + 1) * 2048] for s in mys], 1),
            "uscT": cs([g["up_scT"][:, s * 2048:(s + 1) * 2048] for s in mys], 1),
            "dpkT": cs([g["down_pkT"][1024 * s:1024 * (s + 1), :] for s in mys], 0),
            "dscT": cs([g["down_scT"][64 * s:64 * (s + 1), :] for s in mys], 0),
            "winT": g["winT"][role * 1024:(role + 1) * 1024],
            "woutT": g["woutT"][role * 1024:(role + 1) * 1024],
            "wq": g["wq"][role * 1024:(role + 1) * 1024],
            "wk": g["wk"][role * 1024:(role + 1) * 1024],
            "wv": g["wv"][role * 1024:(role + 1) * 1024],
            "wo": g["wo"][role * 1024:(role + 1) * 1024],
            "l1g": g["l1gT"][role * 1024:(role + 1) * 1024],
            "l1u": g["l1uT"][role * 1024:(role + 1) * 1024],
            "l1d": g["l1dT"][role * 4096:(role + 1) * 4096],
        }
    bias_pt = np.ascontiguousarray(
        inputs["L0_attn_bias"][PI2048].reshape(16, 128).astype(np.float32).T)
    escb = np.ascontiguousarray(
        inputs["L1_ffn_e_score_correction_bias"].reshape(1, 12).astype(np.float32))
    identf = np.eye(128, dtype=np.float32)
    for c in range(8):
        h = halves[c % 2]
        m = {
            "xin": xs[c],
            "gpkT": h["gpkT"], "gscT": h["gscT"],
            "upkT": h["upkT"], "uscT": h["uscT"],
            "dpkT": h["dpkT"], "dscT": h["dscT"],
            "winT": h["winT"], "woutT": h["woutT"],
            "wq": h["wq"], "wk": h["wk"], "wv": h["wv"], "wo": h["wo"],
            "l1g": h["l1g"], "l1u": h["l1u"], "l1d": h["l1d"],
            "clfT": g["clfT"], "escb": escb,
            "decay_pt": g["decay_pt"], "bias_pt": bias_pt,
            "ind16": g["ind16"], "ind16h": g["ind16h"], "identf": identf,
        }
        m.update(host_xchg_inputs(c, nonce))
        maps.append(m)
    return maps



def emit_decode_v3(p, S, bufs, tiles, ind_default):
    v, a, t = S["v"], S["a"], S["t"]
    lds = (S["ld0"], S["ld1"])
    lscs = (S["lsc0"], S["lsc1"])
    sts = (S["st0"], S["st1"])
    psf = S.setdefault("psfree", {})
    st8 = S.setdefault("dv3", {"cvt": {}, "v1": {}, "v2": {}, "a1": {}, "scev": {},
                               "mm": {}, "store": {}, "ld": {}, "lsc": {}})
    W = 512
    n = len(tiles)
    base = st8.setdefault("n", 0)

    def emit_front(gi):
        i = gi - base
        pi = gi % 2
        ti = tiles[i]
        pn = ("p6", "p7")[pi]
        if gi - 2 in st8["cvt"]:
            p.w("sync", a, st8["cvt"][gi - 2])
        p.do("sync", lambda E, i=i, pi=pi: E.dma_start(
            out=bufs[f"pk{pi}"], in_=tiles[i]["pk"]), lds[pi], 16)
        st8["ld"][gi] = lds[pi].n
        if gi - 2 in st8["mm"]:
            p.w("gpsimd", t, st8["mm"][gi - 2])
        p.do("gpsimd", lambda E, i=i, pi=pi: E.dma_start(
            out=bufs[f"scr{pi}"], in_=tiles[i]["sc"]), lscs[pi], 16)
        st8["lsc"][gi] = lscs[pi].n
        if pn in psf:
            p.w("tensor", psf[pn][0], psf[pn][1])
        p.w("tensor", lscs[pi], st8["lsc"][gi])
        ind = ti.get("ind", ind_default)
        p.do("tensor", lambda E, pi=pi, pn=pn, ind=ind: E.matmul(
            p.ps[pn][:, 0:512], ind[0:8, :], bufs[f"scr{pi}"][0:8, 0:512],
            start=True, stop=True), t)
        st8["mm"][gi] = t.n
        p.w("scalar", lds[pi], st8["ld"][gi])
        if gi - 2 in st8["v2"]:
            p.w("scalar", v, st8["v2"][gi - 2])
        p.do("scalar", lambda E, pi=pi: E.activation(
            bufs[f"t16_{pi}"], bufs[f"pk{pi}"], AF.Identity), a)
        st8["cvt"][gi] = a.n
        p.w("scalar", t, st8["mm"][gi])
        p.do("scalar", lambda E, pi=pi, pn=pn: E.copy(
            bufs[f"sce{pi}"], p.ps[pn][0:128, 0:512]), a)
        st8["scev"][gi] = a.n
        psf[pn] = (a, a.n)
        p.w("vector", a, st8["cvt"][gi])
        p.do("vector", lambda E, pi=pi: E.tensor_scalar(
            bufs[f"a16lo_{pi}"], bufs[f"t16_{pi}"], 7, 6,
            ALU.bitwise_and, ALU.logical_shift_left), v)
        p.do("vector", lambda E, pi=pi: E.tensor_scalar(
            bufs[f"a16hi_{pi}"], bufs[f"t16_{pi}"], 0x70, 2,
            ALU.bitwise_and, ALU.logical_shift_left), v)
        st8["v1"][gi] = v.n
        p.w("scalar", v, st8["v1"][gi])
        if gi - 2 in st8["v2"]:
            p.w("scalar", v, st8["v2"][gi - 2])
        for h in ("lo", "hi"):
            p.do("scalar", lambda E, pi=pi, h=h: E.activation(
                bufs[f"a16{h}_{pi}"], bufs[f"a16{h}_{pi}"], AF.Identity,
                bias=p.sb["c16k"][:, 0:1]), a)
            p.do("scalar", lambda E, pi=pi, h=h: E.activation(
                bufs[f"{h}16_{pi}"].bitcast(BF16), bufs[f"a16{h}_{pi}"].bitcast(BF16),
                AF.Relu, scale=-1.0, bias=1.0), a)
        st8["a1"][gi] = a.n

    def emit_back(gi):
        i = gi - base
        pi = gi % 2
        p.w("vector", a, st8["a1"][gi])
        p.w("vector", a, st8["scev"][gi])
        if gi - 2 in st8["store"]:
            p.w("vector", sts[pi], st8["store"][gi - 2])
        for h, (mS, shS) in (("lo", (8, 12)), ("hi", (0x80, 8))):
            p.do("vector", lambda E, pi=pi, h=h, mS=mS, shS=shS: E.tensor_scalar(
                bufs["s16"], bufs[f"t16_{pi}"], mS, shS,
                ALU.bitwise_and, ALU.logical_shift_left), v)
            p.do("vector", lambda E, pi=pi, h=h: E.scalar_tensor_tensor(
                bufs[f"{h}16_{pi}"].bitcast(BF16), bufs[f"{h}16_{pi}"].bitcast(BF16),
                -1.0, bufs[f"a16{h}_{pi}"].bitcast(BF16), ALU.mult, ALU.add), v)
            p.do("vector", lambda E, pi=pi, h=h: E.tensor_tensor(
                bufs[f"{h}16_{pi}"], bufs[f"{h}16_{pi}"], bufs["s16"],
                ALU.bitwise_xor), v)
            p.do("vector", lambda E, pi=pi, h=h: E.tensor_tensor(
                bufs[f"pl{pi}"][:, (0 if h == "lo" else 512):(512 if h == "lo" else 1024)],
                bufs[f"{h}16_{pi}"].bitcast(BF16), bufs[f"sce{pi}"], ALU.mult), v)
        st8["v2"][gi] = v.n
        p.w("sync", v, st8["v2"][gi])
        for h, off in (("lo", 0), ("hi", 512)):
            p.do("sync", lambda E, i=i, pi=pi, h=h, off=off: E.dma_start(
                out=tiles[i][f"out_{h}"],
                in_=bufs[f"pl{pi}"][:, off:off + 512]), sts[pi], 16)
        st8["store"][gi] = sts[pi].n

    if st8.pop("pending", None) is not None:
        pass
    for gi in range(base, base + n):
        emit_front(gi)
        if gi - 1 >= 0 and gi - 1 >= base:
            emit_back(gi - 1)
        elif gi - 1 == base - 1 and base > 0:
            pass
    emit_back(base + n - 1)
    st8["n"] = base + n


def dec_tiles_for_super(ins, scr, sb, sup):
    tiles = []
    for pk, sct, out, ind in (("gpkT", "gscT", "sgate", "ind16_sb"),
                              ("upkT", "uscT", "sup", "ind16_sb")):
        for bt in range(8):
            for w in range(4):
                c0 = sup * 2048 + w * 512
                tiles.append({
                    "pk": ins[pk][128 * bt:128 * (bt + 1), c0:c0 + 512],
                    "sc": ins[sct][8 * bt:8 * (bt + 1), c0:c0 + 512],
                    "out_lo": scr[out][128 * (2 * bt):128 * (2 * bt + 1), c0:c0 + 512],
                    "out_hi": scr[out][128 * (2 * bt + 1):128 * (2 * bt + 2), c0:c0 + 512],
                    "ind": sb[ind],
                })
    for bt in range(8):
        gbt = 8 * sup + bt
        for w in range(4):
            d0 = w * 512
            tiles.append({
                "pk": ins["dpkT"][128 * gbt:128 * (gbt + 1), d0:d0 + 512],
                "sc": ins["dscT"][8 * gbt:8 * (gbt + 1), d0:d0 + 512],
                "out_lo": scr["sdown"][128 * (2 * gbt):128 * (2 * gbt + 1), d0:d0 + 512],
                "out_hi": scr["sdown"][128 * (2 * gbt + 1):128 * (2 * gbt + 2), d0:d0 + 512],
                "ind": sb["ind16h_sb"],
            })
    return tiles



def emit_decode_v4(p, S, bufs, bts, ind_default):
    v, a, t = S["v"], S["a"], S["t"]
    lds = (S["ld0"], S["ld1"])
    lscs = (S["lsc0"], S["lsc1"])
    sts = (S["st0"], S["st1"])
    psf = S.setdefault("psfree", {})
    st8 = S.setdefault("dv4", {"cvt": {}, "v1": {}, "v2": {}, "a1": {}, "scev": {},
                              "mm": {}, "store": {}, "pk": {}, "sc": {},
                              "cvtlast": {}, "mmlast": {}, "nb": 0})

    def pwait(pn):
        if pn in psf:
            p.w("tensor", psf[pn][0], psf[pn][1])

    nb0 = st8["nb"]
    subs = []
    for bi, bt in enumerate(bts):
        for w in range(4):
            subs.append((nb0 + bi, bt, w))

    def front(B, bt, w):
        bp = B % 2
        ind = bt.get("ind", ind_default)
        gs = 4 * B + w
        sp = gs % 2
        gp = gs // 2
        pp = gp % 2
        pn = ("p6", "p7")[sp]
        if w == 0:
            p.do("scalar", lambda E, bt=bt, bp=bp: E.dma_start(
                out=bufs[f"pkb{bp}"], in_=bt["pk"]), lds[bp], 16, chain=False)
            st8["pk"][B] = lds[bp].n
        if gs - 2 in st8["mm"]:
            p.w("gpsimd", t, st8["mm"][gs - 2])
        p.do("gpsimd", lambda E, bt=bt, w=w, sp=sp: E.dma_start(
            out=bufs[f"scr{sp}"], in_=bt["sc"][0:8, w * 512:(w + 1) * 512]),
            lscs[sp], 16)
        st8["sc"][gs] = lscs[sp].n
        pwait(pn)
        p.w("tensor", lscs[sp], st8["sc"][gs])
        p.do("tensor", lambda E, pn=pn, sp=sp, ind=ind: E.matmul(
            p.ps[pn][:, 0:512], ind[0:8, :],
            bufs[f"scr{sp}"][0:8, 0:512],
            start=True, stop=True), t)
        st8["mm"][gs] = t.n
        p.w("scalar", lds[bp], st8["pk"][B])
        if gs - 2 in st8["v2"]:
            p.w("scalar", v, st8["v2"][gs - 2])
        p.do("scalar", lambda E, sp=sp, bp=bp, w=w: E.activation(
            bufs[f"t16_{sp}"], bufs[f"pkb{bp}"][:, w * 512:(w + 1) * 512],
            AF.Identity), a)
        st8["cvt"][gs] = a.n
        st8["cvtlast"][B] = a.n
        p.w("vector", a, st8["cvt"][gs])
        p.do("vector", lambda E, sp=sp: E.tensor_scalar(
            bufs[f"a16lo_{sp}"], bufs[f"t16_{sp}"], 7, 6,
            ALU.bitwise_and, ALU.logical_shift_left), v)
        p.do("vector", lambda E, sp=sp: E.tensor_scalar(
            bufs[f"a16hi_{sp}"], bufs[f"t16_{sp}"], 0x70, 2,
            ALU.bitwise_and, ALU.logical_shift_left), v)
        st8["v1"][gs] = v.n
        p.w("scalar", v, st8["v1"][gs])
        if gs - 2 in st8["v2"]:
            p.w("scalar", v, st8["v2"][gs - 2])
        for h in ("lo", "hi"):
            p.do("scalar", lambda E, sp=sp, h=h: E.activation(
                bufs[f"a16{h}_{sp}"], bufs[f"a16{h}_{sp}"], AF.Identity,
                bias=p.sb["c16k"][:, 0:1]), a)
            p.do("scalar", lambda E, sp=sp, h=h: E.activation(
                bufs[f"{h}16_{sp}"].bitcast(BF16),
                bufs[f"a16{h}_{sp}"].bitcast(BF16),
                AF.Relu, scale=-1.0, bias=1.0), a)
        st8["a1"][gs] = a.n

    def back(B, bt, w):
        return _dec4_back(p, S, bufs, B, bt, w)

    pend = st8.pop("pend", None)
    for k, (B, bt, w) in enumerate(subs):
        front(B, bt, w)
        if pend is not None:
            back(*pend)
        pend = (B, bt, w)
    st8["pend"] = pend
    st8["nb"] = nb0 + len(bts)


def _dec4_back(p, S, bufs, B, bt, w):
    v, a, t = S["v"], S["a"], S["t"]
    sts = (S["st0"], S["st1"])
    st8 = S["dv4"]
    gs = 4 * B + w
    sp = gs % 2
    gp = gs // 2
    pp = gp % 2
    p.w("vector", a, st8["a1"][gs])
    p.w("vector", a, st8["scev"][gs])
    if gp - 2 in st8["store"]:
        p.w("vector", sts[pp], st8["store"][gp - 2])
    for h, hoff in (("lo", 0), ("hi", 1024)):
        p.do("vector", lambda E, sp=sp, h=h, mS=(8 if h == "lo" else 0x80),
             shS=(12 if h == "lo" else 8): E.tensor_scalar(
                 bufs["s16"], bufs[f"t16_{sp}"], mS, shS,
                 ALU.bitwise_and, ALU.logical_shift_left), v)
        p.do("vector", lambda E, sp=sp, h=h: E.scalar_tensor_tensor(
            bufs[f"{h}16_{sp}"].bitcast(BF16), bufs[f"{h}16_{sp}"].bitcast(BF16),
            -1.0, bufs[f"a16{h}_{sp}"].bitcast(BF16), ALU.mult, ALU.add), v)
        p.do("vector", lambda E, sp=sp, h=h: E.tensor_tensor(
            bufs[f"{h}16_{sp}"], bufs[f"{h}16_{sp}"], bufs["s16"],
            ALU.bitwise_xor), v)
        p.do("vector", lambda E, sp=sp, pp=pp, w=w, h=h, hoff=hoff:
             E.tensor_tensor(
                 bufs[f"pl{pp}"][:, hoff + (w % 2) * 512:
                                 hoff + (w % 2) * 512 + 512],
                 bufs[f"{h}16_{sp}"].bitcast(BF16), bufs[f"sce{sp}"],
                 ALU.mult), v)
    st8["v2"][gs] = v.n
    if w % 2 == 1:
        p.w("gpsimd", v, st8["v2"][gs])
        p.do("gpsimd", lambda E, bt=bt, w=w, pp=pp: E.dma_start(
            out=bt["store"][w // 2], in_=bufs[f"pl{pp}"][:, 0:2048]),
            sts[pp], 16)
        st8["store"][gp] = sts[pp].n


def emit_decode_v4_flush(p, S, bufs):
    pend = S.get("dv4", {}).pop("pend", None)
    if pend is not None:
        _dec4_back(p, S, bufs, *pend)



def dec_bts_for_iter(ins, scr_t, sb, it):
    bts = []
    for pk, sct, out in (("gpkT", "gscT", "sgate"), ("upkT", "uscT", "sup")):
        outt = scr_t[out].tensor
        W = outt.shape[1]
        for bt in range(8):
            c0 = it * 2048
            bts.append({
                "pk": ins[pk][128 * bt:128 * (bt + 1), c0:c0 + 2048],
                "sc": ins[sct][8 * bt:8 * (bt + 1), c0:c0 + 2048],
                "store_br": [bass.AP(outt,
                                     (256 * bt) * W + (2 * it + br) * 2048,
                                     [[W, 128], [128 * W, 2], [1, 2048]])
                             for br in range(2)],
                "ind": sb["ind16_sb"],
            })
    outt = scr_t["sdown"].tensor
    for bt in range(8):
        bts.append({
            "pk": ins["dpkT"][1024 * it + 128 * bt:1024 * it + 128 * (bt + 1),
                              0:2048],
            "sc": ins["dscT"][64 * it + 8 * bt:64 * it + 8 * (bt + 1), 0:2048],
            "store_br": [bass.AP(outt,
                                 (2048 * (2 * it + br) + 256 * bt) * 2048,
                                 [[2048, 128], [128 * 2048, 2], [1, 2048]])
                         for br in range(2)],
            "ind": sb["ind16h_sb"],
        })
    return bts


def emit_decode_v5(p, S, bufs, bts, ind_default):
    v, a, t = S["v"], S["a"], S["t"]
    lds = (S["ld0"], S["ld1"])
    lscs = (S["lsc0"], S["lsc1"])
    psf = S.setdefault("psfree", {})
    st8 = S.setdefault("dv5", {"cvt": {}, "v1": {}, "v2": {}, "a1": {},
                               "scev": {}, "mm": {}, "store": {}, "pk": {},
                               "sc": {}, "cvtlast": {}, "mmlast": {}, "nb": 0})

    def pwait(pn):
        if pn in psf:
            p.w("tensor", psf[pn][0], psf[pn][1])

    nb0 = st8["nb"]
    subs = []
    for bi, bt in enumerate(bts):
        for w in range(4):
            subs.append((nb0 + bi, bt, w))

    def front(B, bt, w):
        bp = B % 2
        ind = bt.get("ind", ind_default)
        gs = 4 * B + w
        sp = gs % 2
        gp = gs // 2
        pp = gp % 2
        pn = ("p6", "p7")[sp]
        if w == 0:
            p.do("scalar", lambda E, bt=bt, bp=bp: E.dma_start(
                out=bufs[f"pkb{bp}"], in_=bt["pk"]), lds[bp], 16, chain=False)
            st8["pk"][B] = lds[bp].n
        if gs - 2 in st8["mm"]:
            p.w("gpsimd", t, st8["mm"][gs - 2])
        p.do("gpsimd", lambda E, bt=bt, w=w, sp=sp: E.dma_start(
            out=bufs[f"scr{sp}"], in_=bt["sc"][0:8, w * 512:(w + 1) * 512]),
            lscs[sp], 16)
        st8["sc"][gs] = lscs[sp].n
        pwait(pn)
        p.w("tensor", lscs[sp], st8["sc"][gs])
        p.do("tensor", lambda E, pn=pn, sp=sp, ind=ind: E.matmul(
            p.ps[pn][:, 0:512], ind[0:8, :],
            bufs[f"scr{sp}"][0:8, 0:512],
            start=True, stop=True), t)
        st8["mm"][gs] = t.n
        p.w("scalar", lds[bp], st8["pk"][B])
        if gs - 2 in st8["v2"]:
            p.w("scalar", v, st8["v2"][gs - 2])
        p.do("scalar", lambda E, sp=sp, bp=bp, w=w: E.activation(
            bufs[f"t16_{sp}"], bufs[f"pkb{bp}"][:, w * 512:(w + 1) * 512],
            AF.Identity), a)
        st8["cvt"][gs] = a.n
        st8["cvtlast"][B] = a.n
        p.w("vector", a, st8["cvt"][gs])
        p.do("vector", lambda E, sp=sp: E.tensor_scalar(
            bufs[f"a16lo_{sp}"], bufs[f"t16_{sp}"], 7, 6,
            ALU.bitwise_and, ALU.logical_shift_left), v)
        p.do("vector", lambda E, sp=sp: E.tensor_scalar(
            bufs[f"a16hi_{sp}"], bufs[f"t16_{sp}"], 0x70, 2,
            ALU.bitwise_and, ALU.logical_shift_left), v)
        st8["v1"][gs] = v.n
        p.w("scalar", v, st8["v1"][gs])
        if gs - 2 in st8["v2"]:
            p.w("scalar", v, st8["v2"][gs - 2])
        for h in ("lo", "hi"):
            p.do("scalar", lambda E, sp=sp, h=h: E.activation(
                bufs[f"a16{h}_{sp}"], bufs[f"a16{h}_{sp}"], AF.Identity,
                bias=p.sb["c16k"][:, 0:1]), a)
            p.do("scalar", lambda E, sp=sp, h=h: E.activation(
                bufs[f"{h}16_{sp}"].bitcast(BF16),
                bufs[f"a16{h}_{sp}"].bitcast(BF16),
                AF.Relu, scale=-1.0, bias=1.0), a)
        st8["a1"][gs] = a.n

    pend = st8.pop("pend", None)
    for k, (B, bt, w) in enumerate(subs):
        front(B, bt, w)
        if pend is not None:
            _dec5_back(p, S, bufs, *pend)
        pend = (B, bt, w)
    st8["pend"] = pend
    st8["nb"] = nb0 + len(bts)


def _dec5_back(p, S, bufs, B, bt, w):
    v, a, t = S["v"], S["a"], S["t"]
    sts = (S["st0"], S["st1"])
    psf = S.setdefault("psfree", {})
    st8 = S["dv5"]
    gs = 4 * B + w
    sp = gs % 2
    pb = B % 2
    pn = ("p6", "p7")[sp]
    p.w("vector", a, st8["a1"][gs])
    p.w("vector", t, st8["mm"][gs])
    if w == 0 and B - 2 in st8["store"]:
        p.w("vector", sts[pb], st8["store"][B - 2])
    for h, hoff in (("lo", 0), ("hi", 2048)):
        p.do("vector", lambda E, sp=sp, h=h: E.scalar_tensor_tensor(
            bufs[f"{h}16_{sp}"].bitcast(BF16), bufs[f"{h}16_{sp}"].bitcast(BF16),
            -1.0, bufs[f"a16{h}_{sp}"].bitcast(BF16), ALU.mult, ALU.add), v)
        p.do("vector", lambda E, sp=sp, h=h, mS=(8 if h == "lo" else 0x80),
             shS=(12 if h == "lo" else 8): E.tensor_scalar(
                 bufs[f"a16lo_{sp}"], bufs[f"t16_{sp}"], mS, shS,
                 ALU.bitwise_and, ALU.logical_shift_left), v)
        p.do("vector", lambda E, sp=sp, h=h: E.tensor_tensor(
            bufs[f"{h}16_{sp}"], bufs[f"{h}16_{sp}"], bufs[f"a16lo_{sp}"],
            ALU.bitwise_xor), v)
        p.do("vector", lambda E, sp=sp, pb=pb, w=w, h=h, hoff=hoff, pn=pn:
             E.tensor_tensor(
                 bufs[f"pla{pb}"][:, hoff + 512 * w:hoff + 512 * w + 512],
                 bufs[f"{h}16_{sp}"].bitcast(BF16), p.ps[pn][0:128, 0:512],
                 ALU.mult), v)
    st8["v2"][gs] = v.n
    psf[pn] = (v, v.n)
    if w == 3:
        sem = sts[pb]
        p.w("gpsimd", v, st8["v2"][gs])
        sem.reserve(16)

        def body(E, bt=bt, pb=pb, sem=sem):
            for br in (0, 1):
                with E.register(_rn("dsb")) as rc:
                    E.reg_load(rc, p.sb["roleint_sb"][0:1, 2 + br:3 + br])
                    with E.While(rc):
                        E.dma_start(out=bt["store_br"][br],
                                    in_=bufs[f"pla{pb}"]).then_inc(sem.h, 16)
                        E.reg_mov(rc, 0)
        p.do("gpsimd", body)
        st8["store"][B] = sem.n


def emit_decode_v5_flush(p, S, bufs):
    pend = S.get("dv5", {}).pop("pend", None)
    if pend is not None:
        _dec5_back(p, S, bufs, *pend)


def emit_dec_flag_poll(p, S, it, xdflag):
    sts = (S["st0"], S["st1"])
    pd = S["pd"]
    pp = S[f"pdp{it}"]
    n0, n1 = sts[0].n, sts[1].n
    sts[0].reserve(16)
    nf = sts[0].n
    pd.reserve(16)

    def body(E, it=it, n0=n0, n1=n1, nf=nf):
        E.wait_ge(sts[0].h, n0)
        E.wait_ge(sts[1].h, n1)
        for br in (0, 1):
            with E.register(_rn("dfw")) as rc:
                E.reg_load(rc, p.sb["roleint_sb"][0:1, 2 + br:3 + br])
                with E.While(rc):
                    E.dma_start(
                        out=bass.AP(xdflag.tensor, br * 16, [[16, 1], [1, 16]]),
                        in_=p.sb["flagv_sb"][it:it + 1, 0:16]
                    ).then_inc(sts[0].h, 16)
                    E.reg_mov(rc, 0)
        E.wait_ge(sts[0].h, nf)
        with (
            E.register(_rn("qf0")) as rf0,
            E.register(_rn("qf1")) as rf1,
            E.register(_rn("qc")) as rc2,
            E.register(_rn("qk")) as rcont,
            E.register(_rn("qt")) as rtmp,
        ):
            E.reg_mov(rc2, 0)
            E.reg_mov(rcont, 1)
            with E.While(rcont):
                E.dma_start(out=p.sb["dflagsc"][0:2, 0:16],
                            in_=bass.AP(xdflag.tensor, 0, [[16, 2], [1, 16]])
                            ).then_inc(pp.h, 16)
                E.reg_add(rc2, rc2, 16)
                E.wait_ge(pp.h, rc2)
                E.reg_load(rtmp, p.sb["flagv_sb"][it:it + 1, 0:1])
                E.reg_load(rf0, p.sb["dflagsc"][0:1, it:it + 1])
                E.reg_load(rf1, p.sb["dflagsc"][1:2, it:it + 1])
                E.reg_alu(rf0, rf0, rtmp, ALU.not_equal)
                E.reg_alu(rf1, rf1, rtmp, ALU.not_equal)
                E.reg_alu(rcont, rf0, rf1, ALU.bitwise_or)
        E.dma_start(out=p.sb["dflagsc"][0:2, 0:16],
                    in_=bass.AP(xdflag.tensor, 0, [[16, 2], [1, 16]])
                    ).then_inc(pd.h, 16)
    p.do("gpsimd", body)


def dec_bts_for_super(ins, scr_t, sb, sup):
    bts = []
    for pk, sct, out, ind in (("gpkT", "gscT", "sgate", "ind16_sb"),
                              ("upkT", "uscT", "sup", "ind16_sb")):
        for bt in range(8):
            c0 = sup * 2048
            outt = scr_t[out].tensor
            W = outt.shape[1]
            bts.append({
                "pk": ins[pk][128 * bt:128 * (bt + 1), c0:c0 + 2048],
                "sc": ins[sct][8 * bt:8 * (bt + 1), c0:c0 + 2048],
                "store": [bass.AP(outt, (128 * 2 * bt) * W + c0 + 1024 * pr,
                                  [[W, 128], [128 * W, 2], [1, 1024]])
                          for pr in range(2)],
                "ind": sb[ind],
            })
    for bt in range(8):
        gbt = 8 * sup + bt
        outt = scr_t["sdown"].tensor
        W = outt.shape[1]
        bts.append({
            "pk": ins["dpkT"][128 * gbt:128 * (gbt + 1), 0:2048],
            "sc": ins["dscT"][8 * gbt:8 * (gbt + 1), 0:2048],
            "store": [bass.AP(outt, (128 * 2 * gbt) * W + 1024 * pr,
                              [[W, 128], [128 * W, 2], [1, 1024]])
                      for pr in range(2)],
            "ind": sb["ind16h_sb"],
        })
    return bts
